# version 26
# baseline (speedup 1.0000x reference)
import numpy as np
import ml_dtypes
import concourse.bass as bass
import concourse.mybir as mybir
from concourse.bass_utils import run_bass_kernel_spmd
from contextlib import ExitStack

F32 = mybir.dt.float32
BF16 = mybir.dt.bfloat16
AF = mybir.ActivationFunctionType
ALU = mybir.AluOpType
AX = mybir.AxisListType

T = 2048
C = 1024
NT = 16
NC = 8
NH = 16
HS = 64
NE = 16
ED = 512
NQS = 4

G_V = 0
G_P = 16640
G_WQK = 18176
G_COLS = 20224
G_SHH = 0

W_COLS = 16384

H_HBUF = 0
H_SCR = 4096
H_COLS = 6144

F_STAT = 0
F_EPS = 64
F_Z = 66
F_COMB = 322
F_T2 = 578
F_MSK = 706
F_IOTA = 770
F_COLS = 786

_STAGE = 9
_DUMPS = []


class _Stop(Exception):
    pass


ENGINES = ("sp", "pool", "act", "dve", "pe")
ENG_SEM = {"sp": None, "pool": None, "act": "act", "dve": "dve", "pe": "pe"}
SEM_NAMES = ("pe", "dve", "act", "dc", "dout",
             "dw", "dwp0", "dwp1", "dq0", "dq1", "ds0", "ds1", "ds2", "ds3") + tuple(
    f"dx{i}" for i in range(16))




class PB:

    def __init__(self):
        self.lists = {e: [] for e in ENGINES}
        self.cnt = {k: 0 for k in SEM_NAMES}

    def emit(self, eng, fn, waits=(), inc=None, drain=False):
        own = ENG_SEM[eng]
        ws = []
        for w in waits:
            if w is None:
                continue
            s, c = w
            if c <= 0:
                continue
            if s == own:
                continue
            assert c <= self.cnt[s], f"wait on future: {s} {c} > {self.cnt[s]}"
            ws.append((s, c))
        tok = None
        if inc:
            d = 16 if inc.startswith("d") else 1
            self.cnt[inc] += d
            tok = (inc, self.cnt[inc])
        self.lists[eng].append((ws, fn, inc, drain))
        return tok


def tmax(*toks):
    best = {}
    for t in toks:
        if t is None:
            continue
        if isinstance(t, list):
            for s, c in t:
                if best.get(s, -1) < c:
                    best[s] = c
            continue
        s, c = t
        if best.get(s, -1) < c:
            best[s] = c
    return [(s, c) for s, c in best.items()]


def build_nc():
    nc = bass.Bass("TRN2", target_bir_lowering=False)

    x_h = nc.dram_tensor("x", [T, C], F32, kind="ExternalInput")
    wqkT_h = nc.dram_tensor("wqkt", [C, NH * 128], F32, kind="ExternalInput")
    wvT_h = nc.dram_tensor("wvt", [C, NH * HS], F32, kind="ExternalInput")
    wpT_h = nc.dram_tensor("wpt", [C, C], F32, kind="ExternalInput")
    bproj_h = nc.dram_tensor("bproj", [1, C], F32, kind="ExternalInput")
    routerT_h = nc.dram_tensor("routert", [C, NE], F32, kind="ExternalInput")
    rbias_h = nc.dram_tensor("rbias", [128, NE], F32, kind="ExternalInput")
    up_h = nc.dram_tensor("up", [NE, C, ED], F32, kind="ExternalInput")
    gate_h = nc.dram_tensor("gate", [NE, C, ED], F32, kind="ExternalInput")
    down_h = nc.dram_tensor("down", [NE, ED, C], F32, kind="ExternalInput")
    shgT_h = nc.dram_tensor("shgt", [C, C], F32, kind="ExternalInput")
    shuT_h = nc.dram_tensor("shut", [C, C], F32, kind="ExternalInput")
    shdT_h = nc.dram_tensor("shdt", [C, C], F32, kind="ExternalInput")
    ident_h = nc.dram_tensor("ident", [128, 128], BF16, kind="ExternalInput")
    maskneg_h = nc.dram_tensor("maskneg", [128, 128], BF16, kind="ExternalInput")
    ones_h = nc.dram_tensor("ones128", [1, 128], BF16, kind="ExternalInput")
    iota_h = nc.dram_tensor("iotam", [128, NE], F32, kind="ExternalInput")
    out_h = nc.dram_tensor("y", [T, C], F32, kind="ExternalOutput")

    with ExitStack() as ctx:
        sb = lambda n, s, d: ctx.enter_context(nc.sbuf_tensor(n, s, d))
        xacc = sb("xacc", [128, NT * C], F32)
        xnf = sb("xnf", [128, NC * T], BF16)
        G = sb("G", [128, G_COLS], BF16)
        W = sb("W", [128, W_COLS], BF16)
        H = sb("H", [128, H_COLS], BF16)
        AP2 = sb("AP2", [128, 4096], BF16)
        WPS = sb("WPS", [128, 2048], BF16)
        LNS = AP2
        F = sb("F", [128, F_COLS], F32)
        Qb = sb("Qb", [65, T], BF16)
        Kb = sb("Kb", [65, T], BF16)
        ident = sb("identity", [128, 128], BF16)
        maskm = sb("maskneg_s", [128, 128], BF16)
        ones = sb("ones_r", [1, 128], BF16)
        bproj = sb("bproj_r", [1, C], BF16)
        routerT = sb("routerT", [128, NC * NE], BF16)
        rbias = sb("rbias_b", [128, NE], F32)
        rrowbf = sb("rrowbf", [1, 512], BF16)
        rbsb = sb("rbsb", [64, 512], BF16)

        ps = [ctx.enter_context(nc.psum_tensor(f"ps{i}", [128, 512], F32))
              for i in range(7)]
        pstr = ctx.enter_context(nc.psum_tensor("pstr", [128, 1024], BF16))

        sems = {k: ctx.enter_context(nc.semaphore(k)) for k in SEM_NAMES}
        block = ctx.enter_context(nc.Block())

        pb = PB()
        bank_free = {i: None for i in range(7)}
        bank_free["tr"] = None
        SB = [0, 1]
        PVB = [2, 3]
        _bg = [0]

        def bgbank():
            b = 4 + (_bg[0] % 3)
            _bg[0] += 1
            return b

        _DUMPS.clear()

        def stage_stop(k, dumps):
            if _STAGE != k:
                return
            for name, shape, dt, ap, waits in dumps:
                d = nc.dram_tensor(name, shape, dt, kind="ExternalOutput")
                pb.emit("sp", lambda e, d=d, ap=ap: e.dma_start(d[:, :], ap),
                        waits=waits, inc="dout")
                _DUMPS.append(name)
            fin = pb.cnt["dout"]
            pb.emit("sp", lambda e: e.nop(), waits=[("dout", fin)])
            raise _Stop()

        t_x = [None] * 16

        def emit_x(c):
            eng = "sp" if (c % 4) < 2 else "pool"
            t_x[c] = pb.emit(eng, lambda e, c=c: e.dma_start(
                xacc[:, :].rearrange("p (r n) -> p r n", n=C)[:, c:c + 1, :],
                x_h[:, :].rearrange("(r p) n -> p r n", p=128)[:, c:c + 1, :]),
                inc=f"dx{c}")

        emit_x(0); emit_x(1)
        for tns, hsrc in ((ident, ident_h), (maskm, maskneg_h),
                          (ones, ones_h), (rbias, rbias_h)):
            pb.emit("sp", lambda e, t=tns, s_=hsrc: e.dma_start(t[:, :], s_[:, :]), inc="dc")
        pb.emit("sp", lambda e: e.dma_start(F[:, F_IOTA:F_IOTA + NE], iota_h[:, :]), inc="dc")
        pb.emit("pool", lambda e: e.dma_start(bproj[:, :], bproj_h[:, :]), inc="dw")
        t_consts = ("dc", pb.cnt["dc"])
        t_kone = pb.emit("dve", lambda e: e.memset(Kb[64:65, :], 1.0), inc="dve", drain=True)
        t_kone = pb.emit("dve", lambda e: e.memset(Qb[64:65, :], 0.0), inc="dve", drain=True)
        t_eps = pb.emit("dve", lambda e: e.memset(F[:, F_EPS:F_EPS + 1], 1e-5), inc="dve", drain=True)
        t_vones = pb.emit("dve", lambda e: e.memset(
            G[:, G_V:G_V + NT * 1040].rearrange("p (k h d) -> p k h d", h=NH, d=65)[:, :, :, 64:65],
            1.0), inc="dve", drain=True)
        t_ident = t_maskm = t_ones = t_rbias = t_iota = t_consts
        t_bproj = ("dw", pb.cnt["dw"])

        t_wv = [None, None]

        def emit_wv(s):
            t_wv[s] = pb.emit("pool", lambda e, s=s: e.dma_start(
                W[:, s * 4096:(s + 1) * 4096].rearrange("p (a v) -> p a v", a=4),
                wvT_h[:, :].rearrange("(a p) v -> p a v", p=128)[:, 4 * s:4 * s + 4, :]),
                inc=f"ds{s}")

        t_wqk = [None] * NH

        def emit_wqk01(h):
            t_wqk[h] = pb.emit("pool", lambda e, h=h: e.dma_start(
                G[:, G_WQK + (h % 2) * 1024:G_WQK + (h % 2) * 1024 + 1024]
                .rearrange("p (a d) -> p a d", a=NC),
                wqkT_h[:, h * 128:(h + 1) * 128].rearrange("(a p) d -> p a d", p=128)),
                inc=f"dq{h % 2}")

        t_wps = [None, None]

        def emit_wps0(p):
            t_wps[p] = pb.emit("pool", lambda e, p=p: e.dma_start(
                WPS[:, p * 1024:(p + 1) * 1024],
                wpT_h[p * 128:(p + 1) * 128, :]), inc=f"dwp{p % 2}")

        t_wdma = {}
        slot_free = [None] * 4

        def emit_wdma(e_i, m):
            src = (up_h, gate_h, down_h)[m]
            s = (3 * e_i + m + 2) % 4
            inap = src[e_i, :, :].rearrange("(a p) j -> p a j", p=128)
            outap = W[:, s * 4096:(s + 1) * 4096].rearrange(
                "p (a j) -> p a j", a=NC if m < 2 else 4)
            t_wdma[(e_i, m)] = pb.emit("pool", lambda e, inap=inap, outap=outap: e.dma_start(
                outap, inap), waits=tmax(slot_free[s]), inc=f"ds{s}")

        for c in (4, 5, 8, 9, 12, 13):
            emit_x(c)
        emit_x(2); emit_x(3)
        emit_wv(0); emit_wv(1)
        emit_x(6); emit_x(7)
        emit_wqk01(0)
        pb.emit("pool", lambda e: e.dma_start(
            routerT[:, :].rearrange("p (a v) -> p a v", a=NC),
            routerT_h[:, :].rearrange("(a p) v -> p a v", p=128)), inc="dw")
        t_routerT = ("dw", pb.cnt["dw"])
        t_bproj = t_routerT
        emit_x(10); emit_x(11)
        emit_wps0(0); emit_wps0(1)
        emit_wdma(0, 0)
        emit_x(14); emit_x(15)
        emit_wdma(0, 1)
        emit_wqk01(1)

        bgq = []

        def fill(n):
            for _ in range(n):
                if not bgq:
                    return
                bgq.pop(0)()

        t_xnf = [None] * NT
        t_vready = [None] * NT
        t_comb = [None] * NT
        t_vlast = [None]

        def v_item(tt):
            def go():
                toks = []
                for half in range(2):
                    b = bgbank()
                    def v_fn(e, tt=tt, half=half, b=b):
                        last = None
                        for cc in range(NC):
                            last = e.matmul(
                                ps[b][:, 0:512],
                                xnf[:, cc * T + tt * 128:cc * T + (tt + 1) * 128],
                                W[:, cc * C + half * 512:cc * C + half * 512 + 512],
                                start=(cc == 0), stop=(cc == NC - 1))
                        return last
                    t_mm = pb.emit("pe", v_fn,
                                   waits=tmax(t_xnf[tt], t_wv[0], t_wv[1], bank_free[b]),
                                   inc="pe")
                    t_ev = pb.emit("dve", lambda e, tt=tt, half=half, b=b: e.tensor_copy(
                        G[:, G_V + tt * 1040:G_V + (tt + 1) * 1040]
                        .rearrange("p (h d) -> p h d", d=65)[:, half * 8:half * 8 + 8, 0:64],
                        ps[b][:, 0:512].rearrange("p (h d) -> p h d", d=64)),
                        waits=[t_mm], inc="dve")
                    bank_free[b] = t_ev
                    toks.append(t_ev)
                    t_vlast[0] = t_mm
                t_vready[tt] = toks[-1]
                if tt == NT - 1:
                    slot_free[0] = t_vlast[0]
                    slot_free[1] = t_vlast[0]
                    emit_wdma(0, 2)
                    emit_wdma(1, 0)
            return go

        def router_item(tt):
            z = F[:, F_Z + tt * NE:F_Z + (tt + 1) * NE]

            def go():
                b = bgbank()
                def r_fn(e, tt=tt, b=b):
                    last = None
                    for cc in range(NC):
                        last = e.matmul(
                            ps[b][:, 0:NE],
                            xnf[:, cc * T + tt * 128:cc * T + (tt + 1) * 128],
                            routerT[:, cc * NE:(cc + 1) * NE],
                            start=(cc == 0), stop=(cc == NC - 1))
                    return last
                t_rmm = pb.emit("pe", r_fn,
                                waits=tmax(t_xnf[tt], t_routerT, bank_free[b]), inc="pe")
                t_z = pb.emit("dve", lambda e, z=z, b=b: e.tensor_tensor(
                    z, ps[b][:, 0:NE], rbias[:, :], op=ALU.add),
                    waits=tmax(t_rmm, t_rbias, t_iota), inc="dve", drain=True)
                bank_free[b] = t_z
            return go

        def emit_router_top2():
            z3 = F[:, F_Z:F_Z + 256].rearrange("p (t e) -> p t e", e=NE)
            cb3 = F[:, F_COMB:F_COMB + 256].rearrange("p (t e) -> p t e", e=NE)
            m1 = F[:, F_T2 + 0:F_T2 + 16]
            m2 = F[:, F_T2 + 16:F_T2 + 32]
            e2 = F[:, F_T2 + 32:F_T2 + 48]
            s1v = F[:, F_T2 + 48:F_T2 + 64]
            s2v = F[:, F_T2 + 64:F_T2 + 80]
            den = F[:, F_T2 + 80:F_T2 + 96]
            i1 = F[:, F_T2 + 96:F_T2 + 112]
            i2 = F[:, F_T2 + 112:F_T2 + 128]
            msk1 = AP2[:, 0:256].rearrange("p (t e) -> p t e", e=NE)
            msk2 = AP2[:, 256:512].rearrange("p (t e) -> p t e", e=NE)
            tmpb = AP2[:, 512:768].rearrange("p (t e) -> p t e", e=NE)
            iotb = F[:, F_IOTA:F_IOTA + NE].rearrange("p e -> p () e").broadcast_to([128, 16, NE])
            bt = lambda a: a.rearrange("p t -> p t ()").broadcast_to([128, 16, NE])
            pb.emit("dve", lambda e: e.tensor_reduce(m1, z3, axis=AX.X, op=ALU.max),
                    inc="dve", drain=True)
            pb.emit("dve", lambda e: e.tensor_tensor(msk1, z3, bt(m1), op=ALU.is_equal),
                    waits=tmax(t_tr[NT - 2]), inc="dve", drain=True)
            pb.emit("dve", lambda e: e.tensor_tensor(tmpb, msk1, iotb, op=ALU.mult),
                    inc="dve", drain=True)
            pb.emit("dve", lambda e: e.tensor_reduce(i1, tmpb, axis=AX.X, op=ALU.min),
                    inc="dve", drain=True)
            pb.emit("dve", lambda e: e.tensor_tensor(msk1, iotb, bt(i1), op=ALU.is_equal),
                    inc="dve", drain=True)
            pb.emit("dve", lambda e: e.scalar_tensor_tensor(
                z3, msk1, -1e30, z3, op0=ALU.mult, op1=ALU.add), inc="dve", drain=True)
            pb.emit("dve", lambda e: e.tensor_reduce(m2, z3, axis=AX.X, op=ALU.max),
                    inc="dve", drain=True)
            pb.emit("dve", lambda e: e.tensor_tensor(msk2, z3, bt(m2), op=ALU.is_equal),
                    inc="dve", drain=True)
            pb.emit("dve", lambda e: e.tensor_tensor(tmpb, msk2, iotb, op=ALU.mult),
                    inc="dve", drain=True)
            pb.emit("dve", lambda e: e.tensor_reduce(i2, tmpb, axis=AX.X, op=ALU.min),
                    inc="dve", drain=True)
            pb.emit("dve", lambda e: e.tensor_tensor(msk2, iotb, bt(i2), op=ALU.is_equal),
                    inc="dve", drain=True)
            t_sub = pb.emit("dve", lambda e: e.tensor_tensor(e2, m2, m1, op=ALU.subtract),
                            inc="dve", drain=True)
            t_exp = pb.emit("act", lambda e: e.activation(e2, e2, AF.Exp),
                            waits=[t_sub], inc="act", drain=True)
            pb.emit("dve", lambda e: e.tensor_scalar(
                den, e2, 1.0, None, op0=ALU.add), waits=[t_exp], inc="dve", drain=True)
            pb.emit("dve", lambda e: e.reciprocal(s1v, den), inc="dve", drain=True)
            pb.emit("dve", lambda e: e.tensor_scalar(
                s2v, s1v, -1.0, 1.0, op0=ALU.mult, op1=ALU.add), inc="dve", drain=True)
            pb.emit("dve", lambda e: e.tensor_tensor(cb3, msk1, bt(s1v), op=ALU.mult),
                    inc="dve", drain=True)
            pb.emit("dve", lambda e: e.tensor_tensor(z3, msk2, bt(s2v), op=ALU.mult),
                    inc="dve", drain=True)
            tok = pb.emit("dve", lambda e: e.tensor_tensor(cb3, cb3, z3, op=ALU.add),
                          inc="dve", drain=True)
            for tt in range(NT):
                t_comb[tt] = tok
            return tok

        h_done = {}
        t_hbuf_rd = [None, None]
        t_scr_rd = [None, None]
        _scr = [0]

        ustate = {}

        def u_item(e_i, ts, jc):
            def go():
                su = (3 * e_i + 0 + 2) % 4
                bu = bgbank()
                par = ts % 2
                def u_fn(e, jc=jc, ts=ts, bu=bu, base=su * 4096):
                    last = None
                    for cc in range(NC):
                        last = e.matmul(
                            ps[bu][:, 0:512],
                            W[:, base + cc * 512 + jc * 128:base + cc * 512 + (jc + 1) * 128],
                            xnf[:, cc * T + ts * 512:cc * T + ts * 512 + 512],
                            start=(cc == 0), stop=(cc == NC - 1))
                    return last
                t_umm = pb.emit("pe", u_fn,
                                waits=tmax(t_wdma[(e_i, 0)], bank_free[bu]), inc="pe")
                t_ue = pb.emit("act", lambda e, bu=bu, par=par, jc=jc: e.activation(
                    H[:, H_HBUF + par * 2048 + jc * 512:H_HBUF + par * 2048 + (jc + 1) * 512],
                    ps[bu][:, 0:512], AF.Identity),
                    waits=tmax(t_umm, t_hbuf_rd[par]), inc="act")
                bank_free[bu] = t_ue
                ustate[(e_i, ts, jc)] = t_ue
                if ts == 3 and jc == 3:
                    slot_free[su] = t_umm
                    if e_i + 1 == NE:
                        t_shg[0] = pb.emit("pool", lambda e: e.dma_start(
                            W[:, 8192:16384].rearrange("p (a v) -> p a v", a=NC),
                            shgT_h[:, :].rearrange("(a p) v -> p a v", p=128)),
                            waits=tmax(slot_free[2], slot_free[3]), inc="ds2")
            return go

        def g_item(e_i, ts, jc):
            def go():
                sg = (3 * e_i + 1 + 2) % 4
                bg2 = bgbank()
                par = ts % 2
                spar = _scr[0] % 2
                _scr[0] += 1
                t_ue = ustate.pop((e_i, ts, jc))
                def g_fn(e, jc=jc, ts=ts, bg2=bg2, base=sg * 4096):
                    last = None
                    for cc in range(NC):
                        last = e.matmul(
                            ps[bg2][:, 0:512],
                            W[:, base + cc * 512 + jc * 128:base + cc * 512 + (jc + 1) * 128],
                            xnf[:, cc * T + ts * 512:cc * T + ts * 512 + 512],
                            start=(cc == 0), stop=(cc == NC - 1))
                    return last
                t_gmm = pb.emit("pe", g_fn,
                                waits=tmax(t_wdma[(e_i, 1)], bank_free[bg2]), inc="pe")
                scr = H_SCR + spar * 1024
                t_th = pb.emit("act", lambda e, bg2=bg2, scr=scr: e.activation(
                    H[:, scr:scr + 512], ps[bg2][:, 0:512], AF.Tanh, scale=0.5),
                    waits=tmax(t_gmm, t_scr_rd[spar]), inc="act")
                t_2s = pb.emit("dve", lambda e, bg2=bg2, scr=scr: e.scalar_tensor_tensor(
                    H[:, scr + 512:scr + 1024], H[:, scr:scr + 512], 1.0,
                    ps[bg2][:, 0:512], op0=ALU.add, op1=ALU.mult),
                    waits=tmax(t_th), inc="dve")
                bank_free[bg2] = t_2s
                t_h = pb.emit("dve", lambda e, par=par, jc=jc, scr=scr: e.scalar_tensor_tensor(
                    H[:, H_HBUF + par * 2048 + jc * 512:H_HBUF + par * 2048 + (jc + 1) * 512],
                    H[:, scr + 512:scr + 1024], 0.5,
                    H[:, H_HBUF + par * 2048 + jc * 512:H_HBUF + par * 2048 + (jc + 1) * 512],
                    op0=ALU.mult, op1=ALU.mult),
                    waits=tmax(t_ue), inc="dve")
                t_scr_rd[spar] = t_h
                if jc == 3:
                    h_done[(e_i, ts)] = t_h
                if ts == 3 and jc == 3:
                    slot_free[sg] = t_gmm
                    if e_i + 1 < NE:
                        emit_wdma(e_i + 1, 1)
            return go

        def down_item(e_i, ts, tti):
            def go():
                sd = (3 * e_i + 2 + 2) % 4
                par = ts % 2
                tt = ts * 4 + tti
                assert t_comb[tt] is not None and h_done.get((e_i, ts)) is not None
                cbcol = F[:, F_COMB + tt * NE + e_i:F_COMB + tt * NE + e_i + 1]
                for dh in range(2):
                    b = bgbank()
                    def d_fn(e, par=par, tti=tti, dh=dh, b=b, base=sd * 4096):
                        last = None
                        for jc in range(4):
                            last = e.matmul(
                                ps[b][:, 0:512],
                                H[:, H_HBUF + par * 2048 + jc * 512 + tti * 128:
                                   H_HBUF + par * 2048 + jc * 512 + (tti + 1) * 128],
                                W[:, base + jc * 1024 + dh * 512:base + jc * 1024 + dh * 512 + 512],
                                start=(jc == 0), stop=(jc == 3))
                        return last
                    t_dmm = pb.emit("pe", d_fn,
                                    waits=tmax(h_done[(e_i, ts)], t_wdma[(e_i, 2)],
                                               bank_free[b]), inc="pe")
                    t_hbuf_rd[par] = t_dmm
                    t_de = pb.emit("dve", lambda e, tt=tt, dh=dh, b=b, cbcol=cbcol: e.scalar_tensor_tensor(
                        xacc[:, tt * C + dh * 512:tt * C + dh * 512 + 512],
                        ps[b][:, 0:512], cbcol,
                        xacc[:, tt * C + dh * 512:tt * C + dh * 512 + 512],
                        op0=ALU.mult, op1=ALU.add),
                        waits=tmax(t_dmm, t_comb[tt]), inc="dve")
                    bank_free[b] = t_de
                if ts == 3 and tti == 3:
                    slot_free[sd] = ("pe", pb.cnt["pe"])
                    if e_i + 1 < NE:
                        emit_wdma(e_i + 1, 2)
                    if e_i + 2 < NE:
                        emit_wdma(e_i + 2, 0)
                    if e_i + 1 == NE:
                        t_shu[0] = pb.emit("pool", lambda e: e.dma_start(
                            W[:, 0:8192].rearrange("p (a v) -> p a v", a=NC),
                            shuT_h[:, :].rearrange("(a p) v -> p a v", p=128)),
                            waits=tmax(slot_free[0], slot_free[1]), inc="ds0")
            return go

        t_shg = [None]
        t_shu = [None]
        t_shd = [None]

        try:
            t_tr = [None] * NT
            t_qk_evac = {}
            wqk_reader = [None, None]
            t_nmrl = [None] * NT

            def emit_stats(tt):
                xs = xacc[:, tt * C:(tt + 1) * C]
                st = F_STAT + tt * 4
                par = tt % 2
                lns = LNS[:, par * 1024:(par + 1) * 1024]
                pb.emit("dve", lambda e, xs=xs, st=st: e.reduce_sum(
                    F[:, st:st + 1], xs, axis=AX.X), waits=[t_x[tt]], inc="dve", drain=True)
                t_ssq = pb.emit("act", lambda e, xs=xs, st=st, lns=lns: e.activation(
                    lns, xs, AF.Square, accum_out=F[:, st + 3:st + 4]),
                    waits=tmax(t_x[tt], t_tr[tt - 2] if tt >= 2 else None),
                    inc="act", drain=True)
                pb.emit("dve", lambda e, st=st: e.tensor_scalar(
                    F[:, st:st + 1], F[:, st:st + 1], 1.0 / C, None, op0=ALU.mult), inc="dve", drain=True)
                pb.emit("dve", lambda e, st=st: e.tensor_scalar(
                    F[:, st + 1:st + 2], F[:, st + 3:st + 4], 1.0 / C, None, op0=ALU.mult),
                    waits=[t_ssq], inc="dve", drain=True)
                pb.emit("dve", lambda e, st=st: e.tensor_tensor(
                    F[:, st + 2:st + 3], F[:, st:st + 1], F[:, st:st + 1], op=ALU.mult), inc="dve", drain=True)
                t_var = pb.emit("dve", lambda e, st=st: e.tensor_tensor(
                    F[:, st + 1:st + 2], F[:, st + 1:st + 2], F[:, st + 2:st + 3],
                    op=ALU.subtract), inc="dve", drain=True)
                pb.emit("act", lambda e, st=st: e.activation(
                    F[:, st + 1:st + 2], F[:, st + 1:st + 2], AF.Ln,
                    bias=F[:, F_EPS:F_EPS + 1], scale=1.0),
                    waits=tmax(t_eps, t_var), inc="act", drain=True)
                t_rstd = pb.emit("act", lambda e, st=st: e.activation(
                    F[:, st + 1:st + 2], F[:, st + 1:st + 2], AF.Exp, scale=-0.5),
                    inc="act", drain=True)
                t_nmrl[tt] = pb.emit("dve", lambda e, st=st: e.tensor_scalar(
                    F[:, st + 2:st + 3], F[:, st:st + 1], F[:, st + 1:st + 2], -1.0,
                    op0=ALU.mult, op1=ALU.mult), waits=[t_rstd], inc="dve", drain=True)

            emit_stats(0)
            for tt in range(NT):
                xs = xacc[:, tt * C:(tt + 1) * C]
                st = F_STAT + tt * 4
                par = tt % 2
                lns = LNS[:, par * 1024:(par + 1) * 1024]
                t_apply = pb.emit("act", lambda e, xs=xs, st=st, lns=lns: e.activation(
                    lns, xs, AF.Identity,
                    bias=F[:, st + 2:st + 3], scale=F[:, st + 1:st + 2]),
                    waits=[t_nmrl[tt]], inc="act", drain=True)
                def tr_fn(e, lns=lns):
                    last = None
                    for cc in range(NC):
                        last = e.transpose(pstr[:, cc * 128:(cc + 1) * 128],
                                           lns[:, cc * 128:(cc + 1) * 128],
                                           ident[:, :])
                    return last
                t_tr[tt] = pb.emit("pe", tr_fn,
                                   waits=tmax(t_apply, t_ident, bank_free["tr"]), inc="pe")
                t_xnf[tt] = pb.emit("dve", lambda e, tt=tt: e.tensor_copy(
                    xnf[:, :].rearrange("p (a t) -> p a t", a=NC)[:, :, tt * 128:(tt + 1) * 128],
                    pstr[:, :].rearrange("p (a d) -> p a d", a=NC)),
                    waits=[t_tr[tt]], inc="dve")
                bank_free["tr"] = t_xnf[tt]
                if tt + 1 < NT:
                    emit_stats(tt + 1)
                bgq.append(v_item(tt))
                bgq.append(router_item(tt))
                if tt % 4 == 3:
                    ts_ = tt // 4
                    b = bgbank()
                    def qk_fn(e, ts_=ts_, b=b):
                        last = None
                        for cc in range(NC):
                            last = e.matmul(
                                ps[b][:, 0:512],
                                G[:, G_WQK + cc * 128:G_WQK + (cc + 1) * 128],
                                xnf[:, cc * T + ts_ * 512:cc * T + ts_ * 512 + 512],
                                start=(cc == 0), stop=(cc == NC - 1))
                        return last
                    t_mm = pb.emit("pe", qk_fn,
                                   waits=tmax(t_wqk[0], t_xnf[tt], bank_free[b]), inc="pe")
                    if ts_ == 3:
                        wqk_reader[0] = t_mm
                        t_wqk[2] = pb.emit("pool", lambda e: e.dma_start(
                            G[:, G_WQK:G_WQK + 1024].rearrange("p (a d) -> p a d", a=NC),
                            wqkT_h[:, 2 * 128:3 * 128].rearrange("(a p) d -> p a d", p=128)),
                            waits=tmax(wqk_reader[0]), inc="dq0")
                    pb.emit("dve", lambda e, ts_=ts_, b=b: e.tensor_copy(
                        Qb[0:64, ts_ * 512:(ts_ + 1) * 512], ps[b][0:64, 0:512]),
                        waits=[t_mm], inc="dve")
                    t_qk_evac[ts_] = pb.emit("dve", lambda e, ts_=ts_, b=b: e.tensor_copy(
                        Kb[0:64, ts_ * 512:(ts_ + 1) * 512], ps[b][64:128, 0:512]),
                        inc="dve")
                    bank_free[b] = t_qk_evac[ts_]
                if tt >= 2:
                    fill(3)

            stage_stop(1, [
                ("d_xnf", [128, NC * T], BF16, xnf[:, :], tmax(t_xnf[NT - 1])),
            ])

            while bgq:
                bgq.pop(0)()
            pend = None
            for e_i in range(NE):
                for ts in range(4):
                    for jc in range(4):
                        bgq.append(u_item(e_i, ts, jc))
                        bgq.append(g_item(e_i, ts, jc))
                    if pend is not None:
                        pe_, pts = pend
                        for tti in range(4):
                            bgq.append(down_item(pe_, pts, tti))
                    pend = (e_i, ts)
            for tti in range(4):
                bgq.append(down_item(NE - 1, 3, tti))

            t_gpfree = emit_router_top2()

            NPP = 3
            t_rb_used = None
            t_s_last = None
            t_amult = [[], []]
            proj_free = [None, None]

            def emit_qk(h):
                for ts_ in range(4):
                    b = bgbank()
                    def qk_fn(e, ts_=ts_, b=b, h=h):
                        last = None
                        for cc in range(NC):
                            last = e.matmul(
                                ps[b][:, 0:512],
                                G[:, G_WQK + (h % 2) * 1024 + cc * 128:
                                   G_WQK + (h % 2) * 1024 + (cc + 1) * 128],
                                xnf[:, cc * T + ts_ * 512:cc * T + ts_ * 512 + 512],
                                start=(cc == 0), stop=(cc == NC - 1))
                        return last
                    t_mm = pb.emit("pe", qk_fn,
                                   waits=tmax(t_wqk[h], t_xnf[NT - 1], bank_free[b]), inc="pe")
                    if ts_ == 3:
                        wqk_reader[h % 2] = t_mm
                        if h + 2 < NH:
                            par2 = (h + 2) % 2
                            t_wqk[h + 2] = pb.emit("pool", lambda e, h2=h + 2, par2=par2: e.dma_start(
                                G[:, G_WQK + par2 * 1024:G_WQK + par2 * 1024 + 1024]
                                .rearrange("p (a d) -> p a d", a=NC),
                                wqkT_h[:, h2 * 128:(h2 + 1) * 128].rearrange("(a p) d -> p a d", p=128)),
                                waits=tmax(wqk_reader[par2]), inc=f"dq{par2}")
                    pb.emit("dve", lambda e, ts_=ts_, b=b: e.tensor_copy(
                        Qb[0:64, ts_ * 512:(ts_ + 1) * 512], ps[b][0:64, 0:512]),
                        waits=tmax(t_mm, t_s_last), inc="dve")
                    t_qk_evac[ts_] = pb.emit("dve", lambda e, ts_=ts_, b=b: e.tensor_copy(
                        Kb[0:64, ts_ * 512:(ts_ + 1) * 512], ps[b][64:128, 0:512]), inc="dve")
                    bank_free[b] = t_qk_evac[ts_]
                    fill(1)

            def emit_proj(p):
                par = p % 2
                t_mm = None
                for tt in range(NT):
                    for dh in range(2):
                        b = bgbank()
                        def pj_fn(e, tt=tt, dh=dh, b=b, par=par, p=p):
                            m = e.matmul(ps[b][:, 0:512],
                                         AP2[:, par * 2048 + tt * 128:par * 2048 + (tt + 1) * 128],
                                         WPS[:, par * 1024 + dh * 512:par * 1024 + dh * 512 + 512],
                                         start=True, stop=(p != 0))
                            if p == 0:
                                m = e.matmul(ps[b][:, 0:512], ones[0:1, :],
                                             bproj[0:1, dh * 512:dh * 512 + 512],
                                             start=False, stop=True)
                            return m
                        t_mm = pb.emit("pe", pj_fn,
                                       waits=tmax(t_amult[par], t_wps[par], t_ones,
                                                  t_bproj, bank_free[b]), inc="pe")
                        t_pe = pb.emit("dve", lambda e, tt=tt, dh=dh, b=b: e.tensor_tensor(
                            xacc[:, tt * C + dh * 512:tt * C + dh * 512 + 512],
                            ps[b][:, 0:512],
                            xacc[:, tt * C + dh * 512:tt * C + dh * 512 + 512],
                            op=ALU.add), waits=[t_mm], inc="dve")
                        bank_free[b] = t_pe
                proj_free[par] = t_mm
                t_amult[par] = []
                if p + 2 < 8:
                    t_wps[par] = pb.emit("pool", lambda e, p2=p + 2, par=par: e.dma_start(
                        WPS[:, par * 1024:(par + 1) * 1024],
                        wpT_h[p2 * 128:(p2 + 1) * 128, :]),
                        waits=[t_mm], inc=f"dwp{par}")

            for h in range(NH):
                if h > 0:
                    emit_qk(h)
                else:
                    pass
                it = 0
                t_pv_stop = [None] * NQS
                p_used = [None] * NPP
                t_exp_of = [None] * NPP
                pend_q = []
                def push_pending(fn):
                    pend_q.append(fn)
                    if len(pend_q) > 3:
                        pend_q.pop(0)()
                def flush_pending():
                    while pend_q:
                        pend_q.pop(0)()
                for qs in range(NQS):
                    for kt in range(4 * qs + 4):
                        b = SB[it % 2]
                        j = kt - 4 * qs
                        if j > 0:
                            off, masked = 128 * j, True
                        elif j == 0:
                            off, masked = 0, True
                        else:
                            off, masked = 0, False
                        N = 512 - off
                        pp = it % NPP
                        def s_fn(e, b=b, kt=kt, qs=qs, off=off, N=N, masked=masked):
                            m = e.matmul(ps[b][:, 0:N],
                                         Kb[0:65, kt * 128:(kt + 1) * 128],
                                         Qb[0:65, qs * 512 + off:qs * 512 + 512],
                                         start=True, stop=(not masked))
                            if masked:
                                m = e.matmul(ps[b][:, 0:128], ident[:, :], maskm[:, :],
                                             start=False, stop=True)
                            return m
                        t_s = pb.emit("pe", s_fn,
                            waits=tmax(bank_free[b], t_qk_evac[qs], t_qk_evac[kt // 4],
                                       t_kone, t_maskm, t_ident),
                            inc="pe")
                        t_s_last = t_s

                        def pv_back(kt=kt, h=h, pp=pp, off=off, N=N, qs=qs):
                            pvb = PVB[qs % 2]
                            assert t_vready[kt] is not None
                            w = tmax(t_exp_of[pp], t_vready[kt], t_vones)
                            if kt == 0:
                                w = tmax(w, bank_free[pvb])
                            t_pv = pb.emit("pe", lambda e, pvb=pvb, kt=kt, h=h, pp=pp, off=off, N=N: e.matmul(
                                ps[pvb][0:65, off:512],
                                G[:, G_V + kt * 1040 + h * 65:G_V + kt * 1040 + h * 65 + 65],
                                G[:, G_P + pp * 512:G_P + pp * 512 + N],
                                start=(kt == 0), stop=(kt == 4 * qs + 3)),
                                waits=w, inc="pe")
                            p_used[pp] = t_pv
                            if kt == 4 * qs + 3:
                                t_pv_stop[qs] = t_pv
                        push_pending(pv_back)
                        t_e = pb.emit("act", lambda e, b=b, N=N, pp=pp: e.activation(
                            G[:, G_P + pp * 512:G_P + pp * 512 + N],
                            ps[b][:, 0:N], AF.Exp, scale=0.125),
                            waits=tmax(t_s, p_used[pp]), inc="act", drain=True)
                        bank_free[b] = t_e
                        t_exp_of[pp] = t_e
                        fill(1)
                        it += 1
                    flush_pending()
                    pvb = PVB[qs % 2]
                    t_cp = pb.emit("dve", lambda e, pvb=pvb: e.tensor_copy(
                        rrowbf[0:1, 0:512], ps[pvb][64:65, 0:512]),
                        waits=tmax(t_pv_stop[qs], t_rb_used), inc="dve", drain=True)
                    fill(1)
                    rbb = bgbank()
                    t_bc = pb.emit("pe", lambda e, rbb=rbb: e.matmul(
                        ps[rbb][0:64, 0:512], ones[0:1, 0:64], rrowbf[0:1, 0:512],
                        start=True, stop=True),
                        waits=tmax(t_cp, t_ones, bank_free[rbb]), inc="pe")
                    t_rb_used = t_bc
                    def ri_fn(e, rbb=rbb):
                        with nc.allow_low_precision(reason="rowsum rinv is consumed as a bf16 elementwise operand"):
                            return e.reciprocal(rbsb[:, :], ps[rbb][0:64, 0:512])
                    t_rbe = pb.emit("dve", ri_fn, waits=[t_bc], inc="dve")
                    bank_free[rbb] = t_rbe
                    par = (h // 2) % 2
                    r0 = (h % 2) * 64
                    t_am = pb.emit("dve", lambda e, pvb=pvb, par=par, r0=r0, qs=qs: e.tensor_tensor(
                        AP2[r0:r0 + 64, par * 2048 + qs * 512:par * 2048 + qs * 512 + 512],
                        ps[pvb][0:64, 0:512], rbsb[:, :], op=ALU.mult),
                        waits=tmax(t_rbe, proj_free[par]), inc="dve")
                    bank_free[pvb] = t_am
                    t_amult[par].append(t_am)
                if h % 2 == 1 and h // 2 >= 1:
                    emit_proj(h // 2 - 1)
                fill(1)
            emit_proj(7)
            pe_attn_done = ("pe", pb.cnt["pe"])

            stage_stop(3, [
                ("d_xacc", [128, NT * C], F32, xacc[:, :], tmax(proj_free[1])),
            ])

            while bgq:
                bgq.pop(0)()

            t_gmm_last = None
            t_hid = {}
            for jc in range(NC):
                for ts in range(4):
                    b = bgbank()
                    def sg_fn(e, jc=jc, ts=ts, b=b):
                        last = None
                        for cc in range(NC):
                            last = e.matmul(
                                ps[b][:, 0:512],
                                W[:, 8192 + cc * C + jc * 128:8192 + cc * C + (jc + 1) * 128],
                                xnf[:, cc * T + ts * 512:cc * T + ts * 512 + 512],
                                start=(cc == 0), stop=(cc == NC - 1))
                        return last
                    t_gmm = pb.emit("pe", sg_fn,
                                    waits=tmax(t_shg[0], bank_free[b], pe_attn_done), inc="pe")
                    t_gmm_last = t_gmm
                    spar = _scr[0] % 2
                    _scr[0] += 1
                    scr = H_SCR + spar * 1024
                    t_th = pb.emit("act", lambda e, b=b, scr=scr: e.activation(
                        H[:, scr:scr + 512], ps[b][:, 0:512], AF.Tanh, scale=0.5),
                        waits=tmax(t_gmm, t_scr_rd[spar]), inc="act")
                    t_2s = pb.emit("dve", lambda e, b=b, scr=scr, jc=jc, ts=ts: e.scalar_tensor_tensor(
                        G[:, G_SHH + jc * 2048 + ts * 512:G_SHH + jc * 2048 + (ts + 1) * 512],
                        H[:, scr:scr + 512], 1.0, ps[b][:, 0:512],
                        op0=ALU.add, op1=ALU.mult),
                        waits=tmax(t_th, pe_attn_done), inc="dve")
                    bank_free[b] = t_2s
                    t_scr_rd[spar] = t_2s
                    t_hid[(jc, ts)] = t_2s
            t_shd[0] = pb.emit("pool", lambda e: e.dma_start(
                W[:, 8192:16384].rearrange("p (a v) -> p a v", a=NC),
                shdT_h[:, :].rearrange("(a p) v -> p a v", p=128)),
                waits=tmax(t_gmm_last, t_shg[0]), inc="ds3")
            t_hid_last = None
            for jc in range(NC):
                for ts in range(4):
                    b = bgbank()
                    def su_fn(e, jc=jc, ts=ts, b=b):
                        last = None
                        for cc in range(NC):
                            last = e.matmul(
                                ps[b][:, 0:512],
                                W[:, cc * C + jc * 128:cc * C + (jc + 1) * 128],
                                xnf[:, cc * T + ts * 512:cc * T + ts * 512 + 512],
                                start=(cc == 0), stop=(cc == NC - 1))
                        return last
                    t_umm = pb.emit("pe", su_fn,
                                    waits=tmax(t_shu[0], bank_free[b]), inc="pe")
                    t_hd = pb.emit("dve", lambda e, b=b, jc=jc, ts=ts: e.scalar_tensor_tensor(
                        G[:, G_SHH + jc * 2048 + ts * 512:G_SHH + jc * 2048 + (ts + 1) * 512],
                        G[:, G_SHH + jc * 2048 + ts * 512:G_SHH + jc * 2048 + (ts + 1) * 512],
                        0.5, ps[b][:, 0:512], op0=ALU.mult, op1=ALU.mult),
                        waits=tmax(t_umm, t_hid[(jc, ts)]), inc="dve")
                    bank_free[b] = t_hd
                    t_hid_last = t_hd
            t_out_ready = [None] * NT
            for tt in range(NT):
                for dh in range(2):
                    b = bgbank()
                    def sd_fn(e, tt=tt, dh=dh, b=b):
                        last = None
                        for jc in range(NC):
                            last = e.matmul(
                                ps[b][:, 0:512],
                                G[:, G_SHH + jc * 2048 + tt * 128:G_SHH + jc * 2048 + (tt + 1) * 128],
                                W[:, 8192 + jc * 1024 + dh * 512:8192 + jc * 1024 + dh * 512 + 512],
                                start=(jc == 0), stop=(jc == NC - 1))
                        return last
                    t_dmm = pb.emit("pe", sd_fn,
                                    waits=tmax(t_hid_last, t_shd[0], bank_free[b]), inc="pe")
                    t_de = pb.emit("dve", lambda e, tt=tt, dh=dh, b=b: e.tensor_tensor(
                        xacc[:, tt * C + dh * 512:tt * C + dh * 512 + 512],
                        ps[b][:, 0:512],
                        xacc[:, tt * C + dh * 512:tt * C + dh * 512 + 512],
                        op=ALU.add), waits=[t_dmm], inc="dve", drain=True)
                    bank_free[b] = t_de
                    t_out_ready[tt] = t_de

            for g in range(8):
                eng = "sp" if g % 2 == 0 else "pool"
                pb.emit(eng, lambda e, g=g: e.dma_start(
                    out_h[:, :].rearrange("(r p) n -> p r n", p=128)[:, 2 * g:2 * g + 2, :],
                    xacc[:, :].rearrange("p (r n) -> p r n", n=C)[:, 2 * g:2 * g + 2, :]),
                    waits=tmax(t_out_ready[2 * g + 1]), inc="dout")
            final_dout = pb.cnt["dout"]
            pb.emit("sp", lambda e: e.nop(), waits=[("dout", final_dout)])

        except _Stop:
            pass

        def body(eng_name):
            waited = {}
            def run(eng):
                for ws, fn, inc, drain in pb.lists[eng_name]:
                    for s, c in ws:
                        if waited.get(s, 0) < c:
                            eng.wait_ge(sems[s], c)
                            waited[s] = c
                    ins = fn(eng)
                    if drain:
                        ins = eng.drain()
                    if inc:
                        ins.then_inc(sems[inc], 16 if inc.startswith("d") else 1)
            return run

        block.sync(body("sp"))
        block.gpsimd(body("pool"))
        block.scalar(body("act"))
        block.vector(body("dve"))
        block.tensor(body("pe"))

    return nc


_cache = {}


def host_prep(L0_attn_x, L0_attn_w_qkv, L0_attn_w_proj, L0_attn_b_proj,
              L0_ffn_router_w, L0_ffn_router_bias, L0_ffn_up_proj, L0_ffn_gate_proj,
              L0_ffn_down_proj, L0_ffn_sh_gate, L0_ffn_sh_up, L0_ffn_sh_down):
    x = np.asarray(L0_attn_x, dtype=np.float32)
    wq = np.asarray(L0_attn_w_qkv, dtype=np.float32)
    wp = np.asarray(L0_attn_w_proj, dtype=np.float32)
    bp = np.asarray(L0_attn_b_proj, dtype=np.float32)
    rw = np.asarray(L0_ffn_router_w, dtype=np.float32)
    rb = np.asarray(L0_ffn_router_bias, dtype=np.float32)
    up = np.ascontiguousarray(np.asarray(L0_ffn_up_proj, dtype=np.float32))
    gate = np.ascontiguousarray(np.asarray(L0_ffn_gate_proj, dtype=np.float32))
    down = np.ascontiguousarray(np.asarray(L0_ffn_down_proj, dtype=np.float32))
    shg = np.asarray(L0_ffn_sh_gate, dtype=np.float32)
    shu = np.asarray(L0_ffn_sh_up, dtype=np.float32)
    shd = np.asarray(L0_ffn_sh_down, dtype=np.float32)

    wqkT = np.ascontiguousarray(np.transpose(wq[:, 0:128, :], (2, 0, 1)).reshape(C, NH * 128))
    wvT = np.ascontiguousarray(np.transpose(wq[:, 128:192, :], (2, 0, 1)).reshape(C, NH * HS))
    wpT = np.ascontiguousarray(wp.T)
    routerT = np.ascontiguousarray(rw.T)
    rbias_b = np.ascontiguousarray(np.broadcast_to(rb.reshape(1, NE), (128, NE)))
    bproj = np.ascontiguousarray(bp.reshape(1, C))
    shgT = np.ascontiguousarray(shg.T)
    shuT = np.ascontiguousarray(shu.T)
    shdT = np.ascontiguousarray(shd.T)
    ident = np.eye(128, dtype=ml_dtypes.bfloat16)
    maskneg = np.where(np.triu(np.ones((128, 128), np.float32)) > 0, 0.0, -1e9).astype(ml_dtypes.bfloat16)
    ones128 = np.ones((1, 128), dtype=ml_dtypes.bfloat16)
    iotam = np.ascontiguousarray(np.broadcast_to(
        (np.arange(NE, dtype=np.float32) - 16.0).reshape(1, NE), (128, NE)))

    shared = dict(wqkt=wqkT, wvt=wvT, wpt=wpT, bproj=bproj, routert=routerT,
                  rbias=rbias_b, up=up, gate=gate, down=down, shgt=shgT,
                  shut=shuT, shdt=shdT, ident=ident,
                  ones128=ones128, iotam=iotam, maskneg=maskneg)
    return x, shared


def kernel(**inputs):
    x, shared = host_prep(**inputs)
    if "nc" not in _cache:
        _cache["nc"] = build_nc()
    nc = _cache["nc"]
    in_maps = []
    for i in range(8):
        m = dict(shared)
        m["x"] = np.ascontiguousarray(x[i])
        in_maps.append(m)

    res = run_bass_kernel_spmd(nc, in_maps, core_ids=list(range(8)))
    out = np.stack([np.asarray(r["y"], dtype=np.float32) for r in res.results], axis=0)
    return out


# version 27
# speedup vs baseline: 1.1875x; 1.1875x over previous
import numpy as np
import ml_dtypes
import concourse.bass as bass
import concourse.mybir as mybir
from concourse.bass_utils import run_bass_kernel_spmd
from contextlib import ExitStack

F32 = mybir.dt.float32
BF16 = mybir.dt.bfloat16
AF = mybir.ActivationFunctionType
ALU = mybir.AluOpType
AX = mybir.AxisListType

T = 2048
C = 1024
NT = 16
NC = 8
NH = 16
HS = 64
NE = 16
ED = 512
NQS = 4

G_V = 0
G_P = 16640
G_WQK = 18176
G_COLS = 20224
G_SHH = 0

W_COLS = 16384

H_HBUF = 0
H_SCR = 4096
H_COLS = 6144

F_STAT = 0
F_EPS = 64
F_Z = 66
F_COMB = 322
F_T2 = 578
F_MSK = 706
F_IOTA = 770
F_COLS = 786

_STAGE = 9
_DUMPS = []


class _Stop(Exception):
    pass


ENGINES = ("sp", "pool", "act", "dve", "pe")
ENG_SEM = {"sp": None, "pool": None, "act": "act", "dve": "dve", "pe": "pe"}
SEM_NAMES = ("pe", "dve", "act", "dc", "dout",
             "dw", "dwp0", "dwp1", "dq0", "dq1", "ds0", "ds1", "ds2", "ds3") + tuple(
    f"dx{i}" for i in range(16))




class PB:

    def __init__(self):
        self.lists = {e: [] for e in ENGINES}
        self.cnt = {k: 0 for k in SEM_NAMES}

    def emit(self, eng, fn, waits=(), inc=None, drain=False):
        own = ENG_SEM[eng]
        ws = []
        for w in waits:
            if w is None:
                continue
            s, c = w
            if c <= 0:
                continue
            if s == own:
                continue
            assert c <= self.cnt[s], f"wait on future: {s} {c} > {self.cnt[s]}"
            ws.append((s, c))
        tok = None
        if inc:
            d = 16 if inc.startswith("d") else 1
            self.cnt[inc] += d
            tok = (inc, self.cnt[inc])
        self.lists[eng].append((ws, fn, inc, drain))
        return tok


def tmax(*toks):
    best = {}
    for t in toks:
        if t is None:
            continue
        if isinstance(t, list):
            for s, c in t:
                if best.get(s, -1) < c:
                    best[s] = c
            continue
        s, c = t
        if best.get(s, -1) < c:
            best[s] = c
    return [(s, c) for s, c in best.items()]


def build_nc():
    nc = bass.Bass("TRN2", target_bir_lowering=False)

    x_h = nc.dram_tensor("x", [T, C], F32, kind="ExternalInput")
    wqkT_h = nc.dram_tensor("wqkt", [C, NH * 128], F32, kind="ExternalInput")
    wvT_h = nc.dram_tensor("wvt", [C, NH * HS], F32, kind="ExternalInput")
    wpT_h = nc.dram_tensor("wpt", [C, C], F32, kind="ExternalInput")
    bproj_h = nc.dram_tensor("bproj", [1, C], F32, kind="ExternalInput")
    routerT_h = nc.dram_tensor("routert", [C, NE], F32, kind="ExternalInput")
    rbias_h = nc.dram_tensor("rbias", [128, NE], F32, kind="ExternalInput")
    up_h = nc.dram_tensor("up", [NE, C, ED], F32, kind="ExternalInput")
    gate_h = nc.dram_tensor("gate", [NE, C, ED], F32, kind="ExternalInput")
    down_h = nc.dram_tensor("down", [NE, ED, C], F32, kind="ExternalInput")
    shgT_h = nc.dram_tensor("shgt", [C, C], F32, kind="ExternalInput")
    shuT_h = nc.dram_tensor("shut", [C, C], F32, kind="ExternalInput")
    shdT_h = nc.dram_tensor("shdt", [C, C], F32, kind="ExternalInput")
    ident_h = nc.dram_tensor("ident", [128, 128], BF16, kind="ExternalInput")
    maskneg_h = nc.dram_tensor("maskneg", [128, 128], BF16, kind="ExternalInput")
    ones_h = nc.dram_tensor("ones128", [1, 128], BF16, kind="ExternalInput")
    iota_h = nc.dram_tensor("iotam", [128, NE], F32, kind="ExternalInput")
    out_h = nc.dram_tensor("y", [T, C], F32, kind="ExternalOutput")

    with ExitStack() as ctx:
        sb = lambda n, s, d: ctx.enter_context(nc.sbuf_tensor(n, s, d))
        xacc = sb("xacc", [128, NT * C], F32)
        xnf = sb("xnf", [128, NC * T], BF16)
        G = sb("G", [128, G_COLS], BF16)
        W = sb("W", [128, W_COLS], BF16)
        H = sb("H", [128, H_COLS], BF16)
        AP2 = sb("AP2", [128, 4096], BF16)
        WPS = sb("WPS", [128, 2048], BF16)
        LNS = AP2
        F = sb("F", [128, F_COLS], F32)
        Qb = sb("Qb", [65, T], BF16)
        Kb = sb("Kb", [65, T], BF16)
        ident = sb("identity", [128, 128], BF16)
        maskm = sb("maskneg_s", [128, 128], BF16)
        ones = sb("ones_r", [1, 128], BF16)
        bproj = sb("bproj_r", [1, C], BF16)
        routerT = sb("routerT", [128, NC * NE], BF16)
        rbias = sb("rbias_b", [128, NE], F32)
        rrowbf = sb("rrowbf", [1, 512], BF16)
        rbsb = sb("rbsb", [64, 512], BF16)

        ps = [ctx.enter_context(nc.psum_tensor(f"ps{i}", [128, 512], F32))
              for i in range(7)]
        pstr = ctx.enter_context(nc.psum_tensor("pstr", [128, 1024], BF16))

        sems = {k: ctx.enter_context(nc.semaphore(k)) for k in SEM_NAMES}
        block = ctx.enter_context(nc.Block())

        pb = PB()
        bank_free = {i: None for i in range(7)}
        bank_free["tr"] = None
        SB = [0, 1]
        PVB = [2, 3]
        _bg = [0]

        def bgbank():
            b = 4 + (_bg[0] % 3)
            _bg[0] += 1
            return b

        _DUMPS.clear()

        def stage_stop(k, dumps):
            if _STAGE != k:
                return
            for name, shape, dt, ap, waits in dumps:
                d = nc.dram_tensor(name, shape, dt, kind="ExternalOutput")
                pb.emit("sp", lambda e, d=d, ap=ap: e.dma_start(d[:, :], ap),
                        waits=waits, inc="dout")
                _DUMPS.append(name)
            fin = pb.cnt["dout"]
            pb.emit("sp", lambda e: e.nop(), waits=[("dout", fin)])
            raise _Stop()

        t_x = [None] * 16

        def emit_x(c):
            eng = "sp" if (c % 4) < 2 else "pool"
            t_x[c] = pb.emit(eng, lambda e, c=c: e.dma_start(
                xacc[:, :].rearrange("p (r n) -> p r n", n=C)[:, c:c + 1, :],
                x_h[:, :].rearrange("(r p) n -> p r n", p=128)[:, c:c + 1, :]),
                inc=f"dx{c}")

        emit_x(0); emit_x(1)
        for tns, hsrc in ((ident, ident_h), (maskm, maskneg_h),
                          (ones, ones_h), (rbias, rbias_h)):
            pb.emit("sp", lambda e, t=tns, s_=hsrc: e.dma_start(t[:, :], s_[:, :]), inc="dc")
        pb.emit("sp", lambda e: e.dma_start(F[:, F_IOTA:F_IOTA + NE], iota_h[:, :]), inc="dc")
        pb.emit("pool", lambda e: e.dma_start(bproj[:, :], bproj_h[:, :]), inc="dw")
        t_consts = ("dc", pb.cnt["dc"])
        t_kone = pb.emit("dve", lambda e: e.memset(Kb[64:65, :], 1.0), inc="dve", drain=True)
        t_kone = pb.emit("dve", lambda e: e.memset(Qb[64:65, :], 0.0), inc="dve", drain=True)
        t_eps = pb.emit("dve", lambda e: e.memset(F[:, F_EPS:F_EPS + 1], 1e-5), inc="dve", drain=True)
        t_vones = pb.emit("dve", lambda e: e.memset(
            G[:, G_V:G_V + NT * 1040].rearrange("p (k h d) -> p k h d", h=NH, d=65)[:, :, :, 64:65],
            1.0), inc="dve", drain=True)
        t_ident = t_maskm = t_ones = t_rbias = t_iota = t_consts
        t_bproj = ("dw", pb.cnt["dw"])

        t_wv = [None, None]

        def emit_wv(s):
            t_wv[s] = pb.emit("pool", lambda e, s=s: e.dma_start(
                W[:, s * 4096:(s + 1) * 4096].rearrange("p (a v) -> p a v", a=4),
                wvT_h[:, :].rearrange("(a p) v -> p a v", p=128)[:, 4 * s:4 * s + 4, :]),
                inc=f"ds{s}")

        t_wqk = [None] * NH

        def emit_wqk01(h):
            t_wqk[h] = pb.emit("pool", lambda e, h=h: e.dma_start(
                G[:, G_WQK + (h % 2) * 1024:G_WQK + (h % 2) * 1024 + 1024]
                .rearrange("p (a d) -> p a d", a=NC),
                wqkT_h[:, h * 128:(h + 1) * 128].rearrange("(a p) d -> p a d", p=128)),
                inc=f"dq{h % 2}")

        t_wps = [None, None]

        def emit_wps0(p):
            t_wps[p] = pb.emit("pool", lambda e, p=p: e.dma_start(
                WPS[:, p * 1024:(p + 1) * 1024],
                wpT_h[p * 128:(p + 1) * 128, :]), inc=f"dwp{p % 2}")

        t_wdma = {}
        slot_free = [None] * 4

        def emit_wdma(e_i, m):
            src = (up_h, gate_h, down_h)[m]
            s = (3 * e_i + m + 2) % 4
            inap = src[e_i, :, :].rearrange("(a p) j -> p a j", p=128)
            outap = W[:, s * 4096:(s + 1) * 4096].rearrange(
                "p (a j) -> p a j", a=NC if m < 2 else 4)
            t_wdma[(e_i, m)] = pb.emit("pool", lambda e, inap=inap, outap=outap: e.dma_start(
                outap, inap), waits=tmax(slot_free[s]), inc=f"ds{s}")

        for c in (4, 5, 8, 9, 12, 13):
            emit_x(c)
        emit_x(2); emit_x(3)
        emit_wv(0); emit_wv(1)
        emit_x(6); emit_x(7)
        emit_wqk01(0)
        pb.emit("pool", lambda e: e.dma_start(
            routerT[:, :].rearrange("p (a v) -> p a v", a=NC),
            routerT_h[:, :].rearrange("(a p) v -> p a v", p=128)), inc="dw")
        t_routerT = ("dw", pb.cnt["dw"])
        t_bproj = t_routerT
        emit_x(10); emit_x(11)
        emit_wps0(0); emit_wps0(1)
        emit_wdma(0, 0)
        emit_x(14); emit_x(15)
        emit_wdma(0, 1)
        emit_wqk01(1)

        bgq = []

        def fill(n):
            for _ in range(n):
                if not bgq:
                    return
                bgq.pop(0)()

        t_xnf = [None] * NT
        t_vready = [None] * NT
        t_comb = [None] * NT
        t_vlast = [None]

        def v_item(tt):
            def go():
                toks = []
                for half in range(2):
                    b = bgbank()
                    def v_fn(e, tt=tt, half=half, b=b):
                        last = None
                        for cc in range(NC):
                            last = e.matmul(
                                ps[b][:, 0:512],
                                xnf[:, cc * T + tt * 128:cc * T + (tt + 1) * 128],
                                W[:, cc * C + half * 512:cc * C + half * 512 + 512],
                                start=(cc == 0), stop=(cc == NC - 1))
                        return last
                    t_mm = pb.emit("pe", v_fn,
                                   waits=tmax(t_xnf[tt], t_wv[0], t_wv[1], bank_free[b]),
                                   inc="pe")
                    t_ev = pb.emit("dve", lambda e, tt=tt, half=half, b=b: e.tensor_copy(
                        G[:, G_V + tt * 1040:G_V + (tt + 1) * 1040]
                        .rearrange("p (h d) -> p h d", d=65)[:, half * 8:half * 8 + 8, 0:64],
                        ps[b][:, 0:512].rearrange("p (h d) -> p h d", d=64)),
                        waits=[t_mm], inc="dve")
                    bank_free[b] = t_ev
                    toks.append(t_ev)
                    t_vlast[0] = t_mm
                t_vready[tt] = toks[-1]
                if tt == NT - 1:
                    slot_free[0] = t_vlast[0]
                    slot_free[1] = t_vlast[0]
                    emit_wdma(0, 2)
                    emit_wdma(1, 0)
            return go

        def router_item(tt):
            z = F[:, F_Z + tt * NE:F_Z + (tt + 1) * NE]

            def go():
                b = bgbank()
                def r_fn(e, tt=tt, b=b):
                    last = None
                    for cc in range(NC):
                        last = e.matmul(
                            ps[b][:, 0:NE],
                            xnf[:, cc * T + tt * 128:cc * T + (tt + 1) * 128],
                            routerT[:, cc * NE:(cc + 1) * NE],
                            start=(cc == 0), stop=(cc == NC - 1))
                    return last
                t_rmm = pb.emit("pe", r_fn,
                                waits=tmax(t_xnf[tt], t_routerT, bank_free[b]), inc="pe")
                t_z = pb.emit("dve", lambda e, z=z, b=b: e.tensor_tensor(
                    z, ps[b][:, 0:NE], rbias[:, :], op=ALU.add),
                    waits=tmax(t_rmm, t_rbias, t_iota), inc="dve", drain=True)
                bank_free[b] = t_z
            return go

        def emit_router_top2():
            z3 = F[:, F_Z:F_Z + 256].rearrange("p (t e) -> p t e", e=NE)
            cb3 = F[:, F_COMB:F_COMB + 256].rearrange("p (t e) -> p t e", e=NE)
            m1 = F[:, F_T2 + 0:F_T2 + 16]
            m2 = F[:, F_T2 + 16:F_T2 + 32]
            e2 = F[:, F_T2 + 32:F_T2 + 48]
            s1v = F[:, F_T2 + 48:F_T2 + 64]
            s2v = F[:, F_T2 + 64:F_T2 + 80]
            den = F[:, F_T2 + 80:F_T2 + 96]
            i1 = F[:, F_T2 + 96:F_T2 + 112]
            i2 = F[:, F_T2 + 112:F_T2 + 128]
            msk1 = AP2[:, 0:256].rearrange("p (t e) -> p t e", e=NE)
            msk2 = AP2[:, 256:512].rearrange("p (t e) -> p t e", e=NE)
            tmpb = AP2[:, 512:768].rearrange("p (t e) -> p t e", e=NE)
            iotb = F[:, F_IOTA:F_IOTA + NE].rearrange("p e -> p () e").broadcast_to([128, 16, NE])
            bt = lambda a: a.rearrange("p t -> p t ()").broadcast_to([128, 16, NE])
            pb.emit("dve", lambda e: e.tensor_reduce(m1, z3, axis=AX.X, op=ALU.max),
                    inc="dve", drain=True)
            pb.emit("dve", lambda e: e.tensor_tensor(msk1, z3, bt(m1), op=ALU.is_equal),
                    waits=tmax(t_tr[NT - 2]), inc="dve", drain=True)
            pb.emit("dve", lambda e: e.tensor_tensor(tmpb, msk1, iotb, op=ALU.mult),
                    inc="dve", drain=True)
            pb.emit("dve", lambda e: e.tensor_reduce(i1, tmpb, axis=AX.X, op=ALU.min),
                    inc="dve", drain=True)
            pb.emit("dve", lambda e: e.tensor_tensor(msk1, iotb, bt(i1), op=ALU.is_equal),
                    inc="dve", drain=True)
            pb.emit("dve", lambda e: e.scalar_tensor_tensor(
                z3, msk1, -1e30, z3, op0=ALU.mult, op1=ALU.add), inc="dve", drain=True)
            pb.emit("dve", lambda e: e.tensor_reduce(m2, z3, axis=AX.X, op=ALU.max),
                    inc="dve", drain=True)
            pb.emit("dve", lambda e: e.tensor_tensor(msk2, z3, bt(m2), op=ALU.is_equal),
                    inc="dve", drain=True)
            pb.emit("dve", lambda e: e.tensor_tensor(tmpb, msk2, iotb, op=ALU.mult),
                    inc="dve", drain=True)
            pb.emit("dve", lambda e: e.tensor_reduce(i2, tmpb, axis=AX.X, op=ALU.min),
                    inc="dve", drain=True)
            pb.emit("dve", lambda e: e.tensor_tensor(msk2, iotb, bt(i2), op=ALU.is_equal),
                    inc="dve", drain=True)
            t_sub = pb.emit("dve", lambda e: e.tensor_tensor(e2, m2, m1, op=ALU.subtract),
                            inc="dve", drain=True)
            t_exp = pb.emit("act", lambda e: e.activation(e2, e2, AF.Exp),
                            waits=[t_sub], inc="act", drain=True)
            pb.emit("dve", lambda e: e.tensor_scalar(
                den, e2, 1.0, None, op0=ALU.add), waits=[t_exp], inc="dve", drain=True)
            pb.emit("dve", lambda e: e.reciprocal(s1v, den), inc="dve", drain=True)
            pb.emit("dve", lambda e: e.tensor_scalar(
                s2v, s1v, -1.0, 1.0, op0=ALU.mult, op1=ALU.add), inc="dve", drain=True)
            pb.emit("dve", lambda e: e.tensor_tensor(cb3, msk1, bt(s1v), op=ALU.mult),
                    inc="dve", drain=True)
            pb.emit("dve", lambda e: e.tensor_tensor(z3, msk2, bt(s2v), op=ALU.mult),
                    inc="dve", drain=True)
            tok = pb.emit("dve", lambda e: e.tensor_tensor(cb3, cb3, z3, op=ALU.add),
                          inc="dve", drain=True)
            for tt in range(NT):
                t_comb[tt] = tok
            return tok

        h_done = {}
        t_hbuf_rd = [None, None]
        t_scr_rd = [None, None]
        _scr = [0]

        ustate = {}

        def u_item(e_i, ts, jc):
            def go():
                su = (3 * e_i + 0 + 2) % 4
                bu = bgbank()
                par = ts % 2
                def u_fn(e, jc=jc, ts=ts, bu=bu, base=su * 4096):
                    last = None
                    for cc in range(NC):
                        last = e.matmul(
                            ps[bu][:, 0:512],
                            W[:, base + cc * 512 + jc * 128:base + cc * 512 + (jc + 1) * 128],
                            xnf[:, cc * T + ts * 512:cc * T + ts * 512 + 512],
                            start=(cc == 0), stop=(cc == NC - 1))
                    return last
                t_umm = pb.emit("pe", u_fn,
                                waits=tmax(t_wdma[(e_i, 0)], bank_free[bu]), inc="pe")
                t_ue = pb.emit("act", lambda e, bu=bu, par=par, jc=jc: e.activation(
                    H[:, H_HBUF + par * 2048 + jc * 512:H_HBUF + par * 2048 + (jc + 1) * 512],
                    ps[bu][:, 0:512], AF.Identity),
                    waits=tmax(t_umm, t_hbuf_rd[par]), inc="act")
                bank_free[bu] = t_ue
                ustate[(e_i, ts, jc)] = t_ue
                if ts == 3 and jc == 3:
                    slot_free[su] = t_umm
                    if e_i + 1 == NE:
                        t_shg[0] = pb.emit("pool", lambda e: e.dma_start(
                            W[:, 8192:16384].rearrange("p (a v) -> p a v", a=NC),
                            shgT_h[:, :].rearrange("(a p) v -> p a v", p=128)),
                            waits=tmax(slot_free[2], slot_free[3]), inc="ds2")
            return go

        def g_item(e_i, ts, jc):
            def go():
                sg = (3 * e_i + 1 + 2) % 4
                bg2 = bgbank()
                par = ts % 2
                spar = _scr[0] % 2
                _scr[0] += 1
                t_ue = ustate.pop((e_i, ts, jc))
                def g_fn(e, jc=jc, ts=ts, bg2=bg2, base=sg * 4096):
                    last = None
                    for cc in range(NC):
                        last = e.matmul(
                            ps[bg2][:, 0:512],
                            W[:, base + cc * 512 + jc * 128:base + cc * 512 + (jc + 1) * 128],
                            xnf[:, cc * T + ts * 512:cc * T + ts * 512 + 512],
                            start=(cc == 0), stop=(cc == NC - 1))
                    return last
                t_gmm = pb.emit("pe", g_fn,
                                waits=tmax(t_wdma[(e_i, 1)], bank_free[bg2]), inc="pe")
                scr = H_SCR + spar * 1024
                t_th = pb.emit("act", lambda e, bg2=bg2, scr=scr: e.activation(
                    H[:, scr:scr + 512], ps[bg2][:, 0:512], AF.Tanh, scale=0.5),
                    waits=tmax(t_gmm, t_scr_rd[spar]), inc="act")
                t_2s = pb.emit("dve", lambda e, bg2=bg2, scr=scr: e.scalar_tensor_tensor(
                    H[:, scr + 512:scr + 1024], H[:, scr:scr + 512], 1.0,
                    ps[bg2][:, 0:512], op0=ALU.add, op1=ALU.mult),
                    waits=tmax(t_th), inc="dve")
                bank_free[bg2] = t_2s
                t_h = pb.emit("dve", lambda e, par=par, jc=jc, scr=scr: e.scalar_tensor_tensor(
                    H[:, H_HBUF + par * 2048 + jc * 512:H_HBUF + par * 2048 + (jc + 1) * 512],
                    H[:, scr + 512:scr + 1024], 0.5,
                    H[:, H_HBUF + par * 2048 + jc * 512:H_HBUF + par * 2048 + (jc + 1) * 512],
                    op0=ALU.mult, op1=ALU.mult),
                    waits=tmax(t_ue), inc="dve")
                t_scr_rd[spar] = t_h
                if jc == 3:
                    h_done[(e_i, ts)] = t_h
                if ts == 3 and jc == 3:
                    slot_free[sg] = t_gmm
                    if e_i + 1 < NE:
                        emit_wdma(e_i + 1, 1)
            return go

        def down_item(e_i, ts, tti):
            def go():
                sd = (3 * e_i + 2 + 2) % 4
                par = ts % 2
                tt = ts * 4 + tti
                assert t_comb[tt] is not None and h_done.get((e_i, ts)) is not None
                cbcol = F[:, F_COMB + tt * NE + e_i:F_COMB + tt * NE + e_i + 1]
                for dh in range(2):
                    b = bgbank()
                    def d_fn(e, par=par, tti=tti, dh=dh, b=b, base=sd * 4096):
                        last = None
                        for jc in range(4):
                            last = e.matmul(
                                ps[b][:, 0:512],
                                H[:, H_HBUF + par * 2048 + jc * 512 + tti * 128:
                                   H_HBUF + par * 2048 + jc * 512 + (tti + 1) * 128],
                                W[:, base + jc * 1024 + dh * 512:base + jc * 1024 + dh * 512 + 512],
                                start=(jc == 0), stop=(jc == 3))
                        return last
                    t_dmm = pb.emit("pe", d_fn,
                                    waits=tmax(h_done[(e_i, ts)], t_wdma[(e_i, 2)],
                                               bank_free[b]), inc="pe")
                    t_hbuf_rd[par] = t_dmm
                    t_de = pb.emit("dve", lambda e, tt=tt, dh=dh, b=b, cbcol=cbcol: e.scalar_tensor_tensor(
                        xacc[:, tt * C + dh * 512:tt * C + dh * 512 + 512],
                        ps[b][:, 0:512], cbcol,
                        xacc[:, tt * C + dh * 512:tt * C + dh * 512 + 512],
                        op0=ALU.mult, op1=ALU.add),
                        waits=tmax(t_dmm, t_comb[tt]), inc="dve")
                    bank_free[b] = t_de
                if ts == 3 and tti == 3:
                    slot_free[sd] = ("pe", pb.cnt["pe"])
                    if e_i + 1 < NE:
                        emit_wdma(e_i + 1, 2)
                    if e_i + 2 < NE:
                        emit_wdma(e_i + 2, 0)
                    if e_i + 1 == NE:
                        t_shu[0] = pb.emit("pool", lambda e: e.dma_start(
                            W[:, 0:8192].rearrange("p (a v) -> p a v", a=NC),
                            shuT_h[:, :].rearrange("(a p) v -> p a v", p=128)),
                            waits=tmax(slot_free[0], slot_free[1]), inc="ds0")
            return go

        t_shg = [None]
        t_shu = [None]
        t_shd = [None]

        try:
            t_tr = [None] * NT
            t_qk_evac = {}
            wqk_reader = [None, None]
            t_nmrl = [None] * NT

            def emit_stats(tt):
                xs = xacc[:, tt * C:(tt + 1) * C]
                st = F_STAT + tt * 4
                par = tt % 2
                lns = LNS[:, par * 1024:(par + 1) * 1024]
                pb.emit("dve", lambda e, xs=xs, st=st: e.reduce_sum(
                    F[:, st:st + 1], xs, axis=AX.X), waits=[t_x[tt]], inc="dve", drain=True)
                t_ssq = pb.emit("act", lambda e, xs=xs, st=st, lns=lns: e.activation(
                    lns, xs, AF.Square, accum_out=F[:, st + 3:st + 4]),
                    waits=tmax(t_x[tt], t_tr[tt - 2] if tt >= 2 else None),
                    inc="act", drain=True)
                pb.emit("dve", lambda e, st=st: e.tensor_scalar(
                    F[:, st:st + 1], F[:, st:st + 1], 1.0 / C, None, op0=ALU.mult), inc="dve", drain=True)
                pb.emit("dve", lambda e, st=st: e.tensor_scalar(
                    F[:, st + 1:st + 2], F[:, st + 3:st + 4], 1.0 / C, None, op0=ALU.mult),
                    waits=[t_ssq], inc="dve", drain=True)
                pb.emit("dve", lambda e, st=st: e.tensor_tensor(
                    F[:, st + 2:st + 3], F[:, st:st + 1], F[:, st:st + 1], op=ALU.mult), inc="dve", drain=True)
                t_var = pb.emit("dve", lambda e, st=st: e.tensor_tensor(
                    F[:, st + 1:st + 2], F[:, st + 1:st + 2], F[:, st + 2:st + 3],
                    op=ALU.subtract), inc="dve", drain=True)
                pb.emit("act", lambda e, st=st: e.activation(
                    F[:, st + 1:st + 2], F[:, st + 1:st + 2], AF.Ln,
                    bias=F[:, F_EPS:F_EPS + 1], scale=1.0),
                    waits=tmax(t_eps, t_var), inc="act", drain=True)
                t_rstd = pb.emit("act", lambda e, st=st: e.activation(
                    F[:, st + 1:st + 2], F[:, st + 1:st + 2], AF.Exp, scale=-0.5),
                    inc="act", drain=True)
                t_nmrl[tt] = pb.emit("dve", lambda e, st=st: e.tensor_scalar(
                    F[:, st + 2:st + 3], F[:, st:st + 1], F[:, st + 1:st + 2], -1.0,
                    op0=ALU.mult, op1=ALU.mult), waits=[t_rstd], inc="dve", drain=True)

            emit_stats(0)
            for tt in range(NT):
                xs = xacc[:, tt * C:(tt + 1) * C]
                st = F_STAT + tt * 4
                par = tt % 2
                lns = LNS[:, par * 1024:(par + 1) * 1024]
                t_apply = pb.emit("act", lambda e, xs=xs, st=st, lns=lns: e.activation(
                    lns, xs, AF.Identity,
                    bias=F[:, st + 2:st + 3], scale=F[:, st + 1:st + 2]),
                    waits=[t_nmrl[tt]], inc="act", drain=True)
                def tr_fn(e, lns=lns):
                    last = None
                    for cc in range(NC):
                        last = e.transpose(pstr[:, cc * 128:(cc + 1) * 128],
                                           lns[:, cc * 128:(cc + 1) * 128],
                                           ident[:, :])
                    return last
                t_tr[tt] = pb.emit("pe", tr_fn,
                                   waits=tmax(t_apply, t_ident, bank_free["tr"]), inc="pe")
                t_xnf[tt] = pb.emit("dve", lambda e, tt=tt: e.tensor_copy(
                    xnf[:, :].rearrange("p (a t) -> p a t", a=NC)[:, :, tt * 128:(tt + 1) * 128],
                    pstr[:, :].rearrange("p (a d) -> p a d", a=NC)),
                    waits=[t_tr[tt]], inc="dve")
                bank_free["tr"] = t_xnf[tt]
                if tt + 1 < NT:
                    emit_stats(tt + 1)
                bgq.append(v_item(tt))
                bgq.append(router_item(tt))
                if tt % 4 == 3:
                    ts_ = tt // 4
                    b = bgbank()
                    def qk_fn(e, ts_=ts_, b=b):
                        last = None
                        for cc in range(NC):
                            last = e.matmul(
                                ps[b][:, 0:512],
                                G[:, G_WQK + cc * 128:G_WQK + (cc + 1) * 128],
                                xnf[:, cc * T + ts_ * 512:cc * T + ts_ * 512 + 512],
                                start=(cc == 0), stop=(cc == NC - 1))
                        return last
                    t_mm = pb.emit("pe", qk_fn,
                                   waits=tmax(t_wqk[0], t_xnf[tt], bank_free[b]), inc="pe")
                    if ts_ == 3:
                        wqk_reader[0] = t_mm
                        t_wqk[2] = pb.emit("pool", lambda e: e.dma_start(
                            G[:, G_WQK:G_WQK + 1024].rearrange("p (a d) -> p a d", a=NC),
                            wqkT_h[:, 2 * 128:3 * 128].rearrange("(a p) d -> p a d", p=128)),
                            waits=tmax(wqk_reader[0]), inc="dq0")
                    pb.emit("dve", lambda e, ts_=ts_, b=b: e.tensor_copy(
                        Qb[0:64, ts_ * 512:(ts_ + 1) * 512], ps[b][0:64, 0:512]),
                        waits=[t_mm], inc="dve")
                    t_qk_evac[ts_] = pb.emit("dve", lambda e, ts_=ts_, b=b: e.tensor_copy(
                        Kb[0:64, ts_ * 512:(ts_ + 1) * 512], ps[b][64:128, 0:512]),
                        inc="dve")
                    bank_free[b] = t_qk_evac[ts_]
                if tt >= 2:
                    fill(3)

            stage_stop(1, [
                ("d_xnf", [128, NC * T], BF16, xnf[:, :], tmax(t_xnf[NT - 1])),
            ])

            while bgq:
                bgq.pop(0)()
            pend = None
            for e_i in range(NE):
                for ts in range(4):
                    for jc in range(4):
                        bgq.append(u_item(e_i, ts, jc))
                        bgq.append(g_item(e_i, ts, jc))
                    if pend is not None:
                        pe_, pts = pend
                        for tti in range(4):
                            bgq.append(down_item(pe_, pts, tti))
                    pend = (e_i, ts)
            for tti in range(4):
                bgq.append(down_item(NE - 1, 3, tti))

            t_gpfree = emit_router_top2()

            NPP = 3
            t_rb_used = None
            t_s_last = None
            t_amult = [[], []]
            proj_free = [None, None]

            def emit_qk(h):
                for ts_ in range(4):
                    b = bgbank()
                    def qk_fn(e, ts_=ts_, b=b, h=h):
                        last = None
                        for cc in range(NC):
                            last = e.matmul(
                                ps[b][:, 0:512],
                                G[:, G_WQK + (h % 2) * 1024 + cc * 128:
                                   G_WQK + (h % 2) * 1024 + (cc + 1) * 128],
                                xnf[:, cc * T + ts_ * 512:cc * T + ts_ * 512 + 512],
                                start=(cc == 0), stop=(cc == NC - 1))
                        return last
                    t_mm = pb.emit("pe", qk_fn,
                                   waits=tmax(t_wqk[h], t_xnf[NT - 1], bank_free[b]), inc="pe")
                    if ts_ == 3:
                        wqk_reader[h % 2] = t_mm
                        if h + 2 < NH:
                            par2 = (h + 2) % 2
                            t_wqk[h + 2] = pb.emit("pool", lambda e, h2=h + 2, par2=par2: e.dma_start(
                                G[:, G_WQK + par2 * 1024:G_WQK + par2 * 1024 + 1024]
                                .rearrange("p (a d) -> p a d", a=NC),
                                wqkT_h[:, h2 * 128:(h2 + 1) * 128].rearrange("(a p) d -> p a d", p=128)),
                                waits=tmax(wqk_reader[par2]), inc=f"dq{par2}")
                    pb.emit("dve", lambda e, ts_=ts_, b=b: e.tensor_copy(
                        Qb[0:64, ts_ * 512:(ts_ + 1) * 512], ps[b][0:64, 0:512]),
                        waits=tmax(t_mm, t_s_last), inc="dve")
                    t_qk_evac[ts_] = pb.emit("dve", lambda e, ts_=ts_, b=b: e.tensor_copy(
                        Kb[0:64, ts_ * 512:(ts_ + 1) * 512], ps[b][64:128, 0:512]), inc="dve")
                    bank_free[b] = t_qk_evac[ts_]
                    fill(1)

            def emit_proj(p):
                par = p % 2
                t_mm = None
                for tt in range(NT):
                    for dh in range(2):
                        b = bgbank()
                        def pj_fn(e, tt=tt, dh=dh, b=b, par=par, p=p):
                            m = e.matmul(ps[b][:, 0:512],
                                         AP2[:, par * 2048 + tt * 128:par * 2048 + (tt + 1) * 128],
                                         WPS[:, par * 1024 + dh * 512:par * 1024 + dh * 512 + 512],
                                         start=True, stop=(p != 0))
                            if p == 0:
                                m = e.matmul(ps[b][:, 0:512], ones[0:1, :],
                                             bproj[0:1, dh * 512:dh * 512 + 512],
                                             start=False, stop=True)
                            return m
                        t_mm = pb.emit("pe", pj_fn,
                                       waits=tmax(t_amult[par], t_wps[par], t_ones,
                                                  t_bproj, bank_free[b]), inc="pe")
                        t_pe = pb.emit("dve", lambda e, tt=tt, dh=dh, b=b: e.tensor_tensor(
                            xacc[:, tt * C + dh * 512:tt * C + dh * 512 + 512],
                            ps[b][:, 0:512],
                            xacc[:, tt * C + dh * 512:tt * C + dh * 512 + 512],
                            op=ALU.add), waits=[t_mm], inc="dve")
                        bank_free[b] = t_pe
                proj_free[par] = t_mm
                t_amult[par] = []
                if p + 2 < 8:
                    t_wps[par] = pb.emit("pool", lambda e, p2=p + 2, par=par: e.dma_start(
                        WPS[:, par * 1024:(par + 1) * 1024],
                        wpT_h[p2 * 128:(p2 + 1) * 128, :]),
                        waits=[t_mm], inc=f"dwp{par}")

            for h in range(NH):
                if h > 0:
                    emit_qk(h)
                else:
                    pass
                it = 0
                t_pv_stop = [None] * NQS
                p_used = [None] * NPP
                t_exp_of = [None] * NPP
                pend_q = []
                def push_pending(fn):
                    pend_q.append(fn)
                    if len(pend_q) > 3:
                        pend_q.pop(0)()
                def flush_pending():
                    while pend_q:
                        pend_q.pop(0)()
                for qs in range(NQS):
                    for kt in range(4 * qs + 4):
                        b = SB[it % 2]
                        j = kt - 4 * qs
                        if j > 0:
                            off, masked = 128 * j, True
                        elif j == 0:
                            off, masked = 0, True
                        else:
                            off, masked = 0, False
                        N = 512 - off
                        pp = it % NPP
                        def s_fn(e, b=b, kt=kt, qs=qs, off=off, N=N, masked=masked):
                            m = e.matmul(ps[b][:, 0:N],
                                         Kb[0:65, kt * 128:(kt + 1) * 128],
                                         Qb[0:65, qs * 512 + off:qs * 512 + 512],
                                         start=True, stop=(not masked))
                            if masked:
                                m = e.matmul(ps[b][:, 0:128], ident[:, :], maskm[:, :],
                                             start=False, stop=True)
                            return m
                        t_s = pb.emit("pe", s_fn,
                            waits=tmax(bank_free[b], t_qk_evac[qs], t_qk_evac[kt // 4],
                                       t_kone, t_maskm, t_ident),
                            inc="pe")
                        t_s_last = t_s

                        def pv_back(kt=kt, h=h, pp=pp, off=off, N=N, qs=qs):
                            pvb = PVB[qs % 2]
                            assert t_vready[kt] is not None
                            w = tmax(t_exp_of[pp], t_vready[kt], t_vones)
                            if kt == 0:
                                w = tmax(w, bank_free[pvb])
                            t_pv = pb.emit("pe", lambda e, pvb=pvb, kt=kt, h=h, pp=pp, off=off, N=N: e.matmul(
                                ps[pvb][0:65, off:512],
                                G[:, G_V + kt * 1040 + h * 65:G_V + kt * 1040 + h * 65 + 65],
                                G[:, G_P + pp * 512:G_P + pp * 512 + N],
                                start=(kt == 0), stop=(kt == 4 * qs + 3)),
                                waits=w, inc="pe")
                            p_used[pp] = t_pv
                            if kt == 4 * qs + 3:
                                t_pv_stop[qs] = t_pv
                        push_pending(pv_back)
                        t_e = pb.emit("act", lambda e, b=b, N=N, pp=pp: e.activation(
                            G[:, G_P + pp * 512:G_P + pp * 512 + N],
                            ps[b][:, 0:N], AF.Exp, scale=0.125),
                            waits=tmax(t_s, p_used[pp]), inc="act", drain=True)
                        bank_free[b] = t_e
                        t_exp_of[pp] = t_e
                        fill(1)
                        it += 1
                    flush_pending()
                    pvb = PVB[qs % 2]
                    t_cp = pb.emit("dve", lambda e, pvb=pvb: e.tensor_copy(
                        rrowbf[0:1, 0:512], ps[pvb][64:65, 0:512]),
                        waits=tmax(t_pv_stop[qs], t_rb_used), inc="dve", drain=True)
                    fill(1)
                    rbb = bgbank()
                    t_bc = pb.emit("pe", lambda e, rbb=rbb: e.matmul(
                        ps[rbb][0:64, 0:512], ones[0:1, 0:64], rrowbf[0:1, 0:512],
                        start=True, stop=True),
                        waits=tmax(t_cp, t_ones, bank_free[rbb]), inc="pe")
                    t_rb_used = t_bc
                    def ri_fn(e, rbb=rbb):
                        with nc.allow_low_precision(reason="rowsum rinv is consumed as a bf16 elementwise operand"):
                            return e.reciprocal(rbsb[:, :], ps[rbb][0:64, 0:512])
                    t_rbe = pb.emit("dve", ri_fn, waits=[t_bc], inc="dve")
                    bank_free[rbb] = t_rbe
                    par = (h // 2) % 2
                    r0 = (h % 2) * 64
                    t_am = pb.emit("dve", lambda e, pvb=pvb, par=par, r0=r0, qs=qs: e.tensor_tensor(
                        AP2[r0:r0 + 64, par * 2048 + qs * 512:par * 2048 + qs * 512 + 512],
                        ps[pvb][0:64, 0:512], rbsb[:, :], op=ALU.mult),
                        waits=tmax(t_rbe, proj_free[par]), inc="dve")
                    bank_free[pvb] = t_am
                    t_amult[par].append(t_am)
                    fill(1)
                if h % 2 == 1 and h // 2 >= 1:
                    emit_proj(h // 2 - 1)
                fill(1)
            emit_proj(7)
            pe_attn_done = ("pe", pb.cnt["pe"])

            stage_stop(3, [
                ("d_xacc", [128, NT * C], F32, xacc[:, :], tmax(proj_free[1])),
            ])

            while bgq:
                bgq.pop(0)()

            t_gmm_last = None
            t_hid = {}
            for jc in range(NC):
                for ts in range(4):
                    b = bgbank()
                    def sg_fn(e, jc=jc, ts=ts, b=b):
                        last = None
                        for cc in range(NC):
                            last = e.matmul(
                                ps[b][:, 0:512],
                                W[:, 8192 + cc * C + jc * 128:8192 + cc * C + (jc + 1) * 128],
                                xnf[:, cc * T + ts * 512:cc * T + ts * 512 + 512],
                                start=(cc == 0), stop=(cc == NC - 1))
                        return last
                    t_gmm = pb.emit("pe", sg_fn,
                                    waits=tmax(t_shg[0], bank_free[b], pe_attn_done), inc="pe")
                    t_gmm_last = t_gmm
                    spar = _scr[0] % 2
                    _scr[0] += 1
                    scr = H_SCR + spar * 1024
                    t_th = pb.emit("act", lambda e, b=b, scr=scr: e.activation(
                        H[:, scr:scr + 512], ps[b][:, 0:512], AF.Tanh, scale=0.5),
                        waits=tmax(t_gmm, t_scr_rd[spar]), inc="act")
                    t_2s = pb.emit("dve", lambda e, b=b, scr=scr, jc=jc, ts=ts: e.scalar_tensor_tensor(
                        G[:, G_SHH + jc * 2048 + ts * 512:G_SHH + jc * 2048 + (ts + 1) * 512],
                        H[:, scr:scr + 512], 1.0, ps[b][:, 0:512],
                        op0=ALU.add, op1=ALU.mult),
                        waits=tmax(t_th, pe_attn_done), inc="dve")
                    bank_free[b] = t_2s
                    t_scr_rd[spar] = t_2s
                    t_hid[(jc, ts)] = t_2s
            t_shd[0] = pb.emit("pool", lambda e: e.dma_start(
                W[:, 8192:16384].rearrange("p (a v) -> p a v", a=NC),
                shdT_h[:, :].rearrange("(a p) v -> p a v", p=128)),
                waits=tmax(t_gmm_last, t_shg[0]), inc="ds3")
            t_hid_last = None
            for jc in range(NC):
                for ts in range(4):
                    b = bgbank()
                    def su_fn(e, jc=jc, ts=ts, b=b):
                        last = None
                        for cc in range(NC):
                            last = e.matmul(
                                ps[b][:, 0:512],
                                W[:, cc * C + jc * 128:cc * C + (jc + 1) * 128],
                                xnf[:, cc * T + ts * 512:cc * T + ts * 512 + 512],
                                start=(cc == 0), stop=(cc == NC - 1))
                        return last
                    t_umm = pb.emit("pe", su_fn,
                                    waits=tmax(t_shu[0], bank_free[b]), inc="pe")
                    t_hd = pb.emit("dve", lambda e, b=b, jc=jc, ts=ts: e.scalar_tensor_tensor(
                        G[:, G_SHH + jc * 2048 + ts * 512:G_SHH + jc * 2048 + (ts + 1) * 512],
                        G[:, G_SHH + jc * 2048 + ts * 512:G_SHH + jc * 2048 + (ts + 1) * 512],
                        0.5, ps[b][:, 0:512], op0=ALU.mult, op1=ALU.mult),
                        waits=tmax(t_umm, t_hid[(jc, ts)]), inc="dve")
                    bank_free[b] = t_hd
                    t_hid_last = t_hd
            t_out_ready = [None] * NT
            for tt in range(NT):
                for dh in range(2):
                    b = bgbank()
                    def sd_fn(e, tt=tt, dh=dh, b=b):
                        last = None
                        for jc in range(NC):
                            last = e.matmul(
                                ps[b][:, 0:512],
                                G[:, G_SHH + jc * 2048 + tt * 128:G_SHH + jc * 2048 + (tt + 1) * 128],
                                W[:, 8192 + jc * 1024 + dh * 512:8192 + jc * 1024 + dh * 512 + 512],
                                start=(jc == 0), stop=(jc == NC - 1))
                        return last
                    t_dmm = pb.emit("pe", sd_fn,
                                    waits=tmax(t_hid_last, t_shd[0], bank_free[b]), inc="pe")
                    t_de = pb.emit("dve", lambda e, tt=tt, dh=dh, b=b: e.tensor_tensor(
                        xacc[:, tt * C + dh * 512:tt * C + dh * 512 + 512],
                        ps[b][:, 0:512],
                        xacc[:, tt * C + dh * 512:tt * C + dh * 512 + 512],
                        op=ALU.add), waits=[t_dmm], inc="dve", drain=True)
                    bank_free[b] = t_de
                    t_out_ready[tt] = t_de

            for g in range(8):
                eng = "sp" if g % 2 == 0 else "pool"
                pb.emit(eng, lambda e, g=g: e.dma_start(
                    out_h[:, :].rearrange("(r p) n -> p r n", p=128)[:, 2 * g:2 * g + 2, :],
                    xacc[:, :].rearrange("p (r n) -> p r n", n=C)[:, 2 * g:2 * g + 2, :]),
                    waits=tmax(t_out_ready[2 * g + 1]), inc="dout")
            final_dout = pb.cnt["dout"]
            pb.emit("sp", lambda e: e.nop(), waits=[("dout", final_dout)])

        except _Stop:
            pass

        def body(eng_name):
            waited = {}
            def run(eng):
                for ws, fn, inc, drain in pb.lists[eng_name]:
                    for s, c in ws:
                        if waited.get(s, 0) < c:
                            eng.wait_ge(sems[s], c)
                            waited[s] = c
                    ins = fn(eng)
                    if drain:
                        ins = eng.drain()
                    if inc:
                        ins.then_inc(sems[inc], 16 if inc.startswith("d") else 1)
            return run

        block.sync(body("sp"))
        block.gpsimd(body("pool"))
        block.scalar(body("act"))
        block.vector(body("dve"))
        block.tensor(body("pe"))

    return nc


_cache = {}


def host_prep(L0_attn_x, L0_attn_w_qkv, L0_attn_w_proj, L0_attn_b_proj,
              L0_ffn_router_w, L0_ffn_router_bias, L0_ffn_up_proj, L0_ffn_gate_proj,
              L0_ffn_down_proj, L0_ffn_sh_gate, L0_ffn_sh_up, L0_ffn_sh_down):
    x = np.asarray(L0_attn_x, dtype=np.float32)
    wq = np.asarray(L0_attn_w_qkv, dtype=np.float32)
    wp = np.asarray(L0_attn_w_proj, dtype=np.float32)
    bp = np.asarray(L0_attn_b_proj, dtype=np.float32)
    rw = np.asarray(L0_ffn_router_w, dtype=np.float32)
    rb = np.asarray(L0_ffn_router_bias, dtype=np.float32)
    up = np.ascontiguousarray(np.asarray(L0_ffn_up_proj, dtype=np.float32))
    gate = np.ascontiguousarray(np.asarray(L0_ffn_gate_proj, dtype=np.float32))
    down = np.ascontiguousarray(np.asarray(L0_ffn_down_proj, dtype=np.float32))
    shg = np.asarray(L0_ffn_sh_gate, dtype=np.float32)
    shu = np.asarray(L0_ffn_sh_up, dtype=np.float32)
    shd = np.asarray(L0_ffn_sh_down, dtype=np.float32)

    wqkT = np.ascontiguousarray(np.transpose(wq[:, 0:128, :], (2, 0, 1)).reshape(C, NH * 128))
    wvT = np.ascontiguousarray(np.transpose(wq[:, 128:192, :], (2, 0, 1)).reshape(C, NH * HS))
    wpT = np.ascontiguousarray(wp.T)
    routerT = np.ascontiguousarray(rw.T)
    rbias_b = np.ascontiguousarray(np.broadcast_to(rb.reshape(1, NE), (128, NE)))
    bproj = np.ascontiguousarray(bp.reshape(1, C))
    shgT = np.ascontiguousarray(shg.T)
    shuT = np.ascontiguousarray(shu.T)
    shdT = np.ascontiguousarray(shd.T)
    ident = np.eye(128, dtype=ml_dtypes.bfloat16)
    maskneg = np.where(np.triu(np.ones((128, 128), np.float32)) > 0, 0.0, -1e9).astype(ml_dtypes.bfloat16)
    ones128 = np.ones((1, 128), dtype=ml_dtypes.bfloat16)
    iotam = np.ascontiguousarray(np.broadcast_to(
        (np.arange(NE, dtype=np.float32) - 16.0).reshape(1, NE), (128, NE)))

    shared = dict(wqkt=wqkT, wvt=wvT, wpt=wpT, bproj=bproj, routert=routerT,
                  rbias=rbias_b, up=up, gate=gate, down=down, shgt=shgT,
                  shut=shuT, shdt=shdT, ident=ident,
                  ones128=ones128, iotam=iotam, maskneg=maskneg)
    return x, shared


def kernel(**inputs):
    x, shared = host_prep(**inputs)
    if "nc" not in _cache:
        _cache["nc"] = build_nc()
    nc = _cache["nc"]
    in_maps = []
    for i in range(8):
        m = dict(shared)
        m["x"] = np.ascontiguousarray(x[i])
        in_maps.append(m)

    res = run_bass_kernel_spmd(nc, in_maps, core_ids=list(range(8)))
    out = np.stack([np.asarray(r["y"], dtype=np.float32) for r in res.results], axis=0)
    return out


# version 28
# speedup vs baseline: 1.1976x; 1.0084x over previous
import numpy as np
import ml_dtypes
import concourse.bass as bass
import concourse.mybir as mybir
from concourse.bass_utils import run_bass_kernel_spmd
from contextlib import ExitStack

F32 = mybir.dt.float32
BF16 = mybir.dt.bfloat16
AF = mybir.ActivationFunctionType
ALU = mybir.AluOpType
AX = mybir.AxisListType

T = 2048
C = 1024
NT = 16
NC = 8
NH = 16
HS = 64
NE = 16
ED = 512
NQS = 4

G_V = 0
G_P = 16640
G_WQK = 18176
G_COLS = 20224
G_SHH = 0

W_COLS = 16384

H_HBUF = 0
H_SCR = 4096
H_COLS = 6144

F_STAT = 0
F_EPS = 64
F_Z = 66
F_COMB = 322
F_T2 = 578
F_MSK = 706
F_IOTA = 770
F_COLS = 786

_STAGE = 9
_DUMPS = []


class _Stop(Exception):
    pass


ENGINES = ("sp", "pool", "act", "dve", "pe")
ENG_SEM = {"sp": None, "pool": None, "act": "act", "dve": "dve", "pe": "pe"}
SEM_NAMES = ("pe", "dve", "act", "dc", "dout",
             "dw", "dwp0", "dwp1", "dq0", "dq1", "ds0", "ds1", "ds2", "ds3") + tuple(
    f"dx{i}" for i in range(16))




class PB:

    def __init__(self):
        self.lists = {e: [] for e in ENGINES}
        self.cnt = {k: 0 for k in SEM_NAMES}

    def emit(self, eng, fn, waits=(), inc=None, drain=False):
        own = ENG_SEM[eng]
        ws = []
        for w in waits:
            if w is None:
                continue
            s, c = w
            if c <= 0:
                continue
            if s == own:
                continue
            assert c <= self.cnt[s], f"wait on future: {s} {c} > {self.cnt[s]}"
            ws.append((s, c))
        tok = None
        if inc:
            d = 16 if inc.startswith("d") else 1
            self.cnt[inc] += d
            tok = (inc, self.cnt[inc])
        self.lists[eng].append((ws, fn, inc, drain))
        return tok


def tmax(*toks):
    best = {}
    for t in toks:
        if t is None:
            continue
        if isinstance(t, list):
            for s, c in t:
                if best.get(s, -1) < c:
                    best[s] = c
            continue
        s, c = t
        if best.get(s, -1) < c:
            best[s] = c
    return [(s, c) for s, c in best.items()]


def build_nc():
    nc = bass.Bass("TRN2", target_bir_lowering=False)

    x_h = nc.dram_tensor("x", [T, C], F32, kind="ExternalInput")
    wqkT_h = nc.dram_tensor("wqkt", [C, NH * 128], F32, kind="ExternalInput")
    wvT_h = nc.dram_tensor("wvt", [C, NH * HS], F32, kind="ExternalInput")
    wpT_h = nc.dram_tensor("wpt", [C, C], F32, kind="ExternalInput")
    bproj_h = nc.dram_tensor("bproj", [1, C], F32, kind="ExternalInput")
    routerT_h = nc.dram_tensor("routert", [C, NE], F32, kind="ExternalInput")
    rbias_h = nc.dram_tensor("rbias", [128, NE], F32, kind="ExternalInput")
    up_h = nc.dram_tensor("up", [NE, C, ED], F32, kind="ExternalInput")
    gate_h = nc.dram_tensor("gate", [NE, C, ED], F32, kind="ExternalInput")
    down_h = nc.dram_tensor("down", [NE, ED, C], F32, kind="ExternalInput")
    shgT_h = nc.dram_tensor("shgt", [C, C], F32, kind="ExternalInput")
    shuT_h = nc.dram_tensor("shut", [C, C], F32, kind="ExternalInput")
    shdT_h = nc.dram_tensor("shdt", [C, C], F32, kind="ExternalInput")
    ident_h = nc.dram_tensor("ident", [128, 128], BF16, kind="ExternalInput")
    maskneg_h = nc.dram_tensor("maskneg", [128, 128], BF16, kind="ExternalInput")
    ones_h = nc.dram_tensor("ones128", [1, 128], BF16, kind="ExternalInput")
    iota_h = nc.dram_tensor("iotam", [128, NE], F32, kind="ExternalInput")
    out_h = nc.dram_tensor("y", [T, C], F32, kind="ExternalOutput")

    with ExitStack() as ctx:
        sb = lambda n, s, d: ctx.enter_context(nc.sbuf_tensor(n, s, d))
        xacc = sb("xacc", [128, NT * C], F32)
        xnf = sb("xnf", [128, NC * T], BF16)
        G = sb("G", [128, G_COLS], BF16)
        W = sb("W", [128, W_COLS], BF16)
        H = sb("H", [128, H_COLS], BF16)
        AP2 = sb("AP2", [128, 4096], BF16)
        WPS = sb("WPS", [128, 2048], BF16)
        LNS = AP2
        F = sb("F", [128, F_COLS], F32)
        Qb = sb("Qb", [65, T], BF16)
        Kb = sb("Kb", [65, T], BF16)
        ident = sb("identity", [128, 128], BF16)
        maskm = sb("maskneg_s", [128, 128], BF16)
        ones = sb("ones_r", [1, 128], BF16)
        bproj = sb("bproj_r", [1, C], BF16)
        routerT = sb("routerT", [128, NC * NE], BF16)
        rbias = sb("rbias_b", [128, NE], F32)
        rrowbf = sb("rrowbf", [1, 512], BF16)
        rbsb = sb("rbsb", [64, 512], BF16)

        ps = [ctx.enter_context(nc.psum_tensor(f"ps{i}", [128, 512], F32))
              for i in range(7)]
        pstr = ctx.enter_context(nc.psum_tensor("pstr", [128, 1024], BF16))

        sems = {k: ctx.enter_context(nc.semaphore(k)) for k in SEM_NAMES}
        block = ctx.enter_context(nc.Block())

        pb = PB()
        bank_free = {i: None for i in range(7)}
        bank_free["tr"] = None
        SB = [0, 1]
        PVB = [2, 3]
        _bg = [0]

        def bgbank():
            b = 4 + (_bg[0] % 3)
            _bg[0] += 1
            return b

        _DUMPS.clear()

        def stage_stop(k, dumps):
            if _STAGE != k:
                return
            for name, shape, dt, ap, waits in dumps:
                d = nc.dram_tensor(name, shape, dt, kind="ExternalOutput")
                pb.emit("sp", lambda e, d=d, ap=ap: e.dma_start(d[:, :], ap),
                        waits=waits, inc="dout")
                _DUMPS.append(name)
            fin = pb.cnt["dout"]
            pb.emit("sp", lambda e: e.nop(), waits=[("dout", fin)])
            raise _Stop()

        for tns, hsrc in ((ident, ident_h), (maskm, maskneg_h),
                          (ones, ones_h), (rbias, rbias_h)):
            pb.emit("sp", lambda e, t=tns, s_=hsrc: e.dma_start(t[:, :], s_[:, :]), inc="dc")
        pb.emit("sp", lambda e: e.dma_start(F[:, F_IOTA:F_IOTA + NE], iota_h[:, :]), inc="dc")
        pb.emit("pool", lambda e: e.dma_start(bproj[:, :], bproj_h[:, :]), inc="dw")
        t_consts = ("dc", pb.cnt["dc"])
        t_kone = pb.emit("dve", lambda e: e.memset(Kb[64:65, :], 1.0), inc="dve", drain=True)
        t_kone = pb.emit("dve", lambda e: e.memset(Qb[64:65, :], 0.0), inc="dve", drain=True)
        t_eps = pb.emit("dve", lambda e: e.memset(F[:, F_EPS:F_EPS + 1], 1e-5), inc="dve", drain=True)
        t_vones = pb.emit("dve", lambda e: e.memset(
            G[:, G_V:G_V + NT * 1040].rearrange("p (k h d) -> p k h d", h=NH, d=65)[:, :, :, 64:65],
            1.0), inc="dve", drain=True)
        t_ident = t_maskm = t_ones = t_rbias = t_iota = t_consts
        t_bproj = ("dw", pb.cnt["dw"])

        t_x = [None] * 16

        def emit_x(c):
            eng = "sp" if (c % 4) < 2 else "pool"
            t_x[c] = pb.emit(eng, lambda e, c=c: e.dma_start(
                xacc[:, :].rearrange("p (r n) -> p r n", n=C)[:, c:c + 1, :],
                x_h[:, :].rearrange("(r p) n -> p r n", p=128)[:, c:c + 1, :]),
                inc=f"dx{c}")

        t_wv = [None, None]

        def emit_wv(s):
            t_wv[s] = pb.emit("pool", lambda e, s=s: e.dma_start(
                W[:, s * 4096:(s + 1) * 4096].rearrange("p (a v) -> p a v", a=4),
                wvT_h[:, :].rearrange("(a p) v -> p a v", p=128)[:, 4 * s:4 * s + 4, :]),
                inc=f"ds{s}")

        t_wqk = [None] * NH

        def emit_wqk01(h):
            t_wqk[h] = pb.emit("pool", lambda e, h=h: e.dma_start(
                G[:, G_WQK + (h % 2) * 1024:G_WQK + (h % 2) * 1024 + 1024]
                .rearrange("p (a d) -> p a d", a=NC),
                wqkT_h[:, h * 128:(h + 1) * 128].rearrange("(a p) d -> p a d", p=128)),
                inc=f"dq{h % 2}")

        t_wps = [None, None]

        def emit_wps0(p):
            t_wps[p] = pb.emit("pool", lambda e, p=p: e.dma_start(
                WPS[:, p * 1024:(p + 1) * 1024],
                wpT_h[p * 128:(p + 1) * 128, :]), inc=f"dwp{p % 2}")

        t_wdma = {}
        slot_free = [None] * 4

        def emit_wdma(e_i, m):
            src = (up_h, gate_h, down_h)[m]
            s = (3 * e_i + m + 2) % 4
            inap = src[e_i, :, :].rearrange("(a p) j -> p a j", p=128)
            outap = W[:, s * 4096:(s + 1) * 4096].rearrange(
                "p (a j) -> p a j", a=NC if m < 2 else 4)
            t_wdma[(e_i, m)] = pb.emit("pool", lambda e, inap=inap, outap=outap: e.dma_start(
                outap, inap), waits=tmax(slot_free[s]), inc=f"ds{s}")

        for c in (0, 1, 4, 5, 8, 9, 12, 13):
            emit_x(c)
        emit_x(2); emit_x(3)
        emit_wv(0); emit_wv(1)
        emit_x(6); emit_x(7)
        emit_wqk01(0)
        pb.emit("pool", lambda e: e.dma_start(
            routerT[:, :].rearrange("p (a v) -> p a v", a=NC),
            routerT_h[:, :].rearrange("(a p) v -> p a v", p=128)), inc="dw")
        t_routerT = ("dw", pb.cnt["dw"])
        t_bproj = t_routerT
        emit_x(10); emit_x(11)
        emit_wps0(0); emit_wps0(1)
        emit_wdma(0, 0)
        emit_x(14); emit_x(15)
        emit_wdma(0, 1)
        emit_wqk01(1)

        bgq = []

        def fill(n):
            for _ in range(n):
                if not bgq:
                    return
                bgq.pop(0)()

        t_xnf = [None] * NT
        t_vready = [None] * NT
        t_comb = [None] * NT
        t_vlast = [None]

        def v_item(tt):
            def go():
                toks = []
                for half in range(2):
                    b = bgbank()
                    def v_fn(e, tt=tt, half=half, b=b):
                        last = None
                        for cc in range(NC):
                            last = e.matmul(
                                ps[b][:, 0:512],
                                xnf[:, cc * T + tt * 128:cc * T + (tt + 1) * 128],
                                W[:, cc * C + half * 512:cc * C + half * 512 + 512],
                                start=(cc == 0), stop=(cc == NC - 1))
                        return last
                    t_mm = pb.emit("pe", v_fn,
                                   waits=tmax(t_xnf[tt], t_wv[0], t_wv[1], bank_free[b]),
                                   inc="pe")
                    t_ev = pb.emit("dve", lambda e, tt=tt, half=half, b=b: e.tensor_copy(
                        G[:, G_V + tt * 1040:G_V + (tt + 1) * 1040]
                        .rearrange("p (h d) -> p h d", d=65)[:, half * 8:half * 8 + 8, 0:64],
                        ps[b][:, 0:512].rearrange("p (h d) -> p h d", d=64)),
                        waits=[t_mm], inc="dve")
                    bank_free[b] = t_ev
                    toks.append(t_ev)
                    t_vlast[0] = t_mm
                t_vready[tt] = toks[-1]
                if tt == NT - 1:
                    slot_free[0] = t_vlast[0]
                    slot_free[1] = t_vlast[0]
                    emit_wdma(0, 2)
                    emit_wdma(1, 0)
            return go

        def router_item(tt):
            z = F[:, F_Z + tt * NE:F_Z + (tt + 1) * NE]

            def go():
                b = bgbank()
                def r_fn(e, tt=tt, b=b):
                    last = None
                    for cc in range(NC):
                        last = e.matmul(
                            ps[b][:, 0:NE],
                            xnf[:, cc * T + tt * 128:cc * T + (tt + 1) * 128],
                            routerT[:, cc * NE:(cc + 1) * NE],
                            start=(cc == 0), stop=(cc == NC - 1))
                    return last
                t_rmm = pb.emit("pe", r_fn,
                                waits=tmax(t_xnf[tt], t_routerT, bank_free[b]), inc="pe")
                t_z = pb.emit("dve", lambda e, z=z, b=b: e.tensor_tensor(
                    z, ps[b][:, 0:NE], rbias[:, :], op=ALU.add),
                    waits=tmax(t_rmm, t_rbias, t_iota), inc="dve", drain=True)
                bank_free[b] = t_z
            return go

        def emit_router_top2():
            z3 = F[:, F_Z:F_Z + 256].rearrange("p (t e) -> p t e", e=NE)
            cb3 = F[:, F_COMB:F_COMB + 256].rearrange("p (t e) -> p t e", e=NE)
            m1 = F[:, F_T2 + 0:F_T2 + 16]
            m2 = F[:, F_T2 + 16:F_T2 + 32]
            e2 = F[:, F_T2 + 32:F_T2 + 48]
            s1v = F[:, F_T2 + 48:F_T2 + 64]
            s2v = F[:, F_T2 + 64:F_T2 + 80]
            den = F[:, F_T2 + 80:F_T2 + 96]
            i1 = F[:, F_T2 + 96:F_T2 + 112]
            i2 = F[:, F_T2 + 112:F_T2 + 128]
            msk1 = AP2[:, 0:256].rearrange("p (t e) -> p t e", e=NE)
            msk2 = AP2[:, 256:512].rearrange("p (t e) -> p t e", e=NE)
            tmpb = AP2[:, 512:768].rearrange("p (t e) -> p t e", e=NE)
            iotb = F[:, F_IOTA:F_IOTA + NE].rearrange("p e -> p () e").broadcast_to([128, 16, NE])
            bt = lambda a: a.rearrange("p t -> p t ()").broadcast_to([128, 16, NE])
            pb.emit("dve", lambda e: e.tensor_reduce(m1, z3, axis=AX.X, op=ALU.max),
                    inc="dve", drain=True)
            pb.emit("dve", lambda e: e.tensor_tensor(msk1, z3, bt(m1), op=ALU.is_equal),
                    waits=tmax(t_tr[NT - 2]), inc="dve", drain=True)
            pb.emit("dve", lambda e: e.tensor_tensor(tmpb, msk1, iotb, op=ALU.mult),
                    inc="dve", drain=True)
            pb.emit("dve", lambda e: e.tensor_reduce(i1, tmpb, axis=AX.X, op=ALU.min),
                    inc="dve", drain=True)
            pb.emit("dve", lambda e: e.tensor_tensor(msk1, iotb, bt(i1), op=ALU.is_equal),
                    inc="dve", drain=True)
            pb.emit("dve", lambda e: e.scalar_tensor_tensor(
                z3, msk1, -1e30, z3, op0=ALU.mult, op1=ALU.add), inc="dve", drain=True)
            pb.emit("dve", lambda e: e.tensor_reduce(m2, z3, axis=AX.X, op=ALU.max),
                    inc="dve", drain=True)
            pb.emit("dve", lambda e: e.tensor_tensor(msk2, z3, bt(m2), op=ALU.is_equal),
                    inc="dve", drain=True)
            pb.emit("dve", lambda e: e.tensor_tensor(tmpb, msk2, iotb, op=ALU.mult),
                    inc="dve", drain=True)
            pb.emit("dve", lambda e: e.tensor_reduce(i2, tmpb, axis=AX.X, op=ALU.min),
                    inc="dve", drain=True)
            pb.emit("dve", lambda e: e.tensor_tensor(msk2, iotb, bt(i2), op=ALU.is_equal),
                    inc="dve", drain=True)
            t_sub = pb.emit("dve", lambda e: e.tensor_tensor(e2, m2, m1, op=ALU.subtract),
                            inc="dve", drain=True)
            t_exp = pb.emit("act", lambda e: e.activation(e2, e2, AF.Exp),
                            waits=[t_sub], inc="act", drain=True)
            pb.emit("dve", lambda e: e.tensor_scalar(
                den, e2, 1.0, None, op0=ALU.add), waits=[t_exp], inc="dve", drain=True)
            pb.emit("dve", lambda e: e.reciprocal(s1v, den), inc="dve", drain=True)
            pb.emit("dve", lambda e: e.tensor_scalar(
                s2v, s1v, -1.0, 1.0, op0=ALU.mult, op1=ALU.add), inc="dve", drain=True)
            pb.emit("dve", lambda e: e.tensor_tensor(cb3, msk1, bt(s1v), op=ALU.mult),
                    inc="dve", drain=True)
            pb.emit("dve", lambda e: e.tensor_tensor(z3, msk2, bt(s2v), op=ALU.mult),
                    inc="dve", drain=True)
            tok = pb.emit("dve", lambda e: e.tensor_tensor(cb3, cb3, z3, op=ALU.add),
                          inc="dve", drain=True)
            for tt in range(NT):
                t_comb[tt] = tok
            return tok

        h_done = {}
        t_hbuf_rd = [None, None]
        t_scr_rd = [None, None]
        _scr = [0]

        ustate = {}

        def u_item(e_i, ts, jc):
            def go():
                su = (3 * e_i + 0 + 2) % 4
                bu = bgbank()
                par = ts % 2
                def u_fn(e, jc=jc, ts=ts, bu=bu, base=su * 4096):
                    last = None
                    for cc in range(NC):
                        last = e.matmul(
                            ps[bu][:, 0:512],
                            W[:, base + cc * 512 + jc * 128:base + cc * 512 + (jc + 1) * 128],
                            xnf[:, cc * T + ts * 512:cc * T + ts * 512 + 512],
                            start=(cc == 0), stop=(cc == NC - 1))
                    return last
                t_umm = pb.emit("pe", u_fn,
                                waits=tmax(t_wdma[(e_i, 0)], bank_free[bu]), inc="pe")
                t_ue = pb.emit("act", lambda e, bu=bu, par=par, jc=jc: e.activation(
                    H[:, H_HBUF + par * 2048 + jc * 512:H_HBUF + par * 2048 + (jc + 1) * 512],
                    ps[bu][:, 0:512], AF.Identity),
                    waits=tmax(t_umm, t_hbuf_rd[par]), inc="act")
                bank_free[bu] = t_ue
                ustate[(e_i, ts, jc)] = t_ue
                if ts == 3 and jc == 3:
                    slot_free[su] = t_umm
                    if e_i + 1 == NE:
                        t_shg[0] = pb.emit("pool", lambda e: e.dma_start(
                            W[:, 8192:16384].rearrange("p (a v) -> p a v", a=NC),
                            shgT_h[:, :].rearrange("(a p) v -> p a v", p=128)),
                            waits=tmax(slot_free[2], slot_free[3]), inc="ds2")
            return go

        def g_item(e_i, ts, jc):
            def go():
                sg = (3 * e_i + 1 + 2) % 4
                bg2 = bgbank()
                par = ts % 2
                spar = _scr[0] % 2
                _scr[0] += 1
                t_ue = ustate.pop((e_i, ts, jc))
                def g_fn(e, jc=jc, ts=ts, bg2=bg2, base=sg * 4096):
                    last = None
                    for cc in range(NC):
                        last = e.matmul(
                            ps[bg2][:, 0:512],
                            W[:, base + cc * 512 + jc * 128:base + cc * 512 + (jc + 1) * 128],
                            xnf[:, cc * T + ts * 512:cc * T + ts * 512 + 512],
                            start=(cc == 0), stop=(cc == NC - 1))
                    return last
                t_gmm = pb.emit("pe", g_fn,
                                waits=tmax(t_wdma[(e_i, 1)], bank_free[bg2]), inc="pe")
                scr = H_SCR + spar * 1024
                t_th = pb.emit("act", lambda e, bg2=bg2, scr=scr: e.activation(
                    H[:, scr:scr + 512], ps[bg2][:, 0:512], AF.Tanh, scale=0.5),
                    waits=tmax(t_gmm, t_scr_rd[spar]), inc="act")
                t_2s = pb.emit("dve", lambda e, bg2=bg2, scr=scr: e.scalar_tensor_tensor(
                    H[:, scr + 512:scr + 1024], H[:, scr:scr + 512], 1.0,
                    ps[bg2][:, 0:512], op0=ALU.add, op1=ALU.mult),
                    waits=tmax(t_th), inc="dve")
                bank_free[bg2] = t_2s
                t_h = pb.emit("dve", lambda e, par=par, jc=jc, scr=scr: e.scalar_tensor_tensor(
                    H[:, H_HBUF + par * 2048 + jc * 512:H_HBUF + par * 2048 + (jc + 1) * 512],
                    H[:, scr + 512:scr + 1024], 0.5,
                    H[:, H_HBUF + par * 2048 + jc * 512:H_HBUF + par * 2048 + (jc + 1) * 512],
                    op0=ALU.mult, op1=ALU.mult),
                    waits=tmax(t_ue), inc="dve")
                t_scr_rd[spar] = t_h
                if jc == 3:
                    h_done[(e_i, ts)] = t_h
                if ts == 3 and jc == 3:
                    slot_free[sg] = t_gmm
                    if e_i + 1 < NE:
                        emit_wdma(e_i + 1, 1)
            return go

        def down_item(e_i, ts, tti, dh):
            def go():
                sd = (3 * e_i + 2 + 2) % 4
                par = ts % 2
                tt = ts * 4 + tti
                assert t_comb[tt] is not None and h_done.get((e_i, ts)) is not None
                cbcol = F[:, F_COMB + tt * NE + e_i:F_COMB + tt * NE + e_i + 1]
                b = bgbank()
                def d_fn(e, par=par, tti=tti, dh=dh, b=b, base=sd * 4096):
                    last = None
                    for jc in range(4):
                        last = e.matmul(
                            ps[b][:, 0:512],
                            H[:, H_HBUF + par * 2048 + jc * 512 + tti * 128:
                               H_HBUF + par * 2048 + jc * 512 + (tti + 1) * 128],
                            W[:, base + jc * 1024 + dh * 512:base + jc * 1024 + dh * 512 + 512],
                            start=(jc == 0), stop=(jc == 3))
                    return last
                t_dmm = pb.emit("pe", d_fn,
                                waits=tmax(h_done[(e_i, ts)], t_wdma[(e_i, 2)],
                                           bank_free[b]), inc="pe")
                t_hbuf_rd[par] = t_dmm
                t_de = pb.emit("dve", lambda e, tt=tt, dh=dh, b=b, cbcol=cbcol: e.scalar_tensor_tensor(
                    xacc[:, tt * C + dh * 512:tt * C + dh * 512 + 512],
                    ps[b][:, 0:512], cbcol,
                    xacc[:, tt * C + dh * 512:tt * C + dh * 512 + 512],
                    op0=ALU.mult, op1=ALU.add),
                    waits=tmax(t_dmm, t_comb[tt]), inc="dve")
                bank_free[b] = t_de
                if ts == 3 and tti == 3 and dh == 1:
                    slot_free[sd] = ("pe", pb.cnt["pe"])
                    if e_i + 1 < NE:
                        emit_wdma(e_i + 1, 2)
                    if e_i + 2 < NE:
                        emit_wdma(e_i + 2, 0)
                    if e_i + 1 == NE:
                        t_shu[0] = pb.emit("pool", lambda e: e.dma_start(
                            W[:, 0:8192].rearrange("p (a v) -> p a v", a=NC),
                            shuT_h[:, :].rearrange("(a p) v -> p a v", p=128)),
                            waits=tmax(slot_free[0], slot_free[1]), inc="ds0")
            return go

        t_shg = [None]
        t_shu = [None]
        t_shd = [None]

        try:
            t_tr = [None] * NT
            t_qk_evac = {}
            wqk_reader = [None, None]
            t_nmrl = [None] * NT

            def emit_stats(tt):
                xs = xacc[:, tt * C:(tt + 1) * C]
                st = F_STAT + tt * 4
                par = tt % 2
                lns = LNS[:, par * 1024:(par + 1) * 1024]
                pb.emit("dve", lambda e, xs=xs, st=st: e.reduce_sum(
                    F[:, st:st + 1], xs, axis=AX.X), waits=[t_x[tt]], inc="dve", drain=True)
                t_ssq = pb.emit("act", lambda e, xs=xs, st=st, lns=lns: e.activation(
                    lns, xs, AF.Square, accum_out=F[:, st + 3:st + 4]),
                    waits=tmax(t_x[tt], t_tr[tt - 2] if tt >= 2 else None),
                    inc="act", drain=True)
                pb.emit("dve", lambda e, st=st: e.tensor_scalar(
                    F[:, st:st + 1], F[:, st:st + 1], 1.0 / C, None, op0=ALU.mult), inc="dve", drain=True)
                pb.emit("dve", lambda e, st=st: e.tensor_scalar(
                    F[:, st + 1:st + 2], F[:, st + 3:st + 4], 1.0 / C, None, op0=ALU.mult),
                    waits=[t_ssq], inc="dve", drain=True)
                pb.emit("dve", lambda e, st=st: e.tensor_tensor(
                    F[:, st + 2:st + 3], F[:, st:st + 1], F[:, st:st + 1], op=ALU.mult), inc="dve", drain=True)
                t_var = pb.emit("dve", lambda e, st=st: e.tensor_tensor(
                    F[:, st + 1:st + 2], F[:, st + 1:st + 2], F[:, st + 2:st + 3],
                    op=ALU.subtract), inc="dve", drain=True)
                pb.emit("act", lambda e, st=st: e.activation(
                    F[:, st + 1:st + 2], F[:, st + 1:st + 2], AF.Ln,
                    bias=F[:, F_EPS:F_EPS + 1], scale=1.0),
                    waits=tmax(t_eps, t_var), inc="act", drain=True)
                t_rstd = pb.emit("act", lambda e, st=st: e.activation(
                    F[:, st + 1:st + 2], F[:, st + 1:st + 2], AF.Exp, scale=-0.5),
                    inc="act", drain=True)
                t_nmrl[tt] = pb.emit("dve", lambda e, st=st: e.tensor_scalar(
                    F[:, st + 2:st + 3], F[:, st:st + 1], F[:, st + 1:st + 2], -1.0,
                    op0=ALU.mult, op1=ALU.mult), waits=[t_rstd], inc="dve", drain=True)

            emit_stats(0)
            for tt in range(NT):
                xs = xacc[:, tt * C:(tt + 1) * C]
                st = F_STAT + tt * 4
                par = tt % 2
                lns = LNS[:, par * 1024:(par + 1) * 1024]
                t_apply = pb.emit("act", lambda e, xs=xs, st=st, lns=lns: e.activation(
                    lns, xs, AF.Identity,
                    bias=F[:, st + 2:st + 3], scale=F[:, st + 1:st + 2]),
                    waits=[t_nmrl[tt]], inc="act", drain=True)
                def tr_fn(e, lns=lns):
                    last = None
                    for cc in range(NC):
                        last = e.transpose(pstr[:, cc * 128:(cc + 1) * 128],
                                           lns[:, cc * 128:(cc + 1) * 128],
                                           ident[:, :])
                    return last
                t_tr[tt] = pb.emit("pe", tr_fn,
                                   waits=tmax(t_apply, t_ident, bank_free["tr"]), inc="pe")
                t_xnf[tt] = pb.emit("dve", lambda e, tt=tt: e.tensor_copy(
                    xnf[:, :].rearrange("p (a t) -> p a t", a=NC)[:, :, tt * 128:(tt + 1) * 128],
                    pstr[:, :].rearrange("p (a d) -> p a d", a=NC)),
                    waits=[t_tr[tt]], inc="dve")
                bank_free["tr"] = t_xnf[tt]
                if tt + 1 < NT:
                    emit_stats(tt + 1)
                bgq.append(v_item(tt))
                bgq.append(router_item(tt))
                if tt % 4 == 3:
                    ts_ = tt // 4
                    b = bgbank()
                    def qk_fn(e, ts_=ts_, b=b):
                        last = None
                        for cc in range(NC):
                            last = e.matmul(
                                ps[b][:, 0:512],
                                G[:, G_WQK + cc * 128:G_WQK + (cc + 1) * 128],
                                xnf[:, cc * T + ts_ * 512:cc * T + ts_ * 512 + 512],
                                start=(cc == 0), stop=(cc == NC - 1))
                        return last
                    t_mm = pb.emit("pe", qk_fn,
                                   waits=tmax(t_wqk[0], t_xnf[tt], bank_free[b]), inc="pe")
                    if ts_ == 3:
                        wqk_reader[0] = t_mm
                        t_wqk[2] = pb.emit("pool", lambda e: e.dma_start(
                            G[:, G_WQK:G_WQK + 1024].rearrange("p (a d) -> p a d", a=NC),
                            wqkT_h[:, 2 * 128:3 * 128].rearrange("(a p) d -> p a d", p=128)),
                            waits=tmax(wqk_reader[0]), inc="dq0")
                    pb.emit("dve", lambda e, ts_=ts_, b=b: e.tensor_copy(
                        Qb[0:64, ts_ * 512:(ts_ + 1) * 512], ps[b][0:64, 0:512]),
                        waits=[t_mm], inc="dve")
                    t_qk_evac[ts_] = pb.emit("dve", lambda e, ts_=ts_, b=b: e.tensor_copy(
                        Kb[0:64, ts_ * 512:(ts_ + 1) * 512], ps[b][64:128, 0:512]),
                        inc="dve")
                    bank_free[b] = t_qk_evac[ts_]
                if tt >= 2:
                    fill(3)

            stage_stop(1, [
                ("d_xnf", [128, NC * T], BF16, xnf[:, :], tmax(t_xnf[NT - 1])),
            ])

            while bgq:
                bgq.pop(0)()
            pend = None
            for e_i in range(NE):
                for ts in range(4):
                    for jc in range(4):
                        bgq.append(u_item(e_i, ts, jc))
                        bgq.append(g_item(e_i, ts, jc))
                    if pend is not None:
                        pe_, pts = pend
                        for tti in range(4):
                            for dh in range(2):
                                bgq.append(down_item(pe_, pts, tti, dh))
                    pend = (e_i, ts)
            for tti in range(4):
                for dh in range(2):
                    bgq.append(down_item(NE - 1, 3, tti, dh))

            t_gpfree = emit_router_top2()

            NPP = 3
            t_rb_used = None
            t_s_last = None
            t_amult = [[], []]
            proj_free = [None, None]

            def emit_qk(h):
                for ts_ in range(4):
                    b = bgbank()
                    def qk_fn(e, ts_=ts_, b=b, h=h):
                        last = None
                        for cc in range(NC):
                            last = e.matmul(
                                ps[b][:, 0:512],
                                G[:, G_WQK + (h % 2) * 1024 + cc * 128:
                                   G_WQK + (h % 2) * 1024 + (cc + 1) * 128],
                                xnf[:, cc * T + ts_ * 512:cc * T + ts_ * 512 + 512],
                                start=(cc == 0), stop=(cc == NC - 1))
                        return last
                    t_mm = pb.emit("pe", qk_fn,
                                   waits=tmax(t_wqk[h], t_xnf[NT - 1], bank_free[b]), inc="pe")
                    if ts_ == 3:
                        wqk_reader[h % 2] = t_mm
                        if h + 2 < NH:
                            par2 = (h + 2) % 2
                            t_wqk[h + 2] = pb.emit("pool", lambda e, h2=h + 2, par2=par2: e.dma_start(
                                G[:, G_WQK + par2 * 1024:G_WQK + par2 * 1024 + 1024]
                                .rearrange("p (a d) -> p a d", a=NC),
                                wqkT_h[:, h2 * 128:(h2 + 1) * 128].rearrange("(a p) d -> p a d", p=128)),
                                waits=tmax(wqk_reader[par2]), inc=f"dq{par2}")
                    pb.emit("dve", lambda e, ts_=ts_, b=b: e.tensor_copy(
                        Qb[0:64, ts_ * 512:(ts_ + 1) * 512], ps[b][0:64, 0:512]),
                        waits=tmax(t_mm, t_s_last), inc="dve")
                    t_qk_evac[ts_] = pb.emit("dve", lambda e, ts_=ts_, b=b: e.tensor_copy(
                        Kb[0:64, ts_ * 512:(ts_ + 1) * 512], ps[b][64:128, 0:512]), inc="dve")
                    bank_free[b] = t_qk_evac[ts_]
                    fill(1)

            def emit_proj(p):
                par = p % 2
                t_mm = None
                for tt in range(NT):
                    for dh in range(2):
                        b = bgbank()
                        def pj_fn(e, tt=tt, dh=dh, b=b, par=par, p=p):
                            m = e.matmul(ps[b][:, 0:512],
                                         AP2[:, par * 2048 + tt * 128:par * 2048 + (tt + 1) * 128],
                                         WPS[:, par * 1024 + dh * 512:par * 1024 + dh * 512 + 512],
                                         start=True, stop=(p != 0))
                            if p == 0:
                                m = e.matmul(ps[b][:, 0:512], ones[0:1, :],
                                             bproj[0:1, dh * 512:dh * 512 + 512],
                                             start=False, stop=True)
                            return m
                        t_mm = pb.emit("pe", pj_fn,
                                       waits=tmax(t_amult[par], t_wps[par], t_ones,
                                                  t_bproj, bank_free[b]), inc="pe")
                        t_pe = pb.emit("dve", lambda e, tt=tt, dh=dh, b=b: e.tensor_tensor(
                            xacc[:, tt * C + dh * 512:tt * C + dh * 512 + 512],
                            ps[b][:, 0:512],
                            xacc[:, tt * C + dh * 512:tt * C + dh * 512 + 512],
                            op=ALU.add), waits=[t_mm], inc="dve")
                        bank_free[b] = t_pe
                proj_free[par] = t_mm
                t_amult[par] = []
                if p + 2 < 8:
                    t_wps[par] = pb.emit("pool", lambda e, p2=p + 2, par=par: e.dma_start(
                        WPS[:, par * 1024:(par + 1) * 1024],
                        wpT_h[p2 * 128:(p2 + 1) * 128, :]),
                        waits=[t_mm], inc=f"dwp{par}")

            for h in range(NH):
                if h > 0:
                    emit_qk(h)
                else:
                    pass
                it = 0
                t_pv_stop = [None] * NQS
                p_used = [None] * NPP
                t_exp_of = [None] * NPP
                pend_q = []
                def push_pending(fn):
                    pend_q.append(fn)
                    if len(pend_q) > 3:
                        pend_q.pop(0)()
                def flush_pending():
                    while pend_q:
                        pend_q.pop(0)()
                for qs in range(NQS):
                    for kt in range(4 * qs + 4):
                        b = SB[it % 2]
                        j = kt - 4 * qs
                        if j > 0:
                            off, masked = 128 * j, True
                        elif j == 0:
                            off, masked = 0, True
                        else:
                            off, masked = 0, False
                        N = 512 - off
                        pp = it % NPP
                        def s_fn(e, b=b, kt=kt, qs=qs, off=off, N=N, masked=masked):
                            m = e.matmul(ps[b][:, 0:N],
                                         Kb[0:65, kt * 128:(kt + 1) * 128],
                                         Qb[0:65, qs * 512 + off:qs * 512 + 512],
                                         start=True, stop=(not masked))
                            if masked:
                                m = e.matmul(ps[b][:, 0:128], ident[:, :], maskm[:, :],
                                             start=False, stop=True)
                            return m
                        t_s = pb.emit("pe", s_fn,
                            waits=tmax(bank_free[b], t_qk_evac[qs], t_qk_evac[kt // 4],
                                       t_kone, t_maskm, t_ident),
                            inc="pe")
                        t_s_last = t_s

                        def pv_back(kt=kt, h=h, pp=pp, off=off, N=N, qs=qs):
                            pvb = PVB[qs % 2]
                            assert t_vready[kt] is not None
                            w = tmax(t_exp_of[pp], t_vready[kt], t_vones)
                            if kt == 0:
                                w = tmax(w, bank_free[pvb])
                            t_pv = pb.emit("pe", lambda e, pvb=pvb, kt=kt, h=h, pp=pp, off=off, N=N: e.matmul(
                                ps[pvb][0:65, off:512],
                                G[:, G_V + kt * 1040 + h * 65:G_V + kt * 1040 + h * 65 + 65],
                                G[:, G_P + pp * 512:G_P + pp * 512 + N],
                                start=(kt == 0), stop=(kt == 4 * qs + 3)),
                                waits=w, inc="pe")
                            p_used[pp] = t_pv
                            if kt == 4 * qs + 3:
                                t_pv_stop[qs] = t_pv
                        push_pending(pv_back)
                        t_e = pb.emit("act", lambda e, b=b, N=N, pp=pp: e.activation(
                            G[:, G_P + pp * 512:G_P + pp * 512 + N],
                            ps[b][:, 0:N], AF.Exp, scale=0.125),
                            waits=tmax(t_s, p_used[pp]), inc="act", drain=True)
                        bank_free[b] = t_e
                        t_exp_of[pp] = t_e
                        fill(1)
                        it += 1
                    flush_pending()
                    pvb = PVB[qs % 2]
                    t_cp = pb.emit("dve", lambda e, pvb=pvb: e.tensor_copy(
                        rrowbf[0:1, 0:512], ps[pvb][64:65, 0:512]),
                        waits=tmax(t_pv_stop[qs], t_rb_used), inc="dve", drain=True)
                    fill(1)
                    rbb = bgbank()
                    t_bc = pb.emit("pe", lambda e, rbb=rbb: e.matmul(
                        ps[rbb][0:64, 0:512], ones[0:1, 0:64], rrowbf[0:1, 0:512],
                        start=True, stop=True),
                        waits=tmax(t_cp, t_ones, bank_free[rbb]), inc="pe")
                    t_rb_used = t_bc
                    def ri_fn(e, rbb=rbb):
                        with nc.allow_low_precision(reason="rowsum rinv is consumed as a bf16 elementwise operand"):
                            return e.reciprocal(rbsb[:, :], ps[rbb][0:64, 0:512])
                    t_rbe = pb.emit("dve", ri_fn, waits=[t_bc], inc="dve")
                    bank_free[rbb] = t_rbe
                    par = (h // 2) % 2
                    r0 = (h % 2) * 64
                    t_am = pb.emit("dve", lambda e, pvb=pvb, par=par, r0=r0, qs=qs: e.tensor_tensor(
                        AP2[r0:r0 + 64, par * 2048 + qs * 512:par * 2048 + qs * 512 + 512],
                        ps[pvb][0:64, 0:512], rbsb[:, :], op=ALU.mult),
                        waits=tmax(t_rbe, proj_free[par]), inc="dve")
                    bank_free[pvb] = t_am
                    t_amult[par].append(t_am)
                    fill(1)
                if h % 2 == 1 and h // 2 >= 1:
                    emit_proj(h // 2 - 1)
                fill(1)
            emit_proj(7)
            pe_attn_done = ("pe", pb.cnt["pe"])

            stage_stop(3, [
                ("d_xacc", [128, NT * C], F32, xacc[:, :], tmax(proj_free[1])),
            ])

            while bgq:
                bgq.pop(0)()

            t_gmm_last = None
            t_hid = {}
            for jc in range(NC):
                for ts in range(4):
                    b = bgbank()
                    def sg_fn(e, jc=jc, ts=ts, b=b):
                        last = None
                        for cc in range(NC):
                            last = e.matmul(
                                ps[b][:, 0:512],
                                W[:, 8192 + cc * C + jc * 128:8192 + cc * C + (jc + 1) * 128],
                                xnf[:, cc * T + ts * 512:cc * T + ts * 512 + 512],
                                start=(cc == 0), stop=(cc == NC - 1))
                        return last
                    t_gmm = pb.emit("pe", sg_fn,
                                    waits=tmax(t_shg[0], bank_free[b], pe_attn_done), inc="pe")
                    t_gmm_last = t_gmm
                    spar = _scr[0] % 2
                    _scr[0] += 1
                    scr = H_SCR + spar * 1024
                    t_th = pb.emit("act", lambda e, b=b, scr=scr: e.activation(
                        H[:, scr:scr + 512], ps[b][:, 0:512], AF.Tanh, scale=0.5),
                        waits=tmax(t_gmm, t_scr_rd[spar]), inc="act")
                    t_2s = pb.emit("dve", lambda e, b=b, scr=scr, jc=jc, ts=ts: e.scalar_tensor_tensor(
                        G[:, G_SHH + jc * 2048 + ts * 512:G_SHH + jc * 2048 + (ts + 1) * 512],
                        H[:, scr:scr + 512], 1.0, ps[b][:, 0:512],
                        op0=ALU.add, op1=ALU.mult),
                        waits=tmax(t_th, pe_attn_done), inc="dve")
                    bank_free[b] = t_2s
                    t_scr_rd[spar] = t_2s
                    t_hid[(jc, ts)] = t_2s
            t_shd[0] = pb.emit("pool", lambda e: e.dma_start(
                W[:, 8192:16384].rearrange("p (a v) -> p a v", a=NC),
                shdT_h[:, :].rearrange("(a p) v -> p a v", p=128)),
                waits=tmax(t_gmm_last, t_shg[0]), inc="ds3")
            t_hid_last = None
            for jc in range(NC):
                for ts in range(4):
                    b = bgbank()
                    def su_fn(e, jc=jc, ts=ts, b=b):
                        last = None
                        for cc in range(NC):
                            last = e.matmul(
                                ps[b][:, 0:512],
                                W[:, cc * C + jc * 128:cc * C + (jc + 1) * 128],
                                xnf[:, cc * T + ts * 512:cc * T + ts * 512 + 512],
                                start=(cc == 0), stop=(cc == NC - 1))
                        return last
                    t_umm = pb.emit("pe", su_fn,
                                    waits=tmax(t_shu[0], bank_free[b]), inc="pe")
                    t_hd = pb.emit("dve", lambda e, b=b, jc=jc, ts=ts: e.scalar_tensor_tensor(
                        G[:, G_SHH + jc * 2048 + ts * 512:G_SHH + jc * 2048 + (ts + 1) * 512],
                        G[:, G_SHH + jc * 2048 + ts * 512:G_SHH + jc * 2048 + (ts + 1) * 512],
                        0.5, ps[b][:, 0:512], op0=ALU.mult, op1=ALU.mult),
                        waits=tmax(t_umm, t_hid[(jc, ts)]), inc="dve")
                    bank_free[b] = t_hd
                    t_hid_last = t_hd
            t_out_ready = [None] * NT
            for tt in range(NT):
                for dh in range(2):
                    b = bgbank()
                    def sd_fn(e, tt=tt, dh=dh, b=b):
                        last = None
                        for jc in range(NC):
                            last = e.matmul(
                                ps[b][:, 0:512],
                                G[:, G_SHH + jc * 2048 + tt * 128:G_SHH + jc * 2048 + (tt + 1) * 128],
                                W[:, 8192 + jc * 1024 + dh * 512:8192 + jc * 1024 + dh * 512 + 512],
                                start=(jc == 0), stop=(jc == NC - 1))
                        return last
                    t_dmm = pb.emit("pe", sd_fn,
                                    waits=tmax(t_hid_last, t_shd[0], bank_free[b]), inc="pe")
                    t_de = pb.emit("dve", lambda e, tt=tt, dh=dh, b=b: e.tensor_tensor(
                        xacc[:, tt * C + dh * 512:tt * C + dh * 512 + 512],
                        ps[b][:, 0:512],
                        xacc[:, tt * C + dh * 512:tt * C + dh * 512 + 512],
                        op=ALU.add), waits=[t_dmm], inc="dve", drain=True)
                    bank_free[b] = t_de
                    t_out_ready[tt] = t_de

            for g in range(8):
                eng = "sp" if g % 2 == 0 else "pool"
                pb.emit(eng, lambda e, g=g: e.dma_start(
                    out_h[:, :].rearrange("(r p) n -> p r n", p=128)[:, 2 * g:2 * g + 2, :],
                    xacc[:, :].rearrange("p (r n) -> p r n", n=C)[:, 2 * g:2 * g + 2, :]),
                    waits=tmax(t_out_ready[2 * g + 1]), inc="dout")
            final_dout = pb.cnt["dout"]
            pb.emit("sp", lambda e: e.nop(), waits=[("dout", final_dout)])

        except _Stop:
            pass

        def body(eng_name):
            waited = {}
            def run(eng):
                for ws, fn, inc, drain in pb.lists[eng_name]:
                    for s, c in ws:
                        if waited.get(s, 0) < c:
                            eng.wait_ge(sems[s], c)
                            waited[s] = c
                    ins = fn(eng)
                    if drain:
                        ins = eng.drain()
                    if inc:
                        ins.then_inc(sems[inc], 16 if inc.startswith("d") else 1)
            return run

        block.sync(body("sp"))
        block.gpsimd(body("pool"))
        block.scalar(body("act"))
        block.vector(body("dve"))
        block.tensor(body("pe"))

    return nc


_cache = {}


def host_prep(L0_attn_x, L0_attn_w_qkv, L0_attn_w_proj, L0_attn_b_proj,
              L0_ffn_router_w, L0_ffn_router_bias, L0_ffn_up_proj, L0_ffn_gate_proj,
              L0_ffn_down_proj, L0_ffn_sh_gate, L0_ffn_sh_up, L0_ffn_sh_down):
    x = np.asarray(L0_attn_x, dtype=np.float32)
    wq = np.asarray(L0_attn_w_qkv, dtype=np.float32)
    wp = np.asarray(L0_attn_w_proj, dtype=np.float32)
    bp = np.asarray(L0_attn_b_proj, dtype=np.float32)
    rw = np.asarray(L0_ffn_router_w, dtype=np.float32)
    rb = np.asarray(L0_ffn_router_bias, dtype=np.float32)
    up = np.ascontiguousarray(np.asarray(L0_ffn_up_proj, dtype=np.float32))
    gate = np.ascontiguousarray(np.asarray(L0_ffn_gate_proj, dtype=np.float32))
    down = np.ascontiguousarray(np.asarray(L0_ffn_down_proj, dtype=np.float32))
    shg = np.asarray(L0_ffn_sh_gate, dtype=np.float32)
    shu = np.asarray(L0_ffn_sh_up, dtype=np.float32)
    shd = np.asarray(L0_ffn_sh_down, dtype=np.float32)

    wqkT = np.ascontiguousarray(np.transpose(wq[:, 0:128, :], (2, 0, 1)).reshape(C, NH * 128))
    wvT = np.ascontiguousarray(np.transpose(wq[:, 128:192, :], (2, 0, 1)).reshape(C, NH * HS))
    wpT = np.ascontiguousarray(wp.T)
    routerT = np.ascontiguousarray(rw.T)
    rbias_b = np.ascontiguousarray(np.broadcast_to(rb.reshape(1, NE), (128, NE)))
    bproj = np.ascontiguousarray(bp.reshape(1, C))
    shgT = np.ascontiguousarray(shg.T)
    shuT = np.ascontiguousarray(shu.T)
    shdT = np.ascontiguousarray(shd.T)
    ident = np.eye(128, dtype=ml_dtypes.bfloat16)
    maskneg = np.where(np.triu(np.ones((128, 128), np.float32)) > 0, 0.0, -1e9).astype(ml_dtypes.bfloat16)
    ones128 = np.ones((1, 128), dtype=ml_dtypes.bfloat16)
    iotam = np.ascontiguousarray(np.broadcast_to(
        (np.arange(NE, dtype=np.float32) - 16.0).reshape(1, NE), (128, NE)))

    shared = dict(wqkt=wqkT, wvt=wvT, wpt=wpT, bproj=bproj, routert=routerT,
                  rbias=rbias_b, up=up, gate=gate, down=down, shgt=shgT,
                  shut=shuT, shdt=shdT, ident=ident,
                  ones128=ones128, iotam=iotam, maskneg=maskneg)
    return x, shared


def kernel(**inputs):
    x, shared = host_prep(**inputs)
    if "nc" not in _cache:
        _cache["nc"] = build_nc()
    nc = _cache["nc"]
    in_maps = []
    for i in range(8):
        m = dict(shared)
        m["x"] = np.ascontiguousarray(x[i])
        in_maps.append(m)

    res = run_bass_kernel_spmd(nc, in_maps, core_ids=list(range(8)))
    out = np.stack([np.asarray(r["y"], dtype=np.float32) for r in res.results], axis=0)
    return out


# version 29
# speedup vs baseline: 1.2037x; 1.0051x over previous
import numpy as np
import ml_dtypes
import concourse.bass as bass
import concourse.mybir as mybir
from concourse.bass_utils import run_bass_kernel_spmd
from contextlib import ExitStack

F32 = mybir.dt.float32
BF16 = mybir.dt.bfloat16
AF = mybir.ActivationFunctionType
ALU = mybir.AluOpType
AX = mybir.AxisListType

T = 2048
C = 1024
NT = 16
NC = 8
NH = 16
HS = 64
NE = 16
ED = 512
NQS = 4

G_V = 0
G_P = 16640
G_WQK = 18176
G_COLS = 20224
G_SHH = 0

W_COLS = 16384

H_HBUF = 0
H_SCR = 4096
H_COLS = 6144

F_STAT = 0
F_EPS = 64
F_Z = 66
F_COMB = 322
F_T2 = 578
F_MSK = 706
F_IOTA = 770
F_COLS = 786

_STAGE = 9
_DUMPS = []


class _Stop(Exception):
    pass


ENGINES = ("sp", "pool", "act", "dve", "pe")
ENG_SEM = {"sp": None, "pool": None, "act": "act", "dve": "dve", "pe": "pe"}
SEM_NAMES = ("pe", "dve", "act", "dc", "dout",
             "dw", "dwp0", "dwp1", "dq0", "dq1", "ds0", "ds1", "ds2", "ds3") + tuple(
    f"dx{i}" for i in range(16))




class PB:

    def __init__(self):
        self.lists = {e: [] for e in ENGINES}
        self.cnt = {k: 0 for k in SEM_NAMES}

    def emit(self, eng, fn, waits=(), inc=None, drain=False):
        own = ENG_SEM[eng]
        ws = []
        for w in waits:
            if w is None:
                continue
            s, c = w
            if c <= 0:
                continue
            if s == own:
                continue
            assert c <= self.cnt[s], f"wait on future: {s} {c} > {self.cnt[s]}"
            ws.append((s, c))
        tok = None
        if inc:
            d = 16 if inc.startswith("d") else 1
            self.cnt[inc] += d
            tok = (inc, self.cnt[inc])
        self.lists[eng].append((ws, fn, inc, drain))
        return tok


def tmax(*toks):
    best = {}
    for t in toks:
        if t is None:
            continue
        if isinstance(t, list):
            for s, c in t:
                if best.get(s, -1) < c:
                    best[s] = c
            continue
        s, c = t
        if best.get(s, -1) < c:
            best[s] = c
    return [(s, c) for s, c in best.items()]


def build_nc():
    nc = bass.Bass("TRN2", target_bir_lowering=False)

    x_h = nc.dram_tensor("x", [T, C], F32, kind="ExternalInput")
    wqkT_h = nc.dram_tensor("wqkt", [C, NH * 128], F32, kind="ExternalInput")
    wvT_h = nc.dram_tensor("wvt", [C, NH * HS], F32, kind="ExternalInput")
    wpT_h = nc.dram_tensor("wpt", [C, C], F32, kind="ExternalInput")
    bproj_h = nc.dram_tensor("bproj", [1, C], F32, kind="ExternalInput")
    routerT_h = nc.dram_tensor("routert", [C, NE], F32, kind="ExternalInput")
    rbias_h = nc.dram_tensor("rbias", [128, NE], F32, kind="ExternalInput")
    up_h = nc.dram_tensor("up", [NE, C, ED], F32, kind="ExternalInput")
    gate_h = nc.dram_tensor("gate", [NE, C, ED], F32, kind="ExternalInput")
    down_h = nc.dram_tensor("down", [NE, ED, C], F32, kind="ExternalInput")
    shgT_h = nc.dram_tensor("shgt", [C, C], F32, kind="ExternalInput")
    shuT_h = nc.dram_tensor("shut", [C, C], F32, kind="ExternalInput")
    shdT_h = nc.dram_tensor("shdt", [C, C], F32, kind="ExternalInput")
    ident_h = nc.dram_tensor("ident", [128, 128], BF16, kind="ExternalInput")
    maskneg_h = nc.dram_tensor("maskneg", [128, 128], BF16, kind="ExternalInput")
    ones_h = nc.dram_tensor("ones128", [1, 128], BF16, kind="ExternalInput")
    iota_h = nc.dram_tensor("iotam", [128, NE], F32, kind="ExternalInput")
    out_h = nc.dram_tensor("y", [T, C], F32, kind="ExternalOutput")

    with ExitStack() as ctx:
        sb = lambda n, s, d: ctx.enter_context(nc.sbuf_tensor(n, s, d))
        xacc = sb("xacc", [128, NT * C], F32)
        xnf = sb("xnf", [128, NC * T], BF16)
        G = sb("G", [128, G_COLS], BF16)
        W = sb("W", [128, W_COLS], BF16)
        H = sb("H", [128, H_COLS], BF16)
        AP2 = sb("AP2", [128, 4096], BF16)
        WPS = sb("WPS", [128, 2048], BF16)
        LNS = AP2
        F = sb("F", [128, F_COLS], F32)
        Qb = sb("Qb", [65, T], BF16)
        Kb = sb("Kb", [65, T], BF16)
        ident = sb("identity", [128, 128], BF16)
        maskm = sb("maskneg_s", [128, 128], BF16)
        ones = sb("ones_r", [1, 128], BF16)
        bproj = sb("bproj_r", [1, C], BF16)
        routerT = sb("routerT", [128, NC * NE], BF16)
        rbias = sb("rbias_b", [128, NE], F32)
        rrowbf = sb("rrowbf", [1, 512], BF16)
        rbsb = sb("rbsb", [64, 512], BF16)

        ps = [ctx.enter_context(nc.psum_tensor(f"ps{i}", [128, 512], F32))
              for i in range(7)]
        pstr = ctx.enter_context(nc.psum_tensor("pstr", [128, 1024], BF16))

        sems = {k: ctx.enter_context(nc.semaphore(k)) for k in SEM_NAMES}
        block = ctx.enter_context(nc.Block())

        pb = PB()
        bank_free = {i: None for i in range(7)}
        bank_free["tr"] = None
        SB = [0, 1]
        PVB = [2, 3]
        _bg = [0]

        def bgbank():
            b = 4 + (_bg[0] % 3)
            _bg[0] += 1
            return b

        _DUMPS.clear()

        def stage_stop(k, dumps):
            if _STAGE != k:
                return
            for name, shape, dt, ap, waits in dumps:
                d = nc.dram_tensor(name, shape, dt, kind="ExternalOutput")
                pb.emit("sp", lambda e, d=d, ap=ap: e.dma_start(d[:, :], ap),
                        waits=waits, inc="dout")
                _DUMPS.append(name)
            fin = pb.cnt["dout"]
            pb.emit("sp", lambda e: e.nop(), waits=[("dout", fin)])
            raise _Stop()

        for tns, hsrc in ((ident, ident_h), (maskm, maskneg_h),
                          (ones, ones_h), (rbias, rbias_h)):
            pb.emit("sp", lambda e, t=tns, s_=hsrc: e.dma_start(t[:, :], s_[:, :]), inc="dc")
        pb.emit("sp", lambda e: e.dma_start(F[:, F_IOTA:F_IOTA + NE], iota_h[:, :]), inc="dc")
        pb.emit("pool", lambda e: e.dma_start(bproj[:, :], bproj_h[:, :]), inc="dw")
        t_consts = ("dc", pb.cnt["dc"])
        t_kone = pb.emit("dve", lambda e: e.memset(Kb[64:65, :], 1.0), inc="dve", drain=True)
        t_kone = pb.emit("dve", lambda e: e.memset(Qb[64:65, :], 0.0), inc="dve", drain=True)
        t_eps = pb.emit("dve", lambda e: e.memset(F[:, F_EPS:F_EPS + 1], 1e-5), inc="dve", drain=True)
        t_vones = pb.emit("dve", lambda e: e.memset(
            G[:, G_V:G_V + NT * 1040].rearrange("p (k h d) -> p k h d", h=NH, d=65)[:, :, :, 64:65],
            1.0), inc="dve", drain=True)
        t_ident = t_maskm = t_ones = t_rbias = t_iota = t_consts
        t_bproj = ("dw", pb.cnt["dw"])

        t_x = [None] * 16

        def emit_x(c):
            eng = "sp" if (c % 4) < 2 else "pool"
            t_x[c] = pb.emit(eng, lambda e, c=c: e.dma_start(
                xacc[:, :].rearrange("p (r n) -> p r n", n=C)[:, c:c + 1, :],
                x_h[:, :].rearrange("(r p) n -> p r n", p=128)[:, c:c + 1, :]),
                inc=f"dx{c}")

        t_wv = [None, None]

        def emit_wv(s):
            t_wv[s] = pb.emit("pool", lambda e, s=s: e.dma_start(
                W[:, s * 4096:(s + 1) * 4096].rearrange("p (a v) -> p a v", a=4),
                wvT_h[:, :].rearrange("(a p) v -> p a v", p=128)[:, 4 * s:4 * s + 4, :]),
                inc=f"ds{s}")

        t_wqk = [None] * NH

        def emit_wqk01(h):
            t_wqk[h] = pb.emit("pool", lambda e, h=h: e.dma_start(
                G[:, G_WQK + (h % 2) * 1024:G_WQK + (h % 2) * 1024 + 1024]
                .rearrange("p (a d) -> p a d", a=NC),
                wqkT_h[:, h * 128:(h + 1) * 128].rearrange("(a p) d -> p a d", p=128)),
                inc=f"dq{h % 2}")

        t_wps = [None, None]

        def emit_wps0(p):
            t_wps[p] = pb.emit("pool", lambda e, p=p: e.dma_start(
                WPS[:, p * 1024:(p + 1) * 1024],
                wpT_h[p * 128:(p + 1) * 128, :]), inc=f"dwp{p % 2}")

        t_wdma = {}
        slot_free = [None] * 4

        def emit_wdma(e_i, m):
            src = (up_h, gate_h, down_h)[m]
            s = (3 * e_i + m + 2) % 4
            inap = src[e_i, :, :].rearrange("(a p) j -> p a j", p=128)
            outap = W[:, s * 4096:(s + 1) * 4096].rearrange(
                "p (a j) -> p a j", a=NC if m < 2 else 4)
            t_wdma[(e_i, m)] = pb.emit("pool", lambda e, inap=inap, outap=outap: e.dma_start(
                outap, inap), waits=tmax(slot_free[s]), inc=f"ds{s}")

        for c in (0, 1, 4, 5, 8, 9, 12, 13):
            emit_x(c)
        emit_x(2); emit_x(3)
        emit_wv(0); emit_wv(1)
        emit_x(6); emit_x(7)
        emit_wqk01(0)
        pb.emit("pool", lambda e: e.dma_start(
            routerT[:, :].rearrange("p (a v) -> p a v", a=NC),
            routerT_h[:, :].rearrange("(a p) v -> p a v", p=128)), inc="dw")
        t_routerT = ("dw", pb.cnt["dw"])
        t_bproj = t_routerT
        emit_x(10); emit_x(11)
        emit_wps0(0); emit_wps0(1)
        emit_wdma(0, 0)
        emit_x(14); emit_x(15)
        emit_wdma(0, 1)
        emit_wqk01(1)

        bgq = []

        def fill(n):
            for _ in range(n):
                if not bgq:
                    return
                bgq.pop(0)()

        t_xnf = [None] * NT
        t_vready = [None] * NT
        t_comb = [None] * NT
        t_vlast = [None]

        def v_item(tt):
            def go():
                toks = []
                for half in range(2):
                    b = bgbank()
                    def v_fn(e, tt=tt, half=half, b=b):
                        last = None
                        for cc in range(NC):
                            last = e.matmul(
                                ps[b][:, 0:512],
                                xnf[:, cc * T + tt * 128:cc * T + (tt + 1) * 128],
                                W[:, cc * C + half * 512:cc * C + half * 512 + 512],
                                start=(cc == 0), stop=(cc == NC - 1))
                        return last
                    t_mm = pb.emit("pe", v_fn,
                                   waits=tmax(t_xnf[tt], t_wv[0], t_wv[1], bank_free[b]),
                                   inc="pe")
                    t_ev = pb.emit("dve", lambda e, tt=tt, half=half, b=b: e.tensor_copy(
                        G[:, G_V + tt * 1040:G_V + (tt + 1) * 1040]
                        .rearrange("p (h d) -> p h d", d=65)[:, half * 8:half * 8 + 8, 0:64],
                        ps[b][:, 0:512].rearrange("p (h d) -> p h d", d=64)),
                        waits=[t_mm], inc="dve")
                    bank_free[b] = t_ev
                    toks.append(t_ev)
                    t_vlast[0] = t_mm
                t_vready[tt] = toks[-1]
                if tt == NT - 1:
                    slot_free[0] = t_vlast[0]
                    slot_free[1] = t_vlast[0]
                    emit_wdma(0, 2)
                    emit_wdma(1, 0)
            return go

        def router_item(tt):
            z = F[:, F_Z + tt * NE:F_Z + (tt + 1) * NE]

            def go():
                b = bgbank()
                def r_fn(e, tt=tt, b=b):
                    last = None
                    for cc in range(NC):
                        last = e.matmul(
                            ps[b][:, 0:NE],
                            xnf[:, cc * T + tt * 128:cc * T + (tt + 1) * 128],
                            routerT[:, cc * NE:(cc + 1) * NE],
                            start=(cc == 0), stop=(cc == NC - 1))
                    return last
                t_rmm = pb.emit("pe", r_fn,
                                waits=tmax(t_xnf[tt], t_routerT, bank_free[b]), inc="pe")
                t_z = pb.emit("dve", lambda e, z=z, b=b: e.tensor_tensor(
                    z, ps[b][:, 0:NE], rbias[:, :], op=ALU.add),
                    waits=tmax(t_rmm, t_rbias, t_iota), inc="dve", drain=True)
                bank_free[b] = t_z
            return go

        def emit_router_top2():
            z3 = F[:, F_Z:F_Z + 256].rearrange("p (t e) -> p t e", e=NE)
            cb3 = F[:, F_COMB:F_COMB + 256].rearrange("p (t e) -> p t e", e=NE)
            m1 = F[:, F_T2 + 0:F_T2 + 16]
            m2 = F[:, F_T2 + 16:F_T2 + 32]
            e2 = F[:, F_T2 + 32:F_T2 + 48]
            s1v = F[:, F_T2 + 48:F_T2 + 64]
            s2v = F[:, F_T2 + 64:F_T2 + 80]
            den = F[:, F_T2 + 80:F_T2 + 96]
            i1 = F[:, F_T2 + 96:F_T2 + 112]
            i2 = F[:, F_T2 + 112:F_T2 + 128]
            msk1 = AP2[:, 0:256].rearrange("p (t e) -> p t e", e=NE)
            msk2 = AP2[:, 256:512].rearrange("p (t e) -> p t e", e=NE)
            tmpb = AP2[:, 512:768].rearrange("p (t e) -> p t e", e=NE)
            iotb = F[:, F_IOTA:F_IOTA + NE].rearrange("p e -> p () e").broadcast_to([128, 16, NE])
            bt = lambda a: a.rearrange("p t -> p t ()").broadcast_to([128, 16, NE])
            pb.emit("dve", lambda e: e.tensor_reduce(m1, z3, axis=AX.X, op=ALU.max),
                    inc="dve", drain=True)
            pb.emit("dve", lambda e: e.tensor_tensor(msk1, z3, bt(m1), op=ALU.is_equal),
                    waits=tmax(t_tr[NT - 2]), inc="dve", drain=True)
            pb.emit("dve", lambda e: e.tensor_tensor(tmpb, msk1, iotb, op=ALU.mult),
                    inc="dve", drain=True)
            pb.emit("dve", lambda e: e.tensor_reduce(i1, tmpb, axis=AX.X, op=ALU.min),
                    inc="dve", drain=True)
            pb.emit("dve", lambda e: e.tensor_tensor(msk1, iotb, bt(i1), op=ALU.is_equal),
                    inc="dve", drain=True)
            pb.emit("dve", lambda e: e.scalar_tensor_tensor(
                z3, msk1, -1e30, z3, op0=ALU.mult, op1=ALU.add), inc="dve", drain=True)
            pb.emit("dve", lambda e: e.tensor_reduce(m2, z3, axis=AX.X, op=ALU.max),
                    inc="dve", drain=True)
            pb.emit("dve", lambda e: e.tensor_tensor(msk2, z3, bt(m2), op=ALU.is_equal),
                    inc="dve", drain=True)
            pb.emit("dve", lambda e: e.tensor_tensor(tmpb, msk2, iotb, op=ALU.mult),
                    inc="dve", drain=True)
            pb.emit("dve", lambda e: e.tensor_reduce(i2, tmpb, axis=AX.X, op=ALU.min),
                    inc="dve", drain=True)
            pb.emit("dve", lambda e: e.tensor_tensor(msk2, iotb, bt(i2), op=ALU.is_equal),
                    inc="dve", drain=True)
            t_sub = pb.emit("dve", lambda e: e.tensor_tensor(e2, m2, m1, op=ALU.subtract),
                            inc="dve", drain=True)
            t_exp = pb.emit("act", lambda e: e.activation(e2, e2, AF.Exp),
                            waits=[t_sub], inc="act", drain=True)
            pb.emit("dve", lambda e: e.tensor_scalar(
                den, e2, 1.0, None, op0=ALU.add), waits=[t_exp], inc="dve", drain=True)
            pb.emit("dve", lambda e: e.reciprocal(s1v, den), inc="dve", drain=True)
            pb.emit("dve", lambda e: e.tensor_scalar(
                s2v, s1v, -1.0, 1.0, op0=ALU.mult, op1=ALU.add), inc="dve", drain=True)
            pb.emit("dve", lambda e: e.tensor_tensor(cb3, msk1, bt(s1v), op=ALU.mult),
                    inc="dve", drain=True)
            pb.emit("dve", lambda e: e.tensor_tensor(z3, msk2, bt(s2v), op=ALU.mult),
                    inc="dve", drain=True)
            tok = pb.emit("dve", lambda e: e.tensor_tensor(cb3, cb3, z3, op=ALU.add),
                          inc="dve", drain=True)
            for tt in range(NT):
                t_comb[tt] = tok
            return tok

        h_done = {}
        t_hbuf_rd = [None, None]
        t_scr_rd = [None, None]
        _scr = [0]

        ustate = {}

        def u_item(e_i, ts, jc):
            def go():
                su = (3 * e_i + 0 + 2) % 4
                bu = bgbank()
                par = ts % 2
                def u_fn(e, jc=jc, ts=ts, bu=bu, base=su * 4096):
                    last = None
                    for cc in range(NC):
                        last = e.matmul(
                            ps[bu][:, 0:512],
                            W[:, base + cc * 512 + jc * 128:base + cc * 512 + (jc + 1) * 128],
                            xnf[:, cc * T + ts * 512:cc * T + ts * 512 + 512],
                            start=(cc == 0), stop=(cc == NC - 1))
                    return last
                t_umm = pb.emit("pe", u_fn,
                                waits=tmax(t_wdma[(e_i, 0)], bank_free[bu]), inc="pe")
                t_ue = pb.emit("act", lambda e, bu=bu, par=par, jc=jc: e.activation(
                    H[:, H_HBUF + par * 2048 + jc * 512:H_HBUF + par * 2048 + (jc + 1) * 512],
                    ps[bu][:, 0:512], AF.Identity),
                    waits=tmax(t_umm, t_hbuf_rd[par]), inc="act")
                bank_free[bu] = t_ue
                ustate[(e_i, ts, jc)] = t_ue
                if ts == 3 and jc == 3:
                    slot_free[su] = t_umm
                    if e_i + 1 == NE:
                        t_shg[0] = pb.emit("pool", lambda e: e.dma_start(
                            W[:, 8192:16384].rearrange("p (a v) -> p a v", a=NC),
                            shgT_h[:, :].rearrange("(a p) v -> p a v", p=128)),
                            waits=tmax(slot_free[2], slot_free[3]), inc="ds2")
            return go

        def g_item(e_i, ts, jc):
            def go():
                sg = (3 * e_i + 1 + 2) % 4
                bg2 = bgbank()
                par = ts % 2
                spar = _scr[0] % 2
                _scr[0] += 1
                t_ue = ustate.pop((e_i, ts, jc))
                def g_fn(e, jc=jc, ts=ts, bg2=bg2, base=sg * 4096):
                    last = None
                    for cc in range(NC):
                        last = e.matmul(
                            ps[bg2][:, 0:512],
                            W[:, base + cc * 512 + jc * 128:base + cc * 512 + (jc + 1) * 128],
                            xnf[:, cc * T + ts * 512:cc * T + ts * 512 + 512],
                            start=(cc == 0), stop=(cc == NC - 1))
                    return last
                t_gmm = pb.emit("pe", g_fn,
                                waits=tmax(t_wdma[(e_i, 1)], bank_free[bg2]), inc="pe")
                scr = H_SCR + spar * 1024
                t_th = pb.emit("act", lambda e, bg2=bg2, scr=scr: e.activation(
                    H[:, scr:scr + 512], ps[bg2][:, 0:512], AF.Tanh, scale=0.5),
                    waits=tmax(t_gmm, t_scr_rd[spar]), inc="act")
                t_2s = pb.emit("dve", lambda e, bg2=bg2, scr=scr: e.scalar_tensor_tensor(
                    H[:, scr + 512:scr + 1024], H[:, scr:scr + 512], 1.0,
                    ps[bg2][:, 0:512], op0=ALU.add, op1=ALU.mult),
                    waits=tmax(t_th), inc="dve")
                bank_free[bg2] = t_2s
                t_h = pb.emit("dve", lambda e, par=par, jc=jc, scr=scr: e.scalar_tensor_tensor(
                    H[:, H_HBUF + par * 2048 + jc * 512:H_HBUF + par * 2048 + (jc + 1) * 512],
                    H[:, scr + 512:scr + 1024], 0.5,
                    H[:, H_HBUF + par * 2048 + jc * 512:H_HBUF + par * 2048 + (jc + 1) * 512],
                    op0=ALU.mult, op1=ALU.mult),
                    waits=tmax(t_ue), inc="dve")
                t_scr_rd[spar] = t_h
                if jc == 3:
                    h_done[(e_i, ts)] = t_h
                if ts == 3 and jc == 3:
                    slot_free[sg] = t_gmm
                    if e_i + 1 < NE:
                        emit_wdma(e_i + 1, 1)
            return go

        def down_item(e_i, ts, tti, dh):
            def go():
                sd = (3 * e_i + 2 + 2) % 4
                par = ts % 2
                tt = ts * 4 + tti
                assert t_comb[tt] is not None and h_done.get((e_i, ts)) is not None
                cbcol = F[:, F_COMB + tt * NE + e_i:F_COMB + tt * NE + e_i + 1]
                b = bgbank()
                def d_fn(e, par=par, tti=tti, dh=dh, b=b, base=sd * 4096):
                    last = None
                    for jc in range(4):
                        last = e.matmul(
                            ps[b][:, 0:512],
                            H[:, H_HBUF + par * 2048 + jc * 512 + tti * 128:
                               H_HBUF + par * 2048 + jc * 512 + (tti + 1) * 128],
                            W[:, base + jc * 1024 + dh * 512:base + jc * 1024 + dh * 512 + 512],
                            start=(jc == 0), stop=(jc == 3))
                    return last
                t_dmm = pb.emit("pe", d_fn,
                                waits=tmax(h_done[(e_i, ts)], t_wdma[(e_i, 2)],
                                           bank_free[b]), inc="pe")
                t_hbuf_rd[par] = t_dmm
                t_de = pb.emit("dve", lambda e, tt=tt, dh=dh, b=b, cbcol=cbcol: e.scalar_tensor_tensor(
                    xacc[:, tt * C + dh * 512:tt * C + dh * 512 + 512],
                    ps[b][:, 0:512], cbcol,
                    xacc[:, tt * C + dh * 512:tt * C + dh * 512 + 512],
                    op0=ALU.mult, op1=ALU.add),
                    waits=tmax(t_dmm, t_comb[tt]), inc="dve")
                bank_free[b] = t_de
                if ts == 3 and tti == 3 and dh == 1:
                    slot_free[sd] = ("pe", pb.cnt["pe"])
                    if e_i + 1 < NE:
                        emit_wdma(e_i + 1, 2)
                    if e_i + 2 < NE:
                        emit_wdma(e_i + 2, 0)
                    if e_i + 1 == NE:
                        t_shu[0] = pb.emit("pool", lambda e: e.dma_start(
                            W[:, 0:8192].rearrange("p (a v) -> p a v", a=NC),
                            shuT_h[:, :].rearrange("(a p) v -> p a v", p=128)),
                            waits=tmax(slot_free[0], slot_free[1]), inc="ds0")
            return go

        t_shg = [None]
        t_shu = [None]
        t_shd = [None]

        try:
            t_tr = [None] * NT
            t_qk_evac = {}
            wqk_reader = [None, None]
            t_nmrl = [None] * NT

            def emit_stats(tt):
                xs = xacc[:, tt * C:(tt + 1) * C]
                st = F_STAT + tt * 4
                par = tt % 2
                lns = LNS[:, par * 1024:(par + 1) * 1024]
                pb.emit("dve", lambda e, xs=xs, st=st: e.reduce_sum(
                    F[:, st:st + 1], xs, axis=AX.X), waits=[t_x[tt]], inc="dve", drain=True)
                t_ssq = pb.emit("act", lambda e, xs=xs, st=st, lns=lns: e.activation(
                    lns, xs, AF.Square, accum_out=F[:, st + 3:st + 4]),
                    waits=tmax(t_x[tt], t_tr[tt - 2] if tt >= 2 else None),
                    inc="act", drain=True)
                pb.emit("dve", lambda e, st=st: e.tensor_scalar(
                    F[:, st:st + 1], F[:, st:st + 1], 1.0 / C, None, op0=ALU.mult), inc="dve", drain=True)
                pb.emit("dve", lambda e, st=st: e.tensor_scalar(
                    F[:, st + 1:st + 2], F[:, st + 3:st + 4], 1.0 / C, None, op0=ALU.mult),
                    waits=[t_ssq], inc="dve", drain=True)
                pb.emit("dve", lambda e, st=st: e.tensor_tensor(
                    F[:, st + 2:st + 3], F[:, st:st + 1], F[:, st:st + 1], op=ALU.mult), inc="dve", drain=True)
                t_var = pb.emit("dve", lambda e, st=st: e.tensor_tensor(
                    F[:, st + 1:st + 2], F[:, st + 1:st + 2], F[:, st + 2:st + 3],
                    op=ALU.subtract), inc="dve", drain=True)
                pb.emit("act", lambda e, st=st: e.activation(
                    F[:, st + 1:st + 2], F[:, st + 1:st + 2], AF.Ln,
                    bias=F[:, F_EPS:F_EPS + 1], scale=1.0),
                    waits=tmax(t_eps, t_var), inc="act", drain=True)
                t_rstd = pb.emit("act", lambda e, st=st: e.activation(
                    F[:, st + 1:st + 2], F[:, st + 1:st + 2], AF.Exp, scale=-0.5),
                    inc="act", drain=True)
                t_nmrl[tt] = pb.emit("dve", lambda e, st=st: e.tensor_scalar(
                    F[:, st + 2:st + 3], F[:, st:st + 1], F[:, st + 1:st + 2], -1.0,
                    op0=ALU.mult, op1=ALU.mult), waits=[t_rstd], inc="dve", drain=True)

            emit_stats(0)
            for tt in range(NT):
                xs = xacc[:, tt * C:(tt + 1) * C]
                st = F_STAT + tt * 4
                par = tt % 2
                lns = LNS[:, par * 1024:(par + 1) * 1024]
                t_apply = pb.emit("act", lambda e, xs=xs, st=st, lns=lns: e.activation(
                    lns, xs, AF.Identity,
                    bias=F[:, st + 2:st + 3], scale=F[:, st + 1:st + 2]),
                    waits=[t_nmrl[tt]], inc="act", drain=True)
                def tr_fn(e, lns=lns):
                    last = None
                    for cc in range(NC):
                        last = e.transpose(pstr[:, cc * 128:(cc + 1) * 128],
                                           lns[:, cc * 128:(cc + 1) * 128],
                                           ident[:, :])
                    return last
                t_tr[tt] = pb.emit("pe", tr_fn,
                                   waits=tmax(t_apply, t_ident, bank_free["tr"]), inc="pe")
                t_xnf[tt] = pb.emit("dve", lambda e, tt=tt: e.tensor_copy(
                    xnf[:, :].rearrange("p (a t) -> p a t", a=NC)[:, :, tt * 128:(tt + 1) * 128],
                    pstr[:, :].rearrange("p (a d) -> p a d", a=NC)),
                    waits=[t_tr[tt]], inc="dve")
                bank_free["tr"] = t_xnf[tt]
                if tt + 1 < NT:
                    emit_stats(tt + 1)
                bgq.append(v_item(tt))
                bgq.append(router_item(tt))
                if tt % 4 == 3:
                    ts_ = tt // 4
                    b = bgbank()
                    def qk_fn(e, ts_=ts_, b=b):
                        last = None
                        for cc in range(NC):
                            last = e.matmul(
                                ps[b][:, 0:512],
                                G[:, G_WQK + cc * 128:G_WQK + (cc + 1) * 128],
                                xnf[:, cc * T + ts_ * 512:cc * T + ts_ * 512 + 512],
                                start=(cc == 0), stop=(cc == NC - 1))
                        return last
                    t_mm = pb.emit("pe", qk_fn,
                                   waits=tmax(t_wqk[0], t_xnf[tt], bank_free[b]), inc="pe")
                    if ts_ == 3:
                        wqk_reader[0] = t_mm
                        t_wqk[2] = pb.emit("pool", lambda e: e.dma_start(
                            G[:, G_WQK:G_WQK + 1024].rearrange("p (a d) -> p a d", a=NC),
                            wqkT_h[:, 2 * 128:3 * 128].rearrange("(a p) d -> p a d", p=128)),
                            waits=tmax(wqk_reader[0]), inc="dq0")
                    pb.emit("act", lambda e, ts_=ts_, b=b: e.activation(
                        Qb[0:64, ts_ * 512:(ts_ + 1) * 512], ps[b][0:64, 0:512],
                        AF.Identity), waits=[t_mm], inc="act")
                    t_qk_evac[ts_] = pb.emit("act", lambda e, ts_=ts_, b=b: e.activation(
                        Kb[0:64, ts_ * 512:(ts_ + 1) * 512], ps[b][64:128, 0:512],
                        AF.Identity), inc="act")
                    bank_free[b] = t_qk_evac[ts_]
                if tt >= 2:
                    fill(3)

            stage_stop(1, [
                ("d_xnf", [128, NC * T], BF16, xnf[:, :], tmax(t_xnf[NT - 1])),
            ])

            while bgq:
                bgq.pop(0)()
            pend = None
            for e_i in range(NE):
                for ts in range(4):
                    for jc in range(4):
                        bgq.append(u_item(e_i, ts, jc))
                        bgq.append(g_item(e_i, ts, jc))
                    if pend is not None:
                        pe_, pts = pend
                        for tti in range(4):
                            for dh in range(2):
                                bgq.append(down_item(pe_, pts, tti, dh))
                    pend = (e_i, ts)
            for tti in range(4):
                for dh in range(2):
                    bgq.append(down_item(NE - 1, 3, tti, dh))

            t_gpfree = emit_router_top2()

            NPP = 3
            t_rb_used = None
            t_s_last = None
            t_amult = [[], []]
            proj_free = [None, None]

            def emit_qk(h):
                for ts_ in range(4):
                    b = bgbank()
                    def qk_fn(e, ts_=ts_, b=b, h=h):
                        last = None
                        for cc in range(NC):
                            last = e.matmul(
                                ps[b][:, 0:512],
                                G[:, G_WQK + (h % 2) * 1024 + cc * 128:
                                   G_WQK + (h % 2) * 1024 + (cc + 1) * 128],
                                xnf[:, cc * T + ts_ * 512:cc * T + ts_ * 512 + 512],
                                start=(cc == 0), stop=(cc == NC - 1))
                        return last
                    t_mm = pb.emit("pe", qk_fn,
                                   waits=tmax(t_wqk[h], t_xnf[NT - 1], bank_free[b]), inc="pe")
                    if ts_ == 3:
                        wqk_reader[h % 2] = t_mm
                        if h + 2 < NH:
                            par2 = (h + 2) % 2
                            t_wqk[h + 2] = pb.emit("pool", lambda e, h2=h + 2, par2=par2: e.dma_start(
                                G[:, G_WQK + par2 * 1024:G_WQK + par2 * 1024 + 1024]
                                .rearrange("p (a d) -> p a d", a=NC),
                                wqkT_h[:, h2 * 128:(h2 + 1) * 128].rearrange("(a p) d -> p a d", p=128)),
                                waits=tmax(wqk_reader[par2]), inc=f"dq{par2}")
                    pb.emit("act", lambda e, ts_=ts_, b=b: e.activation(
                        Qb[0:64, ts_ * 512:(ts_ + 1) * 512], ps[b][0:64, 0:512],
                        AF.Identity), waits=tmax(t_mm, t_s_last), inc="act")
                    t_qk_evac[ts_] = pb.emit("act", lambda e, ts_=ts_, b=b: e.activation(
                        Kb[0:64, ts_ * 512:(ts_ + 1) * 512], ps[b][64:128, 0:512],
                        AF.Identity), inc="act")
                    bank_free[b] = t_qk_evac[ts_]
                    fill(1)

            def emit_proj(p):
                par = p % 2
                t_mm = None
                for tt in range(NT):
                    for dh in range(2):
                        b = bgbank()
                        def pj_fn(e, tt=tt, dh=dh, b=b, par=par, p=p):
                            m = e.matmul(ps[b][:, 0:512],
                                         AP2[:, par * 2048 + tt * 128:par * 2048 + (tt + 1) * 128],
                                         WPS[:, par * 1024 + dh * 512:par * 1024 + dh * 512 + 512],
                                         start=True, stop=(p != 0))
                            if p == 0:
                                m = e.matmul(ps[b][:, 0:512], ones[0:1, :],
                                             bproj[0:1, dh * 512:dh * 512 + 512],
                                             start=False, stop=True)
                            return m
                        t_mm = pb.emit("pe", pj_fn,
                                       waits=tmax(t_amult[par], t_wps[par], t_ones,
                                                  t_bproj, bank_free[b]), inc="pe")
                        t_pe = pb.emit("dve", lambda e, tt=tt, dh=dh, b=b: e.tensor_tensor(
                            xacc[:, tt * C + dh * 512:tt * C + dh * 512 + 512],
                            ps[b][:, 0:512],
                            xacc[:, tt * C + dh * 512:tt * C + dh * 512 + 512],
                            op=ALU.add), waits=[t_mm], inc="dve")
                        bank_free[b] = t_pe
                proj_free[par] = t_mm
                t_amult[par] = []
                if p + 2 < 8:
                    t_wps[par] = pb.emit("pool", lambda e, p2=p + 2, par=par: e.dma_start(
                        WPS[:, par * 1024:(par + 1) * 1024],
                        wpT_h[p2 * 128:(p2 + 1) * 128, :]),
                        waits=[t_mm], inc=f"dwp{par}")

            for h in range(NH):
                if h > 0:
                    emit_qk(h)
                else:
                    pass
                it = 0
                t_pv_stop = [None] * NQS
                p_used = [None] * NPP
                t_exp_of = [None] * NPP
                pend_q = []
                def push_pending(fn):
                    pend_q.append(fn)
                    if len(pend_q) > 3:
                        pend_q.pop(0)()
                def flush_pending():
                    while pend_q:
                        pend_q.pop(0)()
                for qs in range(NQS):
                    for kt in range(4 * qs + 4):
                        b = SB[it % 2]
                        j = kt - 4 * qs
                        if j > 0:
                            off, masked = 128 * j, True
                        elif j == 0:
                            off, masked = 0, True
                        else:
                            off, masked = 0, False
                        N = 512 - off
                        pp = it % NPP
                        def s_fn(e, b=b, kt=kt, qs=qs, off=off, N=N, masked=masked):
                            m = e.matmul(ps[b][:, 0:N],
                                         Kb[0:65, kt * 128:(kt + 1) * 128],
                                         Qb[0:65, qs * 512 + off:qs * 512 + 512],
                                         start=True, stop=(not masked))
                            if masked:
                                m = e.matmul(ps[b][:, 0:128], ident[:, :], maskm[:, :],
                                             start=False, stop=True)
                            return m
                        t_s = pb.emit("pe", s_fn,
                            waits=tmax(bank_free[b], t_qk_evac[qs], t_qk_evac[kt // 4],
                                       t_kone, t_maskm, t_ident),
                            inc="pe")
                        t_s_last = t_s

                        def pv_back(kt=kt, h=h, pp=pp, off=off, N=N, qs=qs):
                            pvb = PVB[qs % 2]
                            assert t_vready[kt] is not None
                            w = tmax(t_exp_of[pp], t_vready[kt], t_vones)
                            if kt == 0:
                                w = tmax(w, bank_free[pvb])
                            t_pv = pb.emit("pe", lambda e, pvb=pvb, kt=kt, h=h, pp=pp, off=off, N=N: e.matmul(
                                ps[pvb][0:65, off:512],
                                G[:, G_V + kt * 1040 + h * 65:G_V + kt * 1040 + h * 65 + 65],
                                G[:, G_P + pp * 512:G_P + pp * 512 + N],
                                start=(kt == 0), stop=(kt == 4 * qs + 3)),
                                waits=w, inc="pe")
                            p_used[pp] = t_pv
                            if kt == 4 * qs + 3:
                                t_pv_stop[qs] = t_pv
                        push_pending(pv_back)
                        t_e = pb.emit("act", lambda e, b=b, N=N, pp=pp: e.activation(
                            G[:, G_P + pp * 512:G_P + pp * 512 + N],
                            ps[b][:, 0:N], AF.Exp, scale=0.125),
                            waits=tmax(t_s, p_used[pp]), inc="act", drain=True)
                        bank_free[b] = t_e
                        t_exp_of[pp] = t_e
                        fill(1)
                        it += 1
                    flush_pending()
                    pvb = PVB[qs % 2]
                    t_cp = pb.emit("dve", lambda e, pvb=pvb: e.tensor_copy(
                        rrowbf[0:1, 0:512], ps[pvb][64:65, 0:512]),
                        waits=tmax(t_pv_stop[qs], t_rb_used), inc="dve", drain=True)
                    fill(2)
                    rbb = bgbank()
                    t_bc = pb.emit("pe", lambda e, rbb=rbb: e.matmul(
                        ps[rbb][0:64, 0:512], ones[0:1, 0:64], rrowbf[0:1, 0:512],
                        start=True, stop=True),
                        waits=tmax(t_cp, t_ones, bank_free[rbb]), inc="pe")
                    t_rb_used = t_bc
                    def ri_fn(e, rbb=rbb):
                        with nc.allow_low_precision(reason="rowsum rinv is consumed as a bf16 elementwise operand"):
                            return e.reciprocal(rbsb[:, :], ps[rbb][0:64, 0:512])
                    t_rbe = pb.emit("dve", ri_fn, waits=[t_bc], inc="dve")
                    bank_free[rbb] = t_rbe
                    par = (h // 2) % 2
                    r0 = (h % 2) * 64
                    t_am = pb.emit("dve", lambda e, pvb=pvb, par=par, r0=r0, qs=qs: e.tensor_tensor(
                        AP2[r0:r0 + 64, par * 2048 + qs * 512:par * 2048 + qs * 512 + 512],
                        ps[pvb][0:64, 0:512], rbsb[:, :], op=ALU.mult),
                        waits=tmax(t_rbe, proj_free[par]), inc="dve")
                    bank_free[pvb] = t_am
                    t_amult[par].append(t_am)
                    fill(1)
                if h % 2 == 1 and h // 2 >= 1:
                    emit_proj(h // 2 - 1)
                fill(1)
            emit_proj(7)
            pe_attn_done = ("pe", pb.cnt["pe"])

            stage_stop(3, [
                ("d_xacc", [128, NT * C], F32, xacc[:, :], tmax(proj_free[1])),
            ])

            while bgq:
                bgq.pop(0)()

            t_gmm_last = None
            t_hid = {}
            for jc in range(NC):
                for ts in range(4):
                    b = bgbank()
                    def sg_fn(e, jc=jc, ts=ts, b=b):
                        last = None
                        for cc in range(NC):
                            last = e.matmul(
                                ps[b][:, 0:512],
                                W[:, 8192 + cc * C + jc * 128:8192 + cc * C + (jc + 1) * 128],
                                xnf[:, cc * T + ts * 512:cc * T + ts * 512 + 512],
                                start=(cc == 0), stop=(cc == NC - 1))
                        return last
                    t_gmm = pb.emit("pe", sg_fn,
                                    waits=tmax(t_shg[0], bank_free[b], pe_attn_done), inc="pe")
                    t_gmm_last = t_gmm
                    spar = _scr[0] % 2
                    _scr[0] += 1
                    scr = H_SCR + spar * 1024
                    t_th = pb.emit("act", lambda e, b=b, scr=scr: e.activation(
                        H[:, scr:scr + 512], ps[b][:, 0:512], AF.Tanh, scale=0.5),
                        waits=tmax(t_gmm, t_scr_rd[spar]), inc="act")
                    t_2s = pb.emit("dve", lambda e, b=b, scr=scr, jc=jc, ts=ts: e.scalar_tensor_tensor(
                        G[:, G_SHH + jc * 2048 + ts * 512:G_SHH + jc * 2048 + (ts + 1) * 512],
                        H[:, scr:scr + 512], 1.0, ps[b][:, 0:512],
                        op0=ALU.add, op1=ALU.mult),
                        waits=tmax(t_th, pe_attn_done), inc="dve")
                    bank_free[b] = t_2s
                    t_scr_rd[spar] = t_2s
                    t_hid[(jc, ts)] = t_2s
            t_shd[0] = pb.emit("pool", lambda e: e.dma_start(
                W[:, 8192:16384].rearrange("p (a v) -> p a v", a=NC),
                shdT_h[:, :].rearrange("(a p) v -> p a v", p=128)),
                waits=tmax(t_gmm_last, t_shg[0]), inc="ds3")
            t_hid_last = None
            for jc in range(NC):
                for ts in range(4):
                    b = bgbank()
                    def su_fn(e, jc=jc, ts=ts, b=b):
                        last = None
                        for cc in range(NC):
                            last = e.matmul(
                                ps[b][:, 0:512],
                                W[:, cc * C + jc * 128:cc * C + (jc + 1) * 128],
                                xnf[:, cc * T + ts * 512:cc * T + ts * 512 + 512],
                                start=(cc == 0), stop=(cc == NC - 1))
                        return last
                    t_umm = pb.emit("pe", su_fn,
                                    waits=tmax(t_shu[0], bank_free[b]), inc="pe")
                    t_hd = pb.emit("dve", lambda e, b=b, jc=jc, ts=ts: e.scalar_tensor_tensor(
                        G[:, G_SHH + jc * 2048 + ts * 512:G_SHH + jc * 2048 + (ts + 1) * 512],
                        G[:, G_SHH + jc * 2048 + ts * 512:G_SHH + jc * 2048 + (ts + 1) * 512],
                        0.5, ps[b][:, 0:512], op0=ALU.mult, op1=ALU.mult),
                        waits=tmax(t_umm, t_hid[(jc, ts)]), inc="dve")
                    bank_free[b] = t_hd
                    t_hid_last = t_hd
            t_out_ready = [None] * NT
            for tt in range(NT):
                for dh in range(2):
                    b = bgbank()
                    def sd_fn(e, tt=tt, dh=dh, b=b):
                        last = None
                        for jc in range(NC):
                            last = e.matmul(
                                ps[b][:, 0:512],
                                G[:, G_SHH + jc * 2048 + tt * 128:G_SHH + jc * 2048 + (tt + 1) * 128],
                                W[:, 8192 + jc * 1024 + dh * 512:8192 + jc * 1024 + dh * 512 + 512],
                                start=(jc == 0), stop=(jc == NC - 1))
                        return last
                    t_dmm = pb.emit("pe", sd_fn,
                                    waits=tmax(t_hid_last, t_shd[0], bank_free[b]), inc="pe")
                    t_de = pb.emit("dve", lambda e, tt=tt, dh=dh, b=b: e.tensor_tensor(
                        xacc[:, tt * C + dh * 512:tt * C + dh * 512 + 512],
                        ps[b][:, 0:512],
                        xacc[:, tt * C + dh * 512:tt * C + dh * 512 + 512],
                        op=ALU.add), waits=[t_dmm], inc="dve", drain=True)
                    bank_free[b] = t_de
                    t_out_ready[tt] = t_de

            for g in range(8):
                eng = "sp" if g % 2 == 0 else "pool"
                pb.emit(eng, lambda e, g=g: e.dma_start(
                    out_h[:, :].rearrange("(r p) n -> p r n", p=128)[:, 2 * g:2 * g + 2, :],
                    xacc[:, :].rearrange("p (r n) -> p r n", n=C)[:, 2 * g:2 * g + 2, :]),
                    waits=tmax(t_out_ready[2 * g + 1]), inc="dout")
            final_dout = pb.cnt["dout"]
            pb.emit("sp", lambda e: e.nop(), waits=[("dout", final_dout)])

        except _Stop:
            pass

        def body(eng_name):
            waited = {}
            def run(eng):
                for ws, fn, inc, drain in pb.lists[eng_name]:
                    for s, c in ws:
                        if waited.get(s, 0) < c:
                            eng.wait_ge(sems[s], c)
                            waited[s] = c
                    ins = fn(eng)
                    if drain:
                        ins = eng.drain()
                    if inc:
                        ins.then_inc(sems[inc], 16 if inc.startswith("d") else 1)
            return run

        block.sync(body("sp"))
        block.gpsimd(body("pool"))
        block.scalar(body("act"))
        block.vector(body("dve"))
        block.tensor(body("pe"))

    return nc


_cache = {}


def host_prep(L0_attn_x, L0_attn_w_qkv, L0_attn_w_proj, L0_attn_b_proj,
              L0_ffn_router_w, L0_ffn_router_bias, L0_ffn_up_proj, L0_ffn_gate_proj,
              L0_ffn_down_proj, L0_ffn_sh_gate, L0_ffn_sh_up, L0_ffn_sh_down):
    x = np.asarray(L0_attn_x, dtype=np.float32)
    wq = np.asarray(L0_attn_w_qkv, dtype=np.float32)
    wp = np.asarray(L0_attn_w_proj, dtype=np.float32)
    bp = np.asarray(L0_attn_b_proj, dtype=np.float32)
    rw = np.asarray(L0_ffn_router_w, dtype=np.float32)
    rb = np.asarray(L0_ffn_router_bias, dtype=np.float32)
    up = np.ascontiguousarray(np.asarray(L0_ffn_up_proj, dtype=np.float32))
    gate = np.ascontiguousarray(np.asarray(L0_ffn_gate_proj, dtype=np.float32))
    down = np.ascontiguousarray(np.asarray(L0_ffn_down_proj, dtype=np.float32))
    shg = np.asarray(L0_ffn_sh_gate, dtype=np.float32)
    shu = np.asarray(L0_ffn_sh_up, dtype=np.float32)
    shd = np.asarray(L0_ffn_sh_down, dtype=np.float32)

    wqkT = np.ascontiguousarray(np.transpose(wq[:, 0:128, :], (2, 0, 1)).reshape(C, NH * 128))
    wvT = np.ascontiguousarray(np.transpose(wq[:, 128:192, :], (2, 0, 1)).reshape(C, NH * HS))
    wpT = np.ascontiguousarray(wp.T)
    routerT = np.ascontiguousarray(rw.T)
    rbias_b = np.ascontiguousarray(np.broadcast_to(rb.reshape(1, NE), (128, NE)))
    bproj = np.ascontiguousarray(bp.reshape(1, C))
    shgT = np.ascontiguousarray(shg.T)
    shuT = np.ascontiguousarray(shu.T)
    shdT = np.ascontiguousarray(shd.T)
    ident = np.eye(128, dtype=ml_dtypes.bfloat16)
    maskneg = np.where(np.triu(np.ones((128, 128), np.float32)) > 0, 0.0, -1e9).astype(ml_dtypes.bfloat16)
    ones128 = np.ones((1, 128), dtype=ml_dtypes.bfloat16)
    iotam = np.ascontiguousarray(np.broadcast_to(
        (np.arange(NE, dtype=np.float32) - 16.0).reshape(1, NE), (128, NE)))

    shared = dict(wqkt=wqkT, wvt=wvT, wpt=wpT, bproj=bproj, routert=routerT,
                  rbias=rbias_b, up=up, gate=gate, down=down, shgt=shgT,
                  shut=shuT, shdt=shdT, ident=ident,
                  ones128=ones128, iotam=iotam, maskneg=maskneg)
    return x, shared


def kernel(**inputs):
    x, shared = host_prep(**inputs)
    if "nc" not in _cache:
        _cache["nc"] = build_nc()
    nc = _cache["nc"]
    in_maps = []
    for i in range(8):
        m = dict(shared)
        m["x"] = np.ascontiguousarray(x[i])
        in_maps.append(m)

    res = run_bass_kernel_spmd(nc, in_maps, core_ids=list(range(8)))
    out = np.stack([np.asarray(r["y"], dtype=np.float32) for r in res.results], axis=0)
    return out


# version 30
# speedup vs baseline: 1.2086x; 1.0041x over previous
import numpy as np
import ml_dtypes
import concourse.bass as bass
import concourse.mybir as mybir
from concourse.bass_utils import run_bass_kernel_spmd
from contextlib import ExitStack

F32 = mybir.dt.float32
BF16 = mybir.dt.bfloat16
AF = mybir.ActivationFunctionType
ALU = mybir.AluOpType
AX = mybir.AxisListType

T = 2048
C = 1024
NT = 16
NC = 8
NH = 16
HS = 64
NE = 16
ED = 512
NQS = 4

G_V = 0
G_P = 16640
G_WQK = 18176
G_COLS = 20224
G_SHH = 0

W_COLS = 16384

H_HBUF = 0
H_SCR = 4096
H_COLS = 6144

F_STAT = 0
F_EPS = 64
F_Z = 66
F_COMB = 322
F_T2 = 578
F_MSK = 706
F_IOTA = 770
F_COLS = 786

_STAGE = 9
_DUMPS = []


class _Stop(Exception):
    pass


ENGINES = ("sp", "pool", "act", "dve", "pe")
ENG_SEM = {"sp": None, "pool": None, "act": "act", "dve": "dve", "pe": "pe"}
SEM_NAMES = ("pe", "dve", "act", "dc", "dout",
             "dw", "dwp0", "dwp1", "dq0", "dq1", "ds0", "ds1", "ds2", "ds3") + tuple(
    f"dx{i}" for i in range(16))




class PB:

    def __init__(self):
        self.lists = {e: [] for e in ENGINES}
        self.cnt = {k: 0 for k in SEM_NAMES}

    def emit(self, eng, fn, waits=(), inc=None, drain=False):
        own = ENG_SEM[eng]
        ws = []
        for w in waits:
            if w is None:
                continue
            s, c = w
            if c <= 0:
                continue
            if s == own:
                continue
            assert c <= self.cnt[s], f"wait on future: {s} {c} > {self.cnt[s]}"
            ws.append((s, c))
        tok = None
        if inc:
            d = 16 if inc.startswith("d") else 1
            self.cnt[inc] += d
            tok = (inc, self.cnt[inc])
        self.lists[eng].append((ws, fn, inc, drain))
        return tok


def tmax(*toks):
    best = {}
    for t in toks:
        if t is None:
            continue
        if isinstance(t, list):
            for s, c in t:
                if best.get(s, -1) < c:
                    best[s] = c
            continue
        s, c = t
        if best.get(s, -1) < c:
            best[s] = c
    return [(s, c) for s, c in best.items()]


def build_nc():
    nc = bass.Bass("TRN2", target_bir_lowering=False)

    x_h = nc.dram_tensor("x", [T, C], F32, kind="ExternalInput")
    wqkT_h = nc.dram_tensor("wqkt", [C, NH * 128], F32, kind="ExternalInput")
    wvT_h = nc.dram_tensor("wvt", [C, NH * HS], F32, kind="ExternalInput")
    wpT_h = nc.dram_tensor("wpt", [C, C], F32, kind="ExternalInput")
    bproj_h = nc.dram_tensor("bproj", [1, C], F32, kind="ExternalInput")
    routerT_h = nc.dram_tensor("routert", [C, NE], F32, kind="ExternalInput")
    rbias_h = nc.dram_tensor("rbias", [128, NE], F32, kind="ExternalInput")
    up_h = nc.dram_tensor("up", [NE, C, ED], F32, kind="ExternalInput")
    gate_h = nc.dram_tensor("gate", [NE, C, ED], F32, kind="ExternalInput")
    down_h = nc.dram_tensor("down", [NE, ED, C], F32, kind="ExternalInput")
    shgT_h = nc.dram_tensor("shgt", [C, C], F32, kind="ExternalInput")
    shuT_h = nc.dram_tensor("shut", [C, C], F32, kind="ExternalInput")
    shdT_h = nc.dram_tensor("shdt", [C, C], F32, kind="ExternalInput")
    ident_h = nc.dram_tensor("ident", [128, 128], BF16, kind="ExternalInput")
    maskneg_h = nc.dram_tensor("maskneg", [128, 128], BF16, kind="ExternalInput")
    ones_h = nc.dram_tensor("ones128", [1, 128], BF16, kind="ExternalInput")
    iota_h = nc.dram_tensor("iotam", [128, NE], F32, kind="ExternalInput")
    out_h = nc.dram_tensor("y", [T, C], F32, kind="ExternalOutput")

    with ExitStack() as ctx:
        sb = lambda n, s, d: ctx.enter_context(nc.sbuf_tensor(n, s, d))
        xacc = sb("xacc", [128, NT * C], F32)
        xnf = sb("xnf", [128, NC * T], BF16)
        G = sb("G", [128, G_COLS], BF16)
        W = sb("W", [128, W_COLS], BF16)
        H = sb("H", [128, H_COLS], BF16)
        AP2 = sb("AP2", [128, 4096], BF16)
        WPS = sb("WPS", [128, 2048], BF16)
        LNS = AP2
        F = sb("F", [128, F_COLS], F32)
        Qb = sb("Qb", [65, T], BF16)
        Kb = sb("Kb", [65, T], BF16)
        ident = sb("identity", [128, 128], BF16)
        maskm = sb("maskneg_s", [128, 128], BF16)
        ones = sb("ones_r", [1, 128], BF16)
        bproj = sb("bproj_r", [1, C], BF16)
        routerT = sb("routerT", [128, NC * NE], BF16)
        rbias = sb("rbias_b", [128, NE], F32)
        rrowbf = sb("rrowbf", [1, 512], BF16)
        rbsb = sb("rbsb", [64, 512], BF16)

        ps = [ctx.enter_context(nc.psum_tensor(f"ps{i}", [128, 512], F32))
              for i in range(7)]
        pstr = ctx.enter_context(nc.psum_tensor("pstr", [128, 1024], BF16))

        sems = {k: ctx.enter_context(nc.semaphore(k)) for k in SEM_NAMES}
        block = ctx.enter_context(nc.Block())

        pb = PB()
        bank_free = {i: None for i in range(7)}
        bank_free["tr"] = None
        SB = [0, 1]
        PVB = [2, 3]
        _bg = [0]

        def bgbank():
            b = 4 + (_bg[0] % 3)
            _bg[0] += 1
            return b

        _DUMPS.clear()

        def stage_stop(k, dumps):
            if _STAGE != k:
                return
            for name, shape, dt, ap, waits in dumps:
                d = nc.dram_tensor(name, shape, dt, kind="ExternalOutput")
                pb.emit("sp", lambda e, d=d, ap=ap: e.dma_start(d[:, :], ap),
                        waits=waits, inc="dout")
                _DUMPS.append(name)
            fin = pb.cnt["dout"]
            pb.emit("sp", lambda e: e.nop(), waits=[("dout", fin)])
            raise _Stop()

        for tns, hsrc in ((ident, ident_h), (maskm, maskneg_h),
                          (ones, ones_h), (rbias, rbias_h)):
            pb.emit("sp", lambda e, t=tns, s_=hsrc: e.dma_start(t[:, :], s_[:, :]), inc="dc")
        pb.emit("sp", lambda e: e.dma_start(F[:, F_IOTA:F_IOTA + NE], iota_h[:, :]), inc="dc")
        pb.emit("pool", lambda e: e.dma_start(bproj[:, :], bproj_h[:, :]), inc="dw")
        t_consts = ("dc", pb.cnt["dc"])
        t_kone = pb.emit("dve", lambda e: e.memset(Kb[64:65, :], 1.0), inc="dve", drain=True)
        t_kone = pb.emit("dve", lambda e: e.memset(Qb[64:65, :], 0.0), inc="dve", drain=True)
        t_eps = pb.emit("dve", lambda e: e.memset(F[:, F_EPS:F_EPS + 1], 1e-5), inc="dve", drain=True)
        t_vones = pb.emit("dve", lambda e: e.memset(
            G[:, G_V:G_V + NT * 1040].rearrange("p (k h d) -> p k h d", h=NH, d=65)[:, :, :, 64:65],
            1.0), inc="dve", drain=True)
        t_ident = t_maskm = t_ones = t_rbias = t_iota = t_consts
        t_bproj = ("dw", pb.cnt["dw"])

        t_x = [None] * 16

        def emit_x(c):
            eng = "sp" if (c % 4) < 2 else "pool"
            t_x[c] = pb.emit(eng, lambda e, c=c: e.dma_start(
                xacc[:, :].rearrange("p (r n) -> p r n", n=C)[:, c:c + 1, :],
                x_h[:, :].rearrange("(r p) n -> p r n", p=128)[:, c:c + 1, :]),
                inc=f"dx{c}")

        t_wv = [None, None]

        def emit_wv(s):
            t_wv[s] = pb.emit("pool", lambda e, s=s: e.dma_start(
                W[:, s * 4096:(s + 1) * 4096].rearrange("p (a v) -> p a v", a=4),
                wvT_h[:, :].rearrange("(a p) v -> p a v", p=128)[:, 4 * s:4 * s + 4, :]),
                inc=f"ds{s}")

        t_wqk = [None] * NH

        def emit_wqk01(h):
            t_wqk[h] = pb.emit("pool", lambda e, h=h: e.dma_start(
                G[:, G_WQK + (h % 2) * 1024:G_WQK + (h % 2) * 1024 + 1024]
                .rearrange("p (a d) -> p a d", a=NC),
                wqkT_h[:, h * 128:(h + 1) * 128].rearrange("(a p) d -> p a d", p=128)),
                inc=f"dq{h % 2}")

        t_wps = [None, None]

        def emit_wps0(p):
            t_wps[p] = pb.emit("pool", lambda e, p=p: e.dma_start(
                WPS[:, p * 1024:(p + 1) * 1024],
                wpT_h[p * 128:(p + 1) * 128, :]), inc=f"dwp{p % 2}")

        t_wdma = {}
        slot_free = [None] * 4

        def emit_wdma(e_i, m):
            src = (up_h, gate_h, down_h)[m]
            s = (3 * e_i + m + 2) % 4
            inap = src[e_i, :, :].rearrange("(a p) j -> p a j", p=128)
            outap = W[:, s * 4096:(s + 1) * 4096].rearrange(
                "p (a j) -> p a j", a=NC if m < 2 else 4)
            t_wdma[(e_i, m)] = pb.emit("pool", lambda e, inap=inap, outap=outap: e.dma_start(
                outap, inap), waits=tmax(slot_free[s]), inc=f"ds{s}")

        for c in (0, 1, 4, 5, 8, 9, 12, 13):
            emit_x(c)
        emit_x(2); emit_x(3)
        emit_wv(0); emit_wv(1)
        emit_x(6); emit_x(7)
        emit_wqk01(0)
        pb.emit("pool", lambda e: e.dma_start(
            routerT[:, :].rearrange("p (a v) -> p a v", a=NC),
            routerT_h[:, :].rearrange("(a p) v -> p a v", p=128)), inc="dw")
        t_routerT = ("dw", pb.cnt["dw"])
        t_bproj = t_routerT
        emit_x(10); emit_x(11)
        emit_wps0(0); emit_wps0(1)
        emit_wdma(0, 0)
        emit_x(14); emit_x(15)
        emit_wdma(0, 1)
        emit_wqk01(1)

        bgq = []

        def fill(n):
            for _ in range(n):
                if not bgq:
                    return
                bgq.pop(0)()

        t_xnf = [None] * NT
        t_vready = [None] * NT
        t_comb = [None] * NT
        t_vlast = [None]

        def v_item(tt):
            def go():
                toks = []
                for half in range(2):
                    b = bgbank()
                    def v_fn(e, tt=tt, half=half, b=b):
                        last = None
                        for cc in range(NC):
                            last = e.matmul(
                                ps[b][:, 0:512],
                                xnf[:, cc * T + tt * 128:cc * T + (tt + 1) * 128],
                                W[:, cc * C + half * 512:cc * C + half * 512 + 512],
                                start=(cc == 0), stop=(cc == NC - 1))
                        return last
                    t_mm = pb.emit("pe", v_fn,
                                   waits=tmax(t_xnf[tt], t_wv[0], t_wv[1], bank_free[b]),
                                   inc="pe")
                    t_ev = pb.emit("dve", lambda e, tt=tt, half=half, b=b: e.tensor_copy(
                        G[:, G_V + tt * 1040:G_V + (tt + 1) * 1040]
                        .rearrange("p (h d) -> p h d", d=65)[:, half * 8:half * 8 + 8, 0:64],
                        ps[b][:, 0:512].rearrange("p (h d) -> p h d", d=64)),
                        waits=[t_mm], inc="dve")
                    bank_free[b] = t_ev
                    toks.append(t_ev)
                    t_vlast[0] = t_mm
                t_vready[tt] = toks[-1]
                if tt == NT - 1:
                    slot_free[0] = t_vlast[0]
                    slot_free[1] = t_vlast[0]
                    emit_wdma(0, 2)
                    emit_wdma(1, 0)
            return go

        def router_item(tt):
            z = F[:, F_Z + tt * NE:F_Z + (tt + 1) * NE]

            def go():
                b = bgbank()
                def r_fn(e, tt=tt, b=b):
                    last = None
                    for cc in range(NC):
                        last = e.matmul(
                            ps[b][:, 0:NE],
                            xnf[:, cc * T + tt * 128:cc * T + (tt + 1) * 128],
                            routerT[:, cc * NE:(cc + 1) * NE],
                            start=(cc == 0), stop=(cc == NC - 1))
                    return last
                t_rmm = pb.emit("pe", r_fn,
                                waits=tmax(t_xnf[tt], t_routerT, bank_free[b]), inc="pe")
                t_z = pb.emit("dve", lambda e, z=z, b=b: e.tensor_tensor(
                    z, ps[b][:, 0:NE], rbias[:, :], op=ALU.add),
                    waits=tmax(t_rmm, t_rbias, t_iota), inc="dve", drain=True)
                bank_free[b] = t_z
            return go

        def emit_router_top2():
            z3 = F[:, F_Z:F_Z + 256].rearrange("p (t e) -> p t e", e=NE)
            cb3 = F[:, F_COMB:F_COMB + 256].rearrange("p (t e) -> p t e", e=NE)
            m1 = F[:, F_T2 + 0:F_T2 + 16]
            m2 = F[:, F_T2 + 16:F_T2 + 32]
            e2 = F[:, F_T2 + 32:F_T2 + 48]
            s1v = F[:, F_T2 + 48:F_T2 + 64]
            s2v = F[:, F_T2 + 64:F_T2 + 80]
            den = F[:, F_T2 + 80:F_T2 + 96]
            i1 = F[:, F_T2 + 96:F_T2 + 112]
            i2 = F[:, F_T2 + 112:F_T2 + 128]
            msk1 = AP2[:, 0:256].rearrange("p (t e) -> p t e", e=NE)
            msk2 = AP2[:, 256:512].rearrange("p (t e) -> p t e", e=NE)
            tmpb = AP2[:, 512:768].rearrange("p (t e) -> p t e", e=NE)
            iotb = F[:, F_IOTA:F_IOTA + NE].rearrange("p e -> p () e").broadcast_to([128, 16, NE])
            bt = lambda a: a.rearrange("p t -> p t ()").broadcast_to([128, 16, NE])
            pb.emit("dve", lambda e: e.tensor_reduce(m1, z3, axis=AX.X, op=ALU.max),
                    inc="dve", drain=True)
            pb.emit("dve", lambda e: e.tensor_tensor(msk1, z3, bt(m1), op=ALU.is_equal),
                    waits=tmax(t_tr[NT - 2]), inc="dve", drain=True)
            pb.emit("dve", lambda e: e.tensor_tensor(tmpb, msk1, iotb, op=ALU.mult),
                    inc="dve", drain=True)
            pb.emit("dve", lambda e: e.tensor_reduce(i1, tmpb, axis=AX.X, op=ALU.min),
                    inc="dve", drain=True)
            pb.emit("dve", lambda e: e.tensor_tensor(msk1, iotb, bt(i1), op=ALU.is_equal),
                    inc="dve", drain=True)
            pb.emit("dve", lambda e: e.scalar_tensor_tensor(
                z3, msk1, -1e30, z3, op0=ALU.mult, op1=ALU.add), inc="dve", drain=True)
            pb.emit("dve", lambda e: e.tensor_reduce(m2, z3, axis=AX.X, op=ALU.max),
                    inc="dve", drain=True)
            pb.emit("dve", lambda e: e.tensor_tensor(msk2, z3, bt(m2), op=ALU.is_equal),
                    inc="dve", drain=True)
            pb.emit("dve", lambda e: e.tensor_tensor(tmpb, msk2, iotb, op=ALU.mult),
                    inc="dve", drain=True)
            pb.emit("dve", lambda e: e.tensor_reduce(i2, tmpb, axis=AX.X, op=ALU.min),
                    inc="dve", drain=True)
            pb.emit("dve", lambda e: e.tensor_tensor(msk2, iotb, bt(i2), op=ALU.is_equal),
                    inc="dve", drain=True)
            t_sub = pb.emit("dve", lambda e: e.tensor_tensor(e2, m2, m1, op=ALU.subtract),
                            inc="dve", drain=True)
            t_exp = pb.emit("act", lambda e: e.activation(e2, e2, AF.Exp),
                            waits=[t_sub], inc="act", drain=True)
            pb.emit("dve", lambda e: e.tensor_scalar(
                den, e2, 1.0, None, op0=ALU.add), waits=[t_exp], inc="dve", drain=True)
            pb.emit("dve", lambda e: e.reciprocal(s1v, den), inc="dve", drain=True)
            pb.emit("dve", lambda e: e.tensor_scalar(
                s2v, s1v, -1.0, 1.0, op0=ALU.mult, op1=ALU.add), inc="dve", drain=True)
            pb.emit("dve", lambda e: e.tensor_tensor(cb3, msk1, bt(s1v), op=ALU.mult),
                    inc="dve", drain=True)
            pb.emit("dve", lambda e: e.tensor_tensor(z3, msk2, bt(s2v), op=ALU.mult),
                    inc="dve", drain=True)
            tok = pb.emit("dve", lambda e: e.tensor_tensor(cb3, cb3, z3, op=ALU.add),
                          inc="dve", drain=True)
            for tt in range(NT):
                t_comb[tt] = tok
            return tok

        h_done = {}
        t_hbuf_rd = [None, None]
        t_scr_rd = [None, None]
        _scr = [0]

        ustate = {}

        def u_item(e_i, ts, jc):
            def go():
                su = (3 * e_i + 0 + 2) % 4
                bu = bgbank()
                par = ts % 2
                def u_fn(e, jc=jc, ts=ts, bu=bu, base=su * 4096):
                    last = None
                    for cc in range(NC):
                        last = e.matmul(
                            ps[bu][:, 0:512],
                            W[:, base + cc * 512 + jc * 128:base + cc * 512 + (jc + 1) * 128],
                            xnf[:, cc * T + ts * 512:cc * T + ts * 512 + 512],
                            start=(cc == 0), stop=(cc == NC - 1))
                    return last
                t_umm = pb.emit("pe", u_fn,
                                waits=tmax(t_wdma[(e_i, 0)], bank_free[bu]), inc="pe")
                if (ts * 4 + jc) % 2 == 0:
                    t_ue = pb.emit("act", lambda e, bu=bu, par=par, jc=jc: e.activation(
                        H[:, H_HBUF + par * 2048 + jc * 512:H_HBUF + par * 2048 + (jc + 1) * 512],
                        ps[bu][:, 0:512], AF.Identity),
                        waits=tmax(t_umm, t_hbuf_rd[par]), inc="act")
                else:
                    t_ue = pb.emit("dve", lambda e, bu=bu, par=par, jc=jc: e.tensor_copy(
                        H[:, H_HBUF + par * 2048 + jc * 512:H_HBUF + par * 2048 + (jc + 1) * 512],
                        ps[bu][:, 0:512]),
                        waits=tmax(t_umm, t_hbuf_rd[par]), inc="dve")
                bank_free[bu] = t_ue
                ustate[(e_i, ts, jc)] = t_ue
                if ts == 3 and jc == 3:
                    slot_free[su] = t_umm
                    if e_i + 1 == NE:
                        t_shg[0] = pb.emit("pool", lambda e: e.dma_start(
                            W[:, 8192:16384].rearrange("p (a v) -> p a v", a=NC),
                            shgT_h[:, :].rearrange("(a p) v -> p a v", p=128)),
                            waits=tmax(slot_free[2], slot_free[3]), inc="ds2")
            return go

        def g_item(e_i, ts, jc):
            def go():
                sg = (3 * e_i + 1 + 2) % 4
                bg2 = bgbank()
                par = ts % 2
                spar = _scr[0] % 2
                _scr[0] += 1
                t_ue = ustate.pop((e_i, ts, jc))
                def g_fn(e, jc=jc, ts=ts, bg2=bg2, base=sg * 4096):
                    last = None
                    for cc in range(NC):
                        last = e.matmul(
                            ps[bg2][:, 0:512],
                            W[:, base + cc * 512 + jc * 128:base + cc * 512 + (jc + 1) * 128],
                            xnf[:, cc * T + ts * 512:cc * T + ts * 512 + 512],
                            start=(cc == 0), stop=(cc == NC - 1))
                    return last
                t_gmm = pb.emit("pe", g_fn,
                                waits=tmax(t_wdma[(e_i, 1)], bank_free[bg2]), inc="pe")
                scr = H_SCR + spar * 1024
                t_th = pb.emit("act", lambda e, bg2=bg2, scr=scr: e.activation(
                    H[:, scr:scr + 512], ps[bg2][:, 0:512], AF.Tanh, scale=0.5),
                    waits=tmax(t_gmm, t_scr_rd[spar]), inc="act")
                t_2s = pb.emit("dve", lambda e, bg2=bg2, scr=scr: e.scalar_tensor_tensor(
                    H[:, scr + 512:scr + 1024], H[:, scr:scr + 512], 1.0,
                    ps[bg2][:, 0:512], op0=ALU.add, op1=ALU.mult),
                    waits=tmax(t_th), inc="dve")
                bank_free[bg2] = t_2s
                t_h = pb.emit("dve", lambda e, par=par, jc=jc, scr=scr: e.scalar_tensor_tensor(
                    H[:, H_HBUF + par * 2048 + jc * 512:H_HBUF + par * 2048 + (jc + 1) * 512],
                    H[:, scr + 512:scr + 1024], 0.5,
                    H[:, H_HBUF + par * 2048 + jc * 512:H_HBUF + par * 2048 + (jc + 1) * 512],
                    op0=ALU.mult, op1=ALU.mult),
                    waits=tmax(t_ue), inc="dve")
                t_scr_rd[spar] = t_h
                if jc == 3:
                    h_done[(e_i, ts)] = t_h
                if ts == 3 and jc == 3:
                    slot_free[sg] = t_gmm
                    if e_i + 1 < NE:
                        emit_wdma(e_i + 1, 1)
            return go

        def down_item(e_i, ts, tti, dh):
            def go():
                sd = (3 * e_i + 2 + 2) % 4
                par = ts % 2
                tt = ts * 4 + tti
                assert t_comb[tt] is not None and h_done.get((e_i, ts)) is not None
                cbcol = F[:, F_COMB + tt * NE + e_i:F_COMB + tt * NE + e_i + 1]
                b = bgbank()
                def d_fn(e, par=par, tti=tti, dh=dh, b=b, base=sd * 4096):
                    last = None
                    for jc in range(4):
                        last = e.matmul(
                            ps[b][:, 0:512],
                            H[:, H_HBUF + par * 2048 + jc * 512 + tti * 128:
                               H_HBUF + par * 2048 + jc * 512 + (tti + 1) * 128],
                            W[:, base + jc * 1024 + dh * 512:base + jc * 1024 + dh * 512 + 512],
                            start=(jc == 0), stop=(jc == 3))
                    return last
                t_dmm = pb.emit("pe", d_fn,
                                waits=tmax(h_done[(e_i, ts)], t_wdma[(e_i, 2)],
                                           bank_free[b]), inc="pe")
                t_hbuf_rd[par] = t_dmm
                t_de = pb.emit("dve", lambda e, tt=tt, dh=dh, b=b, cbcol=cbcol: e.scalar_tensor_tensor(
                    xacc[:, tt * C + dh * 512:tt * C + dh * 512 + 512],
                    ps[b][:, 0:512], cbcol,
                    xacc[:, tt * C + dh * 512:tt * C + dh * 512 + 512],
                    op0=ALU.mult, op1=ALU.add),
                    waits=tmax(t_dmm, t_comb[tt]), inc="dve")
                bank_free[b] = t_de
                if ts == 3 and tti == 3 and dh == 1:
                    slot_free[sd] = ("pe", pb.cnt["pe"])
                    if e_i + 1 < NE:
                        emit_wdma(e_i + 1, 2)
                    if e_i + 2 < NE:
                        emit_wdma(e_i + 2, 0)
                    if e_i + 1 == NE:
                        t_shu[0] = pb.emit("pool", lambda e: e.dma_start(
                            W[:, 0:8192].rearrange("p (a v) -> p a v", a=NC),
                            shuT_h[:, :].rearrange("(a p) v -> p a v", p=128)),
                            waits=tmax(slot_free[0], slot_free[1]), inc="ds0")
            return go

        t_shg = [None]
        t_shu = [None]
        t_shd = [None]

        try:
            t_tr = [None] * NT
            t_qk_evac = {}
            wqk_reader = [None, None]
            t_nmrl = [None] * NT

            def emit_stats(tt):
                xs = xacc[:, tt * C:(tt + 1) * C]
                st = F_STAT + tt * 4
                par = tt % 2
                lns = LNS[:, par * 1024:(par + 1) * 1024]
                pb.emit("dve", lambda e, xs=xs, st=st: e.reduce_sum(
                    F[:, st:st + 1], xs, axis=AX.X), waits=[t_x[tt]], inc="dve", drain=True)
                t_ssq = pb.emit("act", lambda e, xs=xs, st=st, lns=lns: e.activation(
                    lns, xs, AF.Square, accum_out=F[:, st + 3:st + 4]),
                    waits=tmax(t_x[tt], t_tr[tt - 2] if tt >= 2 else None),
                    inc="act", drain=True)
                pb.emit("dve", lambda e, st=st: e.tensor_scalar(
                    F[:, st:st + 1], F[:, st:st + 1], 1.0 / C, None, op0=ALU.mult), inc="dve", drain=True)
                pb.emit("dve", lambda e, st=st: e.tensor_scalar(
                    F[:, st + 1:st + 2], F[:, st + 3:st + 4], 1.0 / C, None, op0=ALU.mult),
                    waits=[t_ssq], inc="dve", drain=True)
                pb.emit("dve", lambda e, st=st: e.tensor_tensor(
                    F[:, st + 2:st + 3], F[:, st:st + 1], F[:, st:st + 1], op=ALU.mult), inc="dve", drain=True)
                t_var = pb.emit("dve", lambda e, st=st: e.tensor_tensor(
                    F[:, st + 1:st + 2], F[:, st + 1:st + 2], F[:, st + 2:st + 3],
                    op=ALU.subtract), inc="dve", drain=True)
                pb.emit("act", lambda e, st=st: e.activation(
                    F[:, st + 1:st + 2], F[:, st + 1:st + 2], AF.Ln,
                    bias=F[:, F_EPS:F_EPS + 1], scale=1.0),
                    waits=tmax(t_eps, t_var), inc="act", drain=True)
                t_rstd = pb.emit("act", lambda e, st=st: e.activation(
                    F[:, st + 1:st + 2], F[:, st + 1:st + 2], AF.Exp, scale=-0.5),
                    inc="act", drain=True)
                t_nmrl[tt] = pb.emit("dve", lambda e, st=st: e.tensor_scalar(
                    F[:, st + 2:st + 3], F[:, st:st + 1], F[:, st + 1:st + 2], -1.0,
                    op0=ALU.mult, op1=ALU.mult), waits=[t_rstd], inc="dve", drain=True)

            emit_stats(0)
            for tt in range(NT):
                xs = xacc[:, tt * C:(tt + 1) * C]
                st = F_STAT + tt * 4
                par = tt % 2
                lns = LNS[:, par * 1024:(par + 1) * 1024]
                t_apply = pb.emit("act", lambda e, xs=xs, st=st, lns=lns: e.activation(
                    lns, xs, AF.Identity,
                    bias=F[:, st + 2:st + 3], scale=F[:, st + 1:st + 2]),
                    waits=[t_nmrl[tt]], inc="act", drain=True)
                def tr_fn(e, lns=lns):
                    last = None
                    for cc in range(NC):
                        last = e.transpose(pstr[:, cc * 128:(cc + 1) * 128],
                                           lns[:, cc * 128:(cc + 1) * 128],
                                           ident[:, :])
                    return last
                t_tr[tt] = pb.emit("pe", tr_fn,
                                   waits=tmax(t_apply, t_ident, bank_free["tr"]), inc="pe")
                t_xnf[tt] = pb.emit("dve", lambda e, tt=tt: e.tensor_copy(
                    xnf[:, :].rearrange("p (a t) -> p a t", a=NC)[:, :, tt * 128:(tt + 1) * 128],
                    pstr[:, :].rearrange("p (a d) -> p a d", a=NC)),
                    waits=[t_tr[tt]], inc="dve")
                bank_free["tr"] = t_xnf[tt]
                if tt + 1 < NT:
                    emit_stats(tt + 1)
                bgq.append(v_item(tt))
                bgq.append(router_item(tt))
                if tt % 4 == 3:
                    ts_ = tt // 4
                    b = bgbank()
                    def qk_fn(e, ts_=ts_, b=b):
                        last = None
                        for cc in range(NC):
                            last = e.matmul(
                                ps[b][:, 0:512],
                                G[:, G_WQK + cc * 128:G_WQK + (cc + 1) * 128],
                                xnf[:, cc * T + ts_ * 512:cc * T + ts_ * 512 + 512],
                                start=(cc == 0), stop=(cc == NC - 1))
                        return last
                    t_mm = pb.emit("pe", qk_fn,
                                   waits=tmax(t_wqk[0], t_xnf[tt], bank_free[b]), inc="pe")
                    if ts_ == 3:
                        wqk_reader[0] = t_mm
                        t_wqk[2] = pb.emit("pool", lambda e: e.dma_start(
                            G[:, G_WQK:G_WQK + 1024].rearrange("p (a d) -> p a d", a=NC),
                            wqkT_h[:, 2 * 128:3 * 128].rearrange("(a p) d -> p a d", p=128)),
                            waits=tmax(wqk_reader[0]), inc="dq0")
                    pb.emit("act", lambda e, ts_=ts_, b=b: e.activation(
                        Qb[0:64, ts_ * 512:(ts_ + 1) * 512], ps[b][0:64, 0:512],
                        AF.Identity), waits=[t_mm], inc="act")
                    t_qk_evac[ts_] = pb.emit("act", lambda e, ts_=ts_, b=b: e.activation(
                        Kb[0:64, ts_ * 512:(ts_ + 1) * 512], ps[b][64:128, 0:512],
                        AF.Identity), inc="act")
                    bank_free[b] = t_qk_evac[ts_]
                if tt >= 2:
                    fill(3)

            stage_stop(1, [
                ("d_xnf", [128, NC * T], BF16, xnf[:, :], tmax(t_xnf[NT - 1])),
            ])

            while bgq:
                bgq.pop(0)()
            pend = None
            for e_i in range(NE):
                for ts in range(4):
                    for jc in range(4):
                        bgq.append(u_item(e_i, ts, jc))
                        bgq.append(g_item(e_i, ts, jc))
                    if pend is not None:
                        pe_, pts = pend
                        for tti in range(4):
                            for dh in range(2):
                                bgq.append(down_item(pe_, pts, tti, dh))
                    pend = (e_i, ts)
            for tti in range(4):
                for dh in range(2):
                    bgq.append(down_item(NE - 1, 3, tti, dh))

            t_gpfree = emit_router_top2()

            NPP = 3
            t_rb_used = None
            t_s_last = None
            t_amult = [[], []]
            proj_free = [None, None]

            def emit_qk(h):
                for ts_ in range(4):
                    b = bgbank()
                    def qk_fn(e, ts_=ts_, b=b, h=h):
                        last = None
                        for cc in range(NC):
                            last = e.matmul(
                                ps[b][:, 0:512],
                                G[:, G_WQK + (h % 2) * 1024 + cc * 128:
                                   G_WQK + (h % 2) * 1024 + (cc + 1) * 128],
                                xnf[:, cc * T + ts_ * 512:cc * T + ts_ * 512 + 512],
                                start=(cc == 0), stop=(cc == NC - 1))
                        return last
                    t_mm = pb.emit("pe", qk_fn,
                                   waits=tmax(t_wqk[h], t_xnf[NT - 1], bank_free[b]), inc="pe")
                    if ts_ == 3:
                        wqk_reader[h % 2] = t_mm
                        if h + 2 < NH:
                            par2 = (h + 2) % 2
                            t_wqk[h + 2] = pb.emit("pool", lambda e, h2=h + 2, par2=par2: e.dma_start(
                                G[:, G_WQK + par2 * 1024:G_WQK + par2 * 1024 + 1024]
                                .rearrange("p (a d) -> p a d", a=NC),
                                wqkT_h[:, h2 * 128:(h2 + 1) * 128].rearrange("(a p) d -> p a d", p=128)),
                                waits=tmax(wqk_reader[par2]), inc=f"dq{par2}")
                    pb.emit("act", lambda e, ts_=ts_, b=b: e.activation(
                        Qb[0:64, ts_ * 512:(ts_ + 1) * 512], ps[b][0:64, 0:512],
                        AF.Identity), waits=tmax(t_mm, t_s_last), inc="act")
                    t_qk_evac[ts_] = pb.emit("act", lambda e, ts_=ts_, b=b: e.activation(
                        Kb[0:64, ts_ * 512:(ts_ + 1) * 512], ps[b][64:128, 0:512],
                        AF.Identity), inc="act")
                    bank_free[b] = t_qk_evac[ts_]
                    fill(1)

            def emit_proj(p):
                par = p % 2
                t_mm = None
                for tt in range(NT):
                    for dh in range(2):
                        b = bgbank()
                        def pj_fn(e, tt=tt, dh=dh, b=b, par=par, p=p):
                            m = e.matmul(ps[b][:, 0:512],
                                         AP2[:, par * 2048 + tt * 128:par * 2048 + (tt + 1) * 128],
                                         WPS[:, par * 1024 + dh * 512:par * 1024 + dh * 512 + 512],
                                         start=True, stop=(p != 0))
                            if p == 0:
                                m = e.matmul(ps[b][:, 0:512], ones[0:1, :],
                                             bproj[0:1, dh * 512:dh * 512 + 512],
                                             start=False, stop=True)
                            return m
                        t_mm = pb.emit("pe", pj_fn,
                                       waits=tmax(t_amult[par], t_wps[par], t_ones,
                                                  t_bproj, bank_free[b]), inc="pe")
                        t_pe = pb.emit("dve", lambda e, tt=tt, dh=dh, b=b: e.tensor_tensor(
                            xacc[:, tt * C + dh * 512:tt * C + dh * 512 + 512],
                            ps[b][:, 0:512],
                            xacc[:, tt * C + dh * 512:tt * C + dh * 512 + 512],
                            op=ALU.add), waits=[t_mm], inc="dve")
                        bank_free[b] = t_pe
                proj_free[par] = t_mm
                t_amult[par] = []
                if p + 2 < 8:
                    t_wps[par] = pb.emit("pool", lambda e, p2=p + 2, par=par: e.dma_start(
                        WPS[:, par * 1024:(par + 1) * 1024],
                        wpT_h[p2 * 128:(p2 + 1) * 128, :]),
                        waits=[t_mm], inc=f"dwp{par}")

            for h in range(NH):
                if h > 0:
                    emit_qk(h)
                else:
                    pass
                it = 0
                t_pv_stop = [None] * NQS
                p_used = [None] * NPP
                t_exp_of = [None] * NPP
                pend_q = []
                def push_pending(fn):
                    pend_q.append(fn)
                    if len(pend_q) > 3:
                        pend_q.pop(0)()
                def flush_pending():
                    while pend_q:
                        pend_q.pop(0)()
                for qs in range(NQS):
                    for kt in range(4 * qs + 4):
                        b = SB[it % 2]
                        j = kt - 4 * qs
                        if j > 0:
                            off, masked = 128 * j, True
                        elif j == 0:
                            off, masked = 0, True
                        else:
                            off, masked = 0, False
                        N = 512 - off
                        pp = it % NPP
                        def s_fn(e, b=b, kt=kt, qs=qs, off=off, N=N, masked=masked):
                            m = e.matmul(ps[b][:, 0:N],
                                         Kb[0:65, kt * 128:(kt + 1) * 128],
                                         Qb[0:65, qs * 512 + off:qs * 512 + 512],
                                         start=True, stop=(not masked))
                            if masked:
                                m = e.matmul(ps[b][:, 0:128], ident[:, :], maskm[:, :],
                                             start=False, stop=True)
                            return m
                        t_s = pb.emit("pe", s_fn,
                            waits=tmax(bank_free[b], t_qk_evac[qs], t_qk_evac[kt // 4],
                                       t_kone, t_maskm, t_ident),
                            inc="pe")
                        t_s_last = t_s

                        def pv_back(kt=kt, h=h, pp=pp, off=off, N=N, qs=qs):
                            pvb = PVB[qs % 2]
                            assert t_vready[kt] is not None
                            w = tmax(t_exp_of[pp], t_vready[kt], t_vones)
                            if kt == 0:
                                w = tmax(w, bank_free[pvb])
                            t_pv = pb.emit("pe", lambda e, pvb=pvb, kt=kt, h=h, pp=pp, off=off, N=N: e.matmul(
                                ps[pvb][0:65, off:512],
                                G[:, G_V + kt * 1040 + h * 65:G_V + kt * 1040 + h * 65 + 65],
                                G[:, G_P + pp * 512:G_P + pp * 512 + N],
                                start=(kt == 0), stop=(kt == 4 * qs + 3)),
                                waits=w, inc="pe")
                            p_used[pp] = t_pv
                            if kt == 4 * qs + 3:
                                t_pv_stop[qs] = t_pv
                        push_pending(pv_back)
                        t_e = pb.emit("act", lambda e, b=b, N=N, pp=pp: e.activation(
                            G[:, G_P + pp * 512:G_P + pp * 512 + N],
                            ps[b][:, 0:N], AF.Exp, scale=0.125),
                            waits=tmax(t_s, p_used[pp]), inc="act", drain=True)
                        bank_free[b] = t_e
                        t_exp_of[pp] = t_e
                        fill(1)
                        it += 1
                    flush_pending()
                    pvb = PVB[qs % 2]
                    t_cp = pb.emit("act", lambda e, pvb=pvb: e.activation(
                        rrowbf[0:1, 0:512], ps[pvb][64:65, 0:512], AF.Identity),
                        waits=tmax(t_pv_stop[qs], t_rb_used), inc="act", drain=True)
                    fill(2)
                    rbb = bgbank()
                    t_bc = pb.emit("pe", lambda e, rbb=rbb: e.matmul(
                        ps[rbb][0:64, 0:512], ones[0:1, 0:64], rrowbf[0:1, 0:512],
                        start=True, stop=True),
                        waits=tmax(t_cp, t_ones, bank_free[rbb]), inc="pe")
                    t_rb_used = t_bc
                    def ri_fn(e, rbb=rbb):
                        with nc.allow_low_precision(reason="rowsum rinv is consumed as a bf16 elementwise operand"):
                            return e.reciprocal(rbsb[:, :], ps[rbb][0:64, 0:512])
                    t_rbe = pb.emit("dve", ri_fn, waits=[t_bc], inc="dve")
                    bank_free[rbb] = t_rbe
                    par = (h // 2) % 2
                    r0 = (h % 2) * 64
                    t_am = pb.emit("dve", lambda e, pvb=pvb, par=par, r0=r0, qs=qs: e.tensor_tensor(
                        AP2[r0:r0 + 64, par * 2048 + qs * 512:par * 2048 + qs * 512 + 512],
                        ps[pvb][0:64, 0:512], rbsb[:, :], op=ALU.mult),
                        waits=tmax(t_rbe, proj_free[par]), inc="dve")
                    bank_free[pvb] = t_am
                    t_amult[par].append(t_am)
                    fill(1)
                if h % 2 == 1 and h // 2 >= 1:
                    emit_proj(h // 2 - 1)
                fill(1)
            emit_proj(7)
            pe_attn_done = ("pe", pb.cnt["pe"])

            stage_stop(3, [
                ("d_xacc", [128, NT * C], F32, xacc[:, :], tmax(proj_free[1])),
            ])

            while bgq:
                bgq.pop(0)()

            t_gmm_last = None
            t_hid = {}
            for jc in range(NC):
                for ts in range(4):
                    b = bgbank()
                    def sg_fn(e, jc=jc, ts=ts, b=b):
                        last = None
                        for cc in range(NC):
                            last = e.matmul(
                                ps[b][:, 0:512],
                                W[:, 8192 + cc * C + jc * 128:8192 + cc * C + (jc + 1) * 128],
                                xnf[:, cc * T + ts * 512:cc * T + ts * 512 + 512],
                                start=(cc == 0), stop=(cc == NC - 1))
                        return last
                    t_gmm = pb.emit("pe", sg_fn,
                                    waits=tmax(t_shg[0], bank_free[b], pe_attn_done), inc="pe")
                    t_gmm_last = t_gmm
                    spar = _scr[0] % 2
                    _scr[0] += 1
                    scr = H_SCR + spar * 1024
                    t_th = pb.emit("act", lambda e, b=b, scr=scr: e.activation(
                        H[:, scr:scr + 512], ps[b][:, 0:512], AF.Tanh, scale=0.5),
                        waits=tmax(t_gmm, t_scr_rd[spar]), inc="act")
                    t_2s = pb.emit("dve", lambda e, b=b, scr=scr, jc=jc, ts=ts: e.scalar_tensor_tensor(
                        G[:, G_SHH + jc * 2048 + ts * 512:G_SHH + jc * 2048 + (ts + 1) * 512],
                        H[:, scr:scr + 512], 1.0, ps[b][:, 0:512],
                        op0=ALU.add, op1=ALU.mult),
                        waits=tmax(t_th, pe_attn_done), inc="dve")
                    bank_free[b] = t_2s
                    t_scr_rd[spar] = t_2s
                    t_hid[(jc, ts)] = t_2s
            t_shd[0] = pb.emit("pool", lambda e: e.dma_start(
                W[:, 8192:16384].rearrange("p (a v) -> p a v", a=NC),
                shdT_h[:, :].rearrange("(a p) v -> p a v", p=128)),
                waits=tmax(t_gmm_last, t_shg[0]), inc="ds3")
            t_hid_last = None
            for jc in range(NC):
                for ts in range(4):
                    b = bgbank()
                    def su_fn(e, jc=jc, ts=ts, b=b):
                        last = None
                        for cc in range(NC):
                            last = e.matmul(
                                ps[b][:, 0:512],
                                W[:, cc * C + jc * 128:cc * C + (jc + 1) * 128],
                                xnf[:, cc * T + ts * 512:cc * T + ts * 512 + 512],
                                start=(cc == 0), stop=(cc == NC - 1))
                        return last
                    t_umm = pb.emit("pe", su_fn,
                                    waits=tmax(t_shu[0], bank_free[b]), inc="pe")
                    t_hd = pb.emit("dve", lambda e, b=b, jc=jc, ts=ts: e.scalar_tensor_tensor(
                        G[:, G_SHH + jc * 2048 + ts * 512:G_SHH + jc * 2048 + (ts + 1) * 512],
                        G[:, G_SHH + jc * 2048 + ts * 512:G_SHH + jc * 2048 + (ts + 1) * 512],
                        0.5, ps[b][:, 0:512], op0=ALU.mult, op1=ALU.mult),
                        waits=tmax(t_umm, t_hid[(jc, ts)]), inc="dve")
                    bank_free[b] = t_hd
                    t_hid_last = t_hd
            t_out_ready = [None] * NT
            for tt in range(NT):
                for dh in range(2):
                    b = bgbank()
                    def sd_fn(e, tt=tt, dh=dh, b=b):
                        last = None
                        for jc in range(NC):
                            last = e.matmul(
                                ps[b][:, 0:512],
                                G[:, G_SHH + jc * 2048 + tt * 128:G_SHH + jc * 2048 + (tt + 1) * 128],
                                W[:, 8192 + jc * 1024 + dh * 512:8192 + jc * 1024 + dh * 512 + 512],
                                start=(jc == 0), stop=(jc == NC - 1))
                        return last
                    t_dmm = pb.emit("pe", sd_fn,
                                    waits=tmax(t_hid_last, t_shd[0], bank_free[b]), inc="pe")
                    t_de = pb.emit("dve", lambda e, tt=tt, dh=dh, b=b: e.tensor_tensor(
                        xacc[:, tt * C + dh * 512:tt * C + dh * 512 + 512],
                        ps[b][:, 0:512],
                        xacc[:, tt * C + dh * 512:tt * C + dh * 512 + 512],
                        op=ALU.add), waits=[t_dmm], inc="dve", drain=True)
                    bank_free[b] = t_de
                    t_out_ready[tt] = t_de

            for g in range(8):
                eng = "sp" if g % 2 == 0 else "pool"
                pb.emit(eng, lambda e, g=g: e.dma_start(
                    out_h[:, :].rearrange("(r p) n -> p r n", p=128)[:, 2 * g:2 * g + 2, :],
                    xacc[:, :].rearrange("p (r n) -> p r n", n=C)[:, 2 * g:2 * g + 2, :]),
                    waits=tmax(t_out_ready[2 * g + 1]), inc="dout")
            final_dout = pb.cnt["dout"]
            pb.emit("sp", lambda e: e.nop(), waits=[("dout", final_dout)])

        except _Stop:
            pass

        def body(eng_name):
            waited = {}
            def run(eng):
                for ws, fn, inc, drain in pb.lists[eng_name]:
                    for s, c in ws:
                        if waited.get(s, 0) < c:
                            eng.wait_ge(sems[s], c)
                            waited[s] = c
                    ins = fn(eng)
                    if drain:
                        ins = eng.drain()
                    if inc:
                        ins.then_inc(sems[inc], 16 if inc.startswith("d") else 1)
            return run

        block.sync(body("sp"))
        block.gpsimd(body("pool"))
        block.scalar(body("act"))
        block.vector(body("dve"))
        block.tensor(body("pe"))

    return nc


_cache = {}


def host_prep(L0_attn_x, L0_attn_w_qkv, L0_attn_w_proj, L0_attn_b_proj,
              L0_ffn_router_w, L0_ffn_router_bias, L0_ffn_up_proj, L0_ffn_gate_proj,
              L0_ffn_down_proj, L0_ffn_sh_gate, L0_ffn_sh_up, L0_ffn_sh_down):
    x = np.asarray(L0_attn_x, dtype=np.float32)
    wq = np.asarray(L0_attn_w_qkv, dtype=np.float32)
    wp = np.asarray(L0_attn_w_proj, dtype=np.float32)
    bp = np.asarray(L0_attn_b_proj, dtype=np.float32)
    rw = np.asarray(L0_ffn_router_w, dtype=np.float32)
    rb = np.asarray(L0_ffn_router_bias, dtype=np.float32)
    up = np.ascontiguousarray(np.asarray(L0_ffn_up_proj, dtype=np.float32))
    gate = np.ascontiguousarray(np.asarray(L0_ffn_gate_proj, dtype=np.float32))
    down = np.ascontiguousarray(np.asarray(L0_ffn_down_proj, dtype=np.float32))
    shg = np.asarray(L0_ffn_sh_gate, dtype=np.float32)
    shu = np.asarray(L0_ffn_sh_up, dtype=np.float32)
    shd = np.asarray(L0_ffn_sh_down, dtype=np.float32)

    wqkT = np.ascontiguousarray(np.transpose(wq[:, 0:128, :], (2, 0, 1)).reshape(C, NH * 128))
    wvT = np.ascontiguousarray(np.transpose(wq[:, 128:192, :], (2, 0, 1)).reshape(C, NH * HS))
    wpT = np.ascontiguousarray(wp.T)
    routerT = np.ascontiguousarray(rw.T)
    rbias_b = np.ascontiguousarray(np.broadcast_to(rb.reshape(1, NE), (128, NE)))
    bproj = np.ascontiguousarray(bp.reshape(1, C))
    shgT = np.ascontiguousarray(shg.T)
    shuT = np.ascontiguousarray(shu.T)
    shdT = np.ascontiguousarray(shd.T)
    ident = np.eye(128, dtype=ml_dtypes.bfloat16)
    maskneg = np.where(np.triu(np.ones((128, 128), np.float32)) > 0, 0.0, -1e9).astype(ml_dtypes.bfloat16)
    ones128 = np.ones((1, 128), dtype=ml_dtypes.bfloat16)
    iotam = np.ascontiguousarray(np.broadcast_to(
        (np.arange(NE, dtype=np.float32) - 16.0).reshape(1, NE), (128, NE)))

    shared = dict(wqkt=wqkT, wvt=wvT, wpt=wpT, bproj=bproj, routert=routerT,
                  rbias=rbias_b, up=up, gate=gate, down=down, shgt=shgT,
                  shut=shuT, shdt=shdT, ident=ident,
                  ones128=ones128, iotam=iotam, maskneg=maskneg)
    return x, shared


def kernel(**inputs):
    x, shared = host_prep(**inputs)
    if "nc" not in _cache:
        _cache["nc"] = build_nc()
    nc = _cache["nc"]
    in_maps = []
    for i in range(8):
        m = dict(shared)
        m["x"] = np.ascontiguousarray(x[i])
        in_maps.append(m)

    res = run_bass_kernel_spmd(nc, in_maps, core_ids=list(range(8)))
    out = np.stack([np.asarray(r["y"], dtype=np.float32) for r in res.results], axis=0)
    return out


# version 31
# speedup vs baseline: 1.2093x; 1.0006x over previous
import numpy as np
import ml_dtypes
import concourse.bass as bass
import concourse.mybir as mybir
from concourse.bass_utils import run_bass_kernel_spmd
from contextlib import ExitStack

F32 = mybir.dt.float32
BF16 = mybir.dt.bfloat16
AF = mybir.ActivationFunctionType
ALU = mybir.AluOpType
AX = mybir.AxisListType

T = 2048
C = 1024
NT = 16
NC = 8
NH = 16
HS = 64
NE = 16
ED = 512
NQS = 4

G_V = 0
G_P = 16640
G_WQK = 18176
G_COLS = 20224
G_SHH = 0

W_COLS = 16384

H_HBUF = 0
H_SCR = 4096
H_COLS = 6144

F_STAT = 0
F_EPS = 64
F_Z = 66
F_COMB = 322
F_T2 = 578
F_MSK = 706
F_IOTA = 770
F_COLS = 786

_STAGE = 9
_DUMPS = []


class _Stop(Exception):
    pass


ENGINES = ("sp", "pool", "act", "dve", "pe")
ENG_SEM = {"sp": None, "pool": None, "act": "act", "dve": "dve", "pe": "pe"}
SEM_NAMES = ("pe", "dve", "act", "dc", "dout",
             "dw", "dwp0", "dwp1", "dq0", "dq1", "ds0", "ds1", "ds2", "ds3") + tuple(
    f"dx{i}" for i in range(16))




class PB:

    def __init__(self):
        self.lists = {e: [] for e in ENGINES}
        self.cnt = {k: 0 for k in SEM_NAMES}

    def emit(self, eng, fn, waits=(), inc=None, drain=False):
        own = ENG_SEM[eng]
        ws = []
        for w in waits:
            if w is None:
                continue
            s, c = w
            if c <= 0:
                continue
            if s == own:
                continue
            assert c <= self.cnt[s], f"wait on future: {s} {c} > {self.cnt[s]}"
            ws.append((s, c))
        tok = None
        if inc:
            d = 16 if inc.startswith("d") else 1
            self.cnt[inc] += d
            tok = (inc, self.cnt[inc])
        self.lists[eng].append((ws, fn, inc, drain))
        return tok


def tmax(*toks):
    best = {}
    for t in toks:
        if t is None:
            continue
        if isinstance(t, list):
            for s, c in t:
                if best.get(s, -1) < c:
                    best[s] = c
            continue
        s, c = t
        if best.get(s, -1) < c:
            best[s] = c
    return [(s, c) for s, c in best.items()]


def build_nc():
    nc = bass.Bass("TRN2", target_bir_lowering=False)

    x_h = nc.dram_tensor("x", [T, C], F32, kind="ExternalInput")
    wqkT_h = nc.dram_tensor("wqkt", [C, NH * 128], F32, kind="ExternalInput")
    wvT_h = nc.dram_tensor("wvt", [C, NH * HS], F32, kind="ExternalInput")
    wpT_h = nc.dram_tensor("wpt", [C, C], F32, kind="ExternalInput")
    bproj_h = nc.dram_tensor("bproj", [1, C], F32, kind="ExternalInput")
    routerT_h = nc.dram_tensor("routert", [C, NE], F32, kind="ExternalInput")
    rbias_h = nc.dram_tensor("rbias", [128, NE], F32, kind="ExternalInput")
    up_h = nc.dram_tensor("up", [NE, C, ED], F32, kind="ExternalInput")
    gate_h = nc.dram_tensor("gate", [NE, C, ED], F32, kind="ExternalInput")
    down_h = nc.dram_tensor("down", [NE, ED, C], F32, kind="ExternalInput")
    shgT_h = nc.dram_tensor("shgt", [C, C], F32, kind="ExternalInput")
    shuT_h = nc.dram_tensor("shut", [C, C], F32, kind="ExternalInput")
    shdT_h = nc.dram_tensor("shdt", [C, C], F32, kind="ExternalInput")
    ident_h = nc.dram_tensor("ident", [128, 128], BF16, kind="ExternalInput")
    maskneg_h = nc.dram_tensor("maskneg", [128, 128], BF16, kind="ExternalInput")
    ones_h = nc.dram_tensor("ones128", [1, 128], BF16, kind="ExternalInput")
    iota_h = nc.dram_tensor("iotam", [128, NE], F32, kind="ExternalInput")
    out_h = nc.dram_tensor("y", [T, C], F32, kind="ExternalOutput")

    with ExitStack() as ctx:
        sb = lambda n, s, d: ctx.enter_context(nc.sbuf_tensor(n, s, d))
        xacc = sb("xacc", [128, NT * C], F32)
        xnf = sb("xnf", [128, NC * T], BF16)
        G = sb("G", [128, G_COLS], BF16)
        W = sb("W", [128, W_COLS], BF16)
        H = sb("H", [128, H_COLS], BF16)
        AP2 = sb("AP2", [128, 4096], BF16)
        WPS = sb("WPS", [128, 2048], BF16)
        LNS = AP2
        F = sb("F", [128, F_COLS], F32)
        Qb = sb("Qb", [65, T], BF16)
        Kb = sb("Kb", [65, T], BF16)
        ident = sb("identity", [128, 128], BF16)
        maskm = sb("maskneg_s", [128, 128], BF16)
        ones = sb("ones_r", [1, 128], BF16)
        bproj = sb("bproj_r", [1, C], BF16)
        routerT = sb("routerT", [128, NC * NE], BF16)
        rbias = sb("rbias_b", [128, NE], F32)
        rrowbf = sb("rrowbf", [1, 512], BF16)
        rbsb = sb("rbsb", [64, 512], BF16)

        ps = [ctx.enter_context(nc.psum_tensor(f"ps{i}", [128, 512], F32))
              for i in range(7)]
        pstr = ctx.enter_context(nc.psum_tensor("pstr", [128, 1024], BF16))

        sems = {k: ctx.enter_context(nc.semaphore(k)) for k in SEM_NAMES}
        block = ctx.enter_context(nc.Block())

        pb = PB()
        bank_free = {i: None for i in range(7)}
        bank_free["tr"] = None
        SB = [0, 1]
        PVB = [2, 3]
        _bg = [0]

        def bgbank():
            b = 4 + (_bg[0] % 3)
            _bg[0] += 1
            return b

        _DUMPS.clear()

        def stage_stop(k, dumps):
            if _STAGE != k:
                return
            for name, shape, dt, ap, waits in dumps:
                d = nc.dram_tensor(name, shape, dt, kind="ExternalOutput")
                pb.emit("sp", lambda e, d=d, ap=ap: e.dma_start(d[:, :], ap),
                        waits=waits, inc="dout")
                _DUMPS.append(name)
            fin = pb.cnt["dout"]
            pb.emit("sp", lambda e: e.nop(), waits=[("dout", fin)])
            raise _Stop()

        t_x = [None] * 16
        for c in (0, 1):
            t_x[c] = pb.emit("sp", lambda e, c=c: e.dma_start(
                xacc[:, :].rearrange("p (r n) -> p r n", n=C)[:, c:c + 1, :],
                x_h[:, :].rearrange("(r p) n -> p r n", p=128)[:, c:c + 1, :]),
                inc=f"dx{c}")
        for tns, hsrc in ((ident, ident_h), (maskm, maskneg_h),
                          (ones, ones_h), (rbias, rbias_h)):
            pb.emit("sp", lambda e, t=tns, s_=hsrc: e.dma_start(t[:, :], s_[:, :]), inc="dc")
        pb.emit("sp", lambda e: e.dma_start(F[:, F_IOTA:F_IOTA + NE], iota_h[:, :]), inc="dc")
        pb.emit("pool", lambda e: e.dma_start(bproj[:, :], bproj_h[:, :]), inc="dw")
        t_consts = ("dc", pb.cnt["dc"])
        t_kone = pb.emit("dve", lambda e: e.memset(Kb[64:65, :], 1.0), inc="dve", drain=True)
        t_kone = pb.emit("dve", lambda e: e.memset(Qb[64:65, :], 0.0), inc="dve", drain=True)
        t_eps = pb.emit("dve", lambda e: e.memset(F[:, F_EPS:F_EPS + 1], 1e-5), inc="dve", drain=True)
        t_vones = pb.emit("dve", lambda e: e.memset(
            G[:, G_V:G_V + NT * 1040].rearrange("p (k h d) -> p k h d", h=NH, d=65)[:, :, :, 64:65],
            1.0), inc="dve", drain=True)
        t_ident = t_maskm = t_ones = t_rbias = t_iota = t_consts
        t_bproj = ("dw", pb.cnt["dw"])


        def emit_x(c):
            eng = "sp" if (c % 4) < 2 else "pool"
            t_x[c] = pb.emit(eng, lambda e, c=c: e.dma_start(
                xacc[:, :].rearrange("p (r n) -> p r n", n=C)[:, c:c + 1, :],
                x_h[:, :].rearrange("(r p) n -> p r n", p=128)[:, c:c + 1, :]),
                inc=f"dx{c}")

        t_wv = [None, None]

        def emit_wv(s):
            t_wv[s] = pb.emit("pool", lambda e, s=s: e.dma_start(
                W[:, s * 4096:(s + 1) * 4096].rearrange("p (a v) -> p a v", a=4),
                wvT_h[:, :].rearrange("(a p) v -> p a v", p=128)[:, 4 * s:4 * s + 4, :]),
                inc=f"ds{s}")

        t_wqk = [None] * NH

        def emit_wqk01(h):
            t_wqk[h] = pb.emit("pool", lambda e, h=h: e.dma_start(
                G[:, G_WQK + (h % 2) * 1024:G_WQK + (h % 2) * 1024 + 1024]
                .rearrange("p (a d) -> p a d", a=NC),
                wqkT_h[:, h * 128:(h + 1) * 128].rearrange("(a p) d -> p a d", p=128)),
                inc=f"dq{h % 2}")

        t_wps = [None, None]

        def emit_wps0(p):
            t_wps[p] = pb.emit("pool", lambda e, p=p: e.dma_start(
                WPS[:, p * 1024:(p + 1) * 1024],
                wpT_h[p * 128:(p + 1) * 128, :]), inc=f"dwp{p % 2}")

        t_wdma = {}
        slot_free = [None] * 4

        def emit_wdma(e_i, m):
            src = (up_h, gate_h, down_h)[m]
            s = (3 * e_i + m + 2) % 4
            inap = src[e_i, :, :].rearrange("(a p) j -> p a j", p=128)
            outap = W[:, s * 4096:(s + 1) * 4096].rearrange(
                "p (a j) -> p a j", a=NC if m < 2 else 4)
            t_wdma[(e_i, m)] = pb.emit("pool", lambda e, inap=inap, outap=outap: e.dma_start(
                outap, inap), waits=tmax(slot_free[s]), inc=f"ds{s}")

        for c in (4, 5, 8, 9, 12, 13):
            emit_x(c)
        emit_x(2); emit_x(3)
        emit_wv(0); emit_wv(1)
        emit_x(6); emit_x(7)
        emit_wqk01(0)
        pb.emit("pool", lambda e: e.dma_start(
            routerT[:, :].rearrange("p (a v) -> p a v", a=NC),
            routerT_h[:, :].rearrange("(a p) v -> p a v", p=128)), inc="dw")
        t_routerT = ("dw", pb.cnt["dw"])
        t_bproj = t_routerT
        emit_x(10); emit_x(11)
        emit_wps0(0); emit_wps0(1)
        emit_wdma(0, 0)
        emit_x(14); emit_x(15)
        emit_wdma(0, 1)
        emit_wqk01(1)

        bgq = []

        def fill(n):
            for _ in range(n):
                if not bgq:
                    return
                bgq.pop(0)()

        t_xnf = [None] * NT
        t_vready = [None] * NT
        t_comb = [None] * NT
        t_vlast = [None]

        def v_item(tt):
            def go():
                toks = []
                for half in range(2):
                    b = bgbank()
                    def v_fn(e, tt=tt, half=half, b=b):
                        last = None
                        for cc in range(NC):
                            last = e.matmul(
                                ps[b][:, 0:512],
                                xnf[:, cc * T + tt * 128:cc * T + (tt + 1) * 128],
                                W[:, cc * C + half * 512:cc * C + half * 512 + 512],
                                start=(cc == 0), stop=(cc == NC - 1))
                        return last
                    t_mm = pb.emit("pe", v_fn,
                                   waits=tmax(t_xnf[tt], t_wv[0], t_wv[1], bank_free[b]),
                                   inc="pe")
                    t_ev = pb.emit("dve", lambda e, tt=tt, half=half, b=b: e.tensor_copy(
                        G[:, G_V + tt * 1040:G_V + (tt + 1) * 1040]
                        .rearrange("p (h d) -> p h d", d=65)[:, half * 8:half * 8 + 8, 0:64],
                        ps[b][:, 0:512].rearrange("p (h d) -> p h d", d=64)),
                        waits=[t_mm], inc="dve")
                    bank_free[b] = t_ev
                    toks.append(t_ev)
                    t_vlast[0] = t_mm
                t_vready[tt] = toks[-1]
                if tt == NT - 1:
                    slot_free[0] = t_vlast[0]
                    slot_free[1] = t_vlast[0]
                    emit_wdma(0, 2)
                    emit_wdma(1, 0)
            return go

        def router_item(tt):
            z = F[:, F_Z + tt * NE:F_Z + (tt + 1) * NE]

            def go():
                b = bgbank()
                def r_fn(e, tt=tt, b=b):
                    last = None
                    for cc in range(NC):
                        last = e.matmul(
                            ps[b][:, 0:NE],
                            xnf[:, cc * T + tt * 128:cc * T + (tt + 1) * 128],
                            routerT[:, cc * NE:(cc + 1) * NE],
                            start=(cc == 0), stop=(cc == NC - 1))
                    return last
                t_rmm = pb.emit("pe", r_fn,
                                waits=tmax(t_xnf[tt], t_routerT, bank_free[b]), inc="pe")
                t_z = pb.emit("dve", lambda e, z=z, b=b: e.tensor_tensor(
                    z, ps[b][:, 0:NE], rbias[:, :], op=ALU.add),
                    waits=tmax(t_rmm, t_rbias, t_iota), inc="dve", drain=True)
                bank_free[b] = t_z
            return go

        def emit_router_top2():
            z3 = F[:, F_Z:F_Z + 256].rearrange("p (t e) -> p t e", e=NE)
            cb3 = F[:, F_COMB:F_COMB + 256].rearrange("p (t e) -> p t e", e=NE)
            m1 = F[:, F_T2 + 0:F_T2 + 16]
            m2 = F[:, F_T2 + 16:F_T2 + 32]
            e2 = F[:, F_T2 + 32:F_T2 + 48]
            s1v = F[:, F_T2 + 48:F_T2 + 64]
            s2v = F[:, F_T2 + 64:F_T2 + 80]
            den = F[:, F_T2 + 80:F_T2 + 96]
            i1 = F[:, F_T2 + 96:F_T2 + 112]
            i2 = F[:, F_T2 + 112:F_T2 + 128]
            msk1 = AP2[:, 0:256].rearrange("p (t e) -> p t e", e=NE)
            msk2 = AP2[:, 256:512].rearrange("p (t e) -> p t e", e=NE)
            tmpb = AP2[:, 512:768].rearrange("p (t e) -> p t e", e=NE)
            iotb = F[:, F_IOTA:F_IOTA + NE].rearrange("p e -> p () e").broadcast_to([128, 16, NE])
            bt = lambda a: a.rearrange("p t -> p t ()").broadcast_to([128, 16, NE])
            pb.emit("dve", lambda e: e.tensor_reduce(m1, z3, axis=AX.X, op=ALU.max),
                    inc="dve", drain=True)
            pb.emit("dve", lambda e: e.tensor_tensor(msk1, z3, bt(m1), op=ALU.is_equal),
                    waits=tmax(t_tr[NT - 2]), inc="dve", drain=True)
            pb.emit("dve", lambda e: e.tensor_tensor(tmpb, msk1, iotb, op=ALU.mult),
                    inc="dve", drain=True)
            pb.emit("dve", lambda e: e.tensor_reduce(i1, tmpb, axis=AX.X, op=ALU.min),
                    inc="dve", drain=True)
            pb.emit("dve", lambda e: e.tensor_tensor(msk1, iotb, bt(i1), op=ALU.is_equal),
                    inc="dve", drain=True)
            pb.emit("dve", lambda e: e.scalar_tensor_tensor(
                z3, msk1, -1e30, z3, op0=ALU.mult, op1=ALU.add), inc="dve", drain=True)
            pb.emit("dve", lambda e: e.tensor_reduce(m2, z3, axis=AX.X, op=ALU.max),
                    inc="dve", drain=True)
            pb.emit("dve", lambda e: e.tensor_tensor(msk2, z3, bt(m2), op=ALU.is_equal),
                    inc="dve", drain=True)
            pb.emit("dve", lambda e: e.tensor_tensor(tmpb, msk2, iotb, op=ALU.mult),
                    inc="dve", drain=True)
            pb.emit("dve", lambda e: e.tensor_reduce(i2, tmpb, axis=AX.X, op=ALU.min),
                    inc="dve", drain=True)
            pb.emit("dve", lambda e: e.tensor_tensor(msk2, iotb, bt(i2), op=ALU.is_equal),
                    inc="dve", drain=True)
            t_sub = pb.emit("dve", lambda e: e.tensor_tensor(e2, m2, m1, op=ALU.subtract),
                            inc="dve", drain=True)
            t_exp = pb.emit("act", lambda e: e.activation(e2, e2, AF.Exp),
                            waits=[t_sub], inc="act", drain=True)
            pb.emit("dve", lambda e: e.tensor_scalar(
                den, e2, 1.0, None, op0=ALU.add), waits=[t_exp], inc="dve", drain=True)
            pb.emit("dve", lambda e: e.reciprocal(s1v, den), inc="dve", drain=True)
            pb.emit("dve", lambda e: e.tensor_scalar(
                s2v, s1v, -1.0, 1.0, op0=ALU.mult, op1=ALU.add), inc="dve", drain=True)
            pb.emit("dve", lambda e: e.tensor_tensor(cb3, msk1, bt(s1v), op=ALU.mult),
                    inc="dve", drain=True)
            pb.emit("dve", lambda e: e.tensor_tensor(z3, msk2, bt(s2v), op=ALU.mult),
                    inc="dve", drain=True)
            tok = pb.emit("dve", lambda e: e.tensor_tensor(cb3, cb3, z3, op=ALU.add),
                          inc="dve", drain=True)
            for tt in range(NT):
                t_comb[tt] = tok
            return tok

        h_done = {}
        t_hbuf_rd = [None, None]
        t_scr_rd = [None, None]
        _scr = [0]

        ustate = {}

        def u_item(e_i, ts, jc):
            def go():
                su = (3 * e_i + 0 + 2) % 4
                bu = bgbank()
                par = ts % 2
                def u_fn(e, jc=jc, ts=ts, bu=bu, base=su * 4096):
                    last = None
                    for cc in range(NC):
                        last = e.matmul(
                            ps[bu][:, 0:512],
                            W[:, base + cc * 512 + jc * 128:base + cc * 512 + (jc + 1) * 128],
                            xnf[:, cc * T + ts * 512:cc * T + ts * 512 + 512],
                            start=(cc == 0), stop=(cc == NC - 1))
                    return last
                t_umm = pb.emit("pe", u_fn,
                                waits=tmax(t_wdma[(e_i, 0)], bank_free[bu]), inc="pe")
                if (ts * 4 + jc) % 2 == 0:
                    t_ue = pb.emit("act", lambda e, bu=bu, par=par, jc=jc: e.activation(
                        H[:, H_HBUF + par * 2048 + jc * 512:H_HBUF + par * 2048 + (jc + 1) * 512],
                        ps[bu][:, 0:512], AF.Identity),
                        waits=tmax(t_umm, t_hbuf_rd[par]), inc="act")
                else:
                    t_ue = pb.emit("dve", lambda e, bu=bu, par=par, jc=jc: e.tensor_copy(
                        H[:, H_HBUF + par * 2048 + jc * 512:H_HBUF + par * 2048 + (jc + 1) * 512],
                        ps[bu][:, 0:512]),
                        waits=tmax(t_umm, t_hbuf_rd[par]), inc="dve")
                bank_free[bu] = t_ue
                ustate[(e_i, ts, jc)] = t_ue
                if ts == 3 and jc == 3:
                    slot_free[su] = t_umm
                    if e_i + 1 == NE:
                        t_shg[0] = pb.emit("pool", lambda e: e.dma_start(
                            W[:, 8192:16384].rearrange("p (a v) -> p a v", a=NC),
                            shgT_h[:, :].rearrange("(a p) v -> p a v", p=128)),
                            waits=tmax(slot_free[2], slot_free[3]), inc="ds2")
            return go

        def g_item(e_i, ts, jc):
            def go():
                sg = (3 * e_i + 1 + 2) % 4
                bg2 = bgbank()
                par = ts % 2
                spar = _scr[0] % 2
                _scr[0] += 1
                t_ue = ustate.pop((e_i, ts, jc))
                def g_fn(e, jc=jc, ts=ts, bg2=bg2, base=sg * 4096):
                    last = None
                    for cc in range(NC):
                        last = e.matmul(
                            ps[bg2][:, 0:512],
                            W[:, base + cc * 512 + jc * 128:base + cc * 512 + (jc + 1) * 128],
                            xnf[:, cc * T + ts * 512:cc * T + ts * 512 + 512],
                            start=(cc == 0), stop=(cc == NC - 1))
                    return last
                t_gmm = pb.emit("pe", g_fn,
                                waits=tmax(t_wdma[(e_i, 1)], bank_free[bg2]), inc="pe")
                scr = H_SCR + spar * 1024
                t_th = pb.emit("act", lambda e, bg2=bg2, scr=scr: e.activation(
                    H[:, scr:scr + 512], ps[bg2][:, 0:512], AF.Tanh, scale=0.5),
                    waits=tmax(t_gmm, t_scr_rd[spar]), inc="act")
                t_2s = pb.emit("dve", lambda e, bg2=bg2, scr=scr: e.scalar_tensor_tensor(
                    H[:, scr + 512:scr + 1024], H[:, scr:scr + 512], 1.0,
                    ps[bg2][:, 0:512], op0=ALU.add, op1=ALU.mult),
                    waits=tmax(t_th), inc="dve")
                bank_free[bg2] = t_2s
                t_h = pb.emit("dve", lambda e, par=par, jc=jc, scr=scr: e.scalar_tensor_tensor(
                    H[:, H_HBUF + par * 2048 + jc * 512:H_HBUF + par * 2048 + (jc + 1) * 512],
                    H[:, scr + 512:scr + 1024], 0.5,
                    H[:, H_HBUF + par * 2048 + jc * 512:H_HBUF + par * 2048 + (jc + 1) * 512],
                    op0=ALU.mult, op1=ALU.mult),
                    waits=tmax(t_ue), inc="dve")
                t_scr_rd[spar] = t_h
                if jc == 3:
                    h_done[(e_i, ts)] = t_h
                if ts == 3 and jc == 3:
                    slot_free[sg] = t_gmm
                    if e_i + 1 < NE:
                        emit_wdma(e_i + 1, 1)
            return go

        def down_item(e_i, ts, tti, dh):
            def go():
                sd = (3 * e_i + 2 + 2) % 4
                par = ts % 2
                tt = ts * 4 + tti
                assert t_comb[tt] is not None and h_done.get((e_i, ts)) is not None
                cbcol = F[:, F_COMB + tt * NE + e_i:F_COMB + tt * NE + e_i + 1]
                b = bgbank()
                def d_fn(e, par=par, tti=tti, dh=dh, b=b, base=sd * 4096):
                    last = None
                    for jc in range(4):
                        last = e.matmul(
                            ps[b][:, 0:512],
                            H[:, H_HBUF + par * 2048 + jc * 512 + tti * 128:
                               H_HBUF + par * 2048 + jc * 512 + (tti + 1) * 128],
                            W[:, base + jc * 1024 + dh * 512:base + jc * 1024 + dh * 512 + 512],
                            start=(jc == 0), stop=(jc == 3))
                    return last
                t_dmm = pb.emit("pe", d_fn,
                                waits=tmax(h_done[(e_i, ts)], t_wdma[(e_i, 2)],
                                           bank_free[b]), inc="pe")
                t_hbuf_rd[par] = t_dmm
                t_de = pb.emit("dve", lambda e, tt=tt, dh=dh, b=b, cbcol=cbcol: e.scalar_tensor_tensor(
                    xacc[:, tt * C + dh * 512:tt * C + dh * 512 + 512],
                    ps[b][:, 0:512], cbcol,
                    xacc[:, tt * C + dh * 512:tt * C + dh * 512 + 512],
                    op0=ALU.mult, op1=ALU.add),
                    waits=tmax(t_dmm, t_comb[tt]), inc="dve")
                bank_free[b] = t_de
                if ts == 3 and tti == 3 and dh == 1:
                    slot_free[sd] = ("pe", pb.cnt["pe"])
                    if e_i + 1 < NE:
                        emit_wdma(e_i + 1, 2)
                    if e_i + 2 < NE:
                        emit_wdma(e_i + 2, 0)
                    if e_i + 1 == NE:
                        t_shu[0] = pb.emit("pool", lambda e: e.dma_start(
                            W[:, 0:8192].rearrange("p (a v) -> p a v", a=NC),
                            shuT_h[:, :].rearrange("(a p) v -> p a v", p=128)),
                            waits=tmax(slot_free[0], slot_free[1]), inc="ds0")
            return go

        t_shg = [None]
        t_shu = [None]
        t_shd = [None]

        try:
            t_tr = [None] * NT
            t_qk_evac = {}
            wqk_reader = [None, None]
            t_nmrl = [None] * NT

            def emit_stats(tt):
                xs = xacc[:, tt * C:(tt + 1) * C]
                st = F_STAT + tt * 4
                par = tt % 2
                lns = LNS[:, par * 1024:(par + 1) * 1024]
                pb.emit("dve", lambda e, xs=xs, st=st: e.reduce_sum(
                    F[:, st:st + 1], xs, axis=AX.X), waits=[t_x[tt]], inc="dve", drain=True)
                t_ssq = pb.emit("act", lambda e, xs=xs, st=st, lns=lns: e.activation(
                    lns, xs, AF.Square, accum_out=F[:, st + 3:st + 4]),
                    waits=tmax(t_x[tt], t_tr[tt - 2] if tt >= 2 else None),
                    inc="act", drain=True)
                pb.emit("dve", lambda e, st=st: e.tensor_scalar(
                    F[:, st:st + 1], F[:, st:st + 1], 1.0 / C, None, op0=ALU.mult), inc="dve", drain=True)
                pb.emit("dve", lambda e, st=st: e.tensor_scalar(
                    F[:, st + 1:st + 2], F[:, st + 3:st + 4], 1.0 / C, None, op0=ALU.mult),
                    waits=[t_ssq], inc="dve", drain=True)
                pb.emit("dve", lambda e, st=st: e.tensor_tensor(
                    F[:, st + 2:st + 3], F[:, st:st + 1], F[:, st:st + 1], op=ALU.mult), inc="dve", drain=True)
                t_var = pb.emit("dve", lambda e, st=st: e.tensor_tensor(
                    F[:, st + 1:st + 2], F[:, st + 1:st + 2], F[:, st + 2:st + 3],
                    op=ALU.subtract), inc="dve", drain=True)
                pb.emit("act", lambda e, st=st: e.activation(
                    F[:, st + 1:st + 2], F[:, st + 1:st + 2], AF.Ln,
                    bias=F[:, F_EPS:F_EPS + 1], scale=1.0),
                    waits=tmax(t_eps, t_var), inc="act", drain=True)
                t_rstd = pb.emit("act", lambda e, st=st: e.activation(
                    F[:, st + 1:st + 2], F[:, st + 1:st + 2], AF.Exp, scale=-0.5),
                    inc="act", drain=True)
                t_nmrl[tt] = pb.emit("dve", lambda e, st=st: e.tensor_scalar(
                    F[:, st + 2:st + 3], F[:, st:st + 1], F[:, st + 1:st + 2], -1.0,
                    op0=ALU.mult, op1=ALU.mult), waits=[t_rstd], inc="dve", drain=True)

            emit_stats(0)
            for tt in range(NT):
                xs = xacc[:, tt * C:(tt + 1) * C]
                st = F_STAT + tt * 4
                par = tt % 2
                lns = LNS[:, par * 1024:(par + 1) * 1024]
                t_apply = pb.emit("act", lambda e, xs=xs, st=st, lns=lns: e.activation(
                    lns, xs, AF.Identity,
                    bias=F[:, st + 2:st + 3], scale=F[:, st + 1:st + 2]),
                    waits=[t_nmrl[tt]], inc="act", drain=True)
                def tr_fn(e, lns=lns):
                    last = None
                    for cc in range(NC):
                        last = e.transpose(pstr[:, cc * 128:(cc + 1) * 128],
                                           lns[:, cc * 128:(cc + 1) * 128],
                                           ident[:, :])
                    return last
                t_tr[tt] = pb.emit("pe", tr_fn,
                                   waits=tmax(t_apply, t_ident, bank_free["tr"]), inc="pe")
                t_xnf[tt] = pb.emit("dve", lambda e, tt=tt: e.tensor_copy(
                    xnf[:, :].rearrange("p (a t) -> p a t", a=NC)[:, :, tt * 128:(tt + 1) * 128],
                    pstr[:, :].rearrange("p (a d) -> p a d", a=NC)),
                    waits=[t_tr[tt]], inc="dve")
                bank_free["tr"] = t_xnf[tt]
                if tt + 1 < NT:
                    emit_stats(tt + 1)
                bgq.append(v_item(tt))
                bgq.append(router_item(tt))
                if tt % 4 == 3:
                    ts_ = tt // 4
                    b = bgbank()
                    def qk_fn(e, ts_=ts_, b=b):
                        last = None
                        for cc in range(NC):
                            last = e.matmul(
                                ps[b][:, 0:512],
                                G[:, G_WQK + cc * 128:G_WQK + (cc + 1) * 128],
                                xnf[:, cc * T + ts_ * 512:cc * T + ts_ * 512 + 512],
                                start=(cc == 0), stop=(cc == NC - 1))
                        return last
                    t_mm = pb.emit("pe", qk_fn,
                                   waits=tmax(t_wqk[0], t_xnf[tt], bank_free[b]), inc="pe")
                    if ts_ == 3:
                        wqk_reader[0] = t_mm
                        t_wqk[2] = pb.emit("pool", lambda e: e.dma_start(
                            G[:, G_WQK:G_WQK + 1024].rearrange("p (a d) -> p a d", a=NC),
                            wqkT_h[:, 2 * 128:3 * 128].rearrange("(a p) d -> p a d", p=128)),
                            waits=tmax(wqk_reader[0]), inc="dq0")
                    pb.emit("act", lambda e, ts_=ts_, b=b: e.activation(
                        Qb[0:64, ts_ * 512:(ts_ + 1) * 512], ps[b][0:64, 0:512],
                        AF.Identity), waits=[t_mm], inc="act")
                    t_qk_evac[ts_] = pb.emit("act", lambda e, ts_=ts_, b=b: e.activation(
                        Kb[0:64, ts_ * 512:(ts_ + 1) * 512], ps[b][64:128, 0:512],
                        AF.Identity), inc="act")
                    bank_free[b] = t_qk_evac[ts_]
                if tt >= 2:
                    fill(3)

            stage_stop(1, [
                ("d_xnf", [128, NC * T], BF16, xnf[:, :], tmax(t_xnf[NT - 1])),
            ])

            while bgq:
                bgq.pop(0)()
            pend = None
            for e_i in range(NE):
                for ts in range(4):
                    for jc in range(4):
                        bgq.append(u_item(e_i, ts, jc))
                        bgq.append(g_item(e_i, ts, jc))
                    if pend is not None:
                        pe_, pts = pend
                        for tti in range(4):
                            for dh in range(2):
                                bgq.append(down_item(pe_, pts, tti, dh))
                    pend = (e_i, ts)
            for tti in range(4):
                for dh in range(2):
                    bgq.append(down_item(NE - 1, 3, tti, dh))

            t_gpfree = emit_router_top2()

            NPP = 3
            t_rb_used = None
            t_s_last = None
            t_amult = [[], []]
            proj_free = [None, None]

            def emit_qk(h):
                for ts_ in range(4):
                    b = bgbank()
                    def qk_fn(e, ts_=ts_, b=b, h=h):
                        last = None
                        for cc in range(NC):
                            last = e.matmul(
                                ps[b][:, 0:512],
                                G[:, G_WQK + (h % 2) * 1024 + cc * 128:
                                   G_WQK + (h % 2) * 1024 + (cc + 1) * 128],
                                xnf[:, cc * T + ts_ * 512:cc * T + ts_ * 512 + 512],
                                start=(cc == 0), stop=(cc == NC - 1))
                        return last
                    t_mm = pb.emit("pe", qk_fn,
                                   waits=tmax(t_wqk[h], t_xnf[NT - 1], bank_free[b]), inc="pe")
                    if ts_ == 3:
                        wqk_reader[h % 2] = t_mm
                        if h + 2 < NH:
                            par2 = (h + 2) % 2
                            t_wqk[h + 2] = pb.emit("pool", lambda e, h2=h + 2, par2=par2: e.dma_start(
                                G[:, G_WQK + par2 * 1024:G_WQK + par2 * 1024 + 1024]
                                .rearrange("p (a d) -> p a d", a=NC),
                                wqkT_h[:, h2 * 128:(h2 + 1) * 128].rearrange("(a p) d -> p a d", p=128)),
                                waits=tmax(wqk_reader[par2]), inc=f"dq{par2}")
                    pb.emit("act", lambda e, ts_=ts_, b=b: e.activation(
                        Qb[0:64, ts_ * 512:(ts_ + 1) * 512], ps[b][0:64, 0:512],
                        AF.Identity), waits=tmax(t_mm, t_s_last), inc="act")
                    t_qk_evac[ts_] = pb.emit("act", lambda e, ts_=ts_, b=b: e.activation(
                        Kb[0:64, ts_ * 512:(ts_ + 1) * 512], ps[b][64:128, 0:512],
                        AF.Identity), inc="act")
                    bank_free[b] = t_qk_evac[ts_]
                    fill(1)

            def emit_proj(p):
                par = p % 2
                t_mm = None
                for tt in range(NT):
                    for dh in range(2):
                        b = bgbank()
                        def pj_fn(e, tt=tt, dh=dh, b=b, par=par, p=p):
                            m = e.matmul(ps[b][:, 0:512],
                                         AP2[:, par * 2048 + tt * 128:par * 2048 + (tt + 1) * 128],
                                         WPS[:, par * 1024 + dh * 512:par * 1024 + dh * 512 + 512],
                                         start=True, stop=(p != 0))
                            if p == 0:
                                m = e.matmul(ps[b][:, 0:512], ones[0:1, :],
                                             bproj[0:1, dh * 512:dh * 512 + 512],
                                             start=False, stop=True)
                            return m
                        t_mm = pb.emit("pe", pj_fn,
                                       waits=tmax(t_amult[par], t_wps[par], t_ones,
                                                  t_bproj, bank_free[b]), inc="pe")
                        t_pe = pb.emit("dve", lambda e, tt=tt, dh=dh, b=b: e.tensor_tensor(
                            xacc[:, tt * C + dh * 512:tt * C + dh * 512 + 512],
                            ps[b][:, 0:512],
                            xacc[:, tt * C + dh * 512:tt * C + dh * 512 + 512],
                            op=ALU.add), waits=[t_mm], inc="dve")
                        bank_free[b] = t_pe
                proj_free[par] = t_mm
                t_amult[par] = []
                if p + 2 < 8:
                    t_wps[par] = pb.emit("pool", lambda e, p2=p + 2, par=par: e.dma_start(
                        WPS[:, par * 1024:(par + 1) * 1024],
                        wpT_h[p2 * 128:(p2 + 1) * 128, :]),
                        waits=[t_mm], inc=f"dwp{par}")

            for h in range(NH):
                if h > 0:
                    emit_qk(h)
                else:
                    pass
                it = 0
                t_pv_stop = [None] * NQS
                p_used = [None] * NPP
                t_exp_of = [None] * NPP
                pend_q = []
                def push_pending(fn):
                    pend_q.append(fn)
                    if len(pend_q) > 3:
                        pend_q.pop(0)()
                def flush_pending():
                    while pend_q:
                        pend_q.pop(0)()
                for qs in range(NQS):
                    for kt in range(4 * qs + 4):
                        b = SB[it % 2]
                        j = kt - 4 * qs
                        if j > 0:
                            off, masked = 128 * j, True
                        elif j == 0:
                            off, masked = 0, True
                        else:
                            off, masked = 0, False
                        N = 512 - off
                        pp = it % NPP
                        def s_fn(e, b=b, kt=kt, qs=qs, off=off, N=N, masked=masked):
                            m = e.matmul(ps[b][:, 0:N],
                                         Kb[0:65, kt * 128:(kt + 1) * 128],
                                         Qb[0:65, qs * 512 + off:qs * 512 + 512],
                                         start=True, stop=(not masked))
                            if masked:
                                m = e.matmul(ps[b][:, 0:128], ident[:, :], maskm[:, :],
                                             start=False, stop=True)
                            return m
                        t_s = pb.emit("pe", s_fn,
                            waits=tmax(bank_free[b], t_qk_evac[qs], t_qk_evac[kt // 4],
                                       t_kone, t_maskm, t_ident),
                            inc="pe")
                        t_s_last = t_s

                        def pv_back(kt=kt, h=h, pp=pp, off=off, N=N, qs=qs):
                            pvb = PVB[qs % 2]
                            assert t_vready[kt] is not None
                            w = tmax(t_exp_of[pp], t_vready[kt], t_vones)
                            if kt == 0:
                                w = tmax(w, bank_free[pvb])
                            t_pv = pb.emit("pe", lambda e, pvb=pvb, kt=kt, h=h, pp=pp, off=off, N=N: e.matmul(
                                ps[pvb][0:65, off:512],
                                G[:, G_V + kt * 1040 + h * 65:G_V + kt * 1040 + h * 65 + 65],
                                G[:, G_P + pp * 512:G_P + pp * 512 + N],
                                start=(kt == 0), stop=(kt == 4 * qs + 3)),
                                waits=w, inc="pe")
                            p_used[pp] = t_pv
                            if kt == 4 * qs + 3:
                                t_pv_stop[qs] = t_pv
                        push_pending(pv_back)
                        t_e = pb.emit("act", lambda e, b=b, N=N, pp=pp: e.activation(
                            G[:, G_P + pp * 512:G_P + pp * 512 + N],
                            ps[b][:, 0:N], AF.Exp, scale=0.125),
                            waits=tmax(t_s, p_used[pp]), inc="act", drain=True)
                        bank_free[b] = t_e
                        t_exp_of[pp] = t_e
                        fill(1)
                        it += 1
                    flush_pending()
                    pvb = PVB[qs % 2]
                    t_cp = pb.emit("act", lambda e, pvb=pvb: e.activation(
                        rrowbf[0:1, 0:512], ps[pvb][64:65, 0:512], AF.Identity),
                        waits=tmax(t_pv_stop[qs], t_rb_used), inc="act", drain=True)
                    fill(2)
                    rbb = bgbank()
                    t_bc = pb.emit("pe", lambda e, rbb=rbb: e.matmul(
                        ps[rbb][0:64, 0:512], ones[0:1, 0:64], rrowbf[0:1, 0:512],
                        start=True, stop=True),
                        waits=tmax(t_cp, t_ones, bank_free[rbb]), inc="pe")
                    t_rb_used = t_bc
                    def ri_fn(e, rbb=rbb):
                        with nc.allow_low_precision(reason="rowsum rinv is consumed as a bf16 elementwise operand"):
                            return e.reciprocal(rbsb[:, :], ps[rbb][0:64, 0:512])
                    t_rbe = pb.emit("dve", ri_fn, waits=[t_bc], inc="dve")
                    bank_free[rbb] = t_rbe
                    par = (h // 2) % 2
                    r0 = (h % 2) * 64
                    t_am = pb.emit("dve", lambda e, pvb=pvb, par=par, r0=r0, qs=qs: e.tensor_tensor(
                        AP2[r0:r0 + 64, par * 2048 + qs * 512:par * 2048 + qs * 512 + 512],
                        ps[pvb][0:64, 0:512], rbsb[:, :], op=ALU.mult),
                        waits=tmax(t_rbe, proj_free[par]), inc="dve")
                    bank_free[pvb] = t_am
                    t_amult[par].append(t_am)
                    fill(1)
                if h % 2 == 1 and h // 2 >= 1:
                    emit_proj(h // 2 - 1)
                fill(2)
            emit_proj(7)
            pe_attn_done = ("pe", pb.cnt["pe"])

            stage_stop(3, [
                ("d_xacc", [128, NT * C], F32, xacc[:, :], tmax(proj_free[1])),
            ])

            while bgq:
                bgq.pop(0)()

            t_gmm_last = None
            t_hid = {}
            for jc in range(NC):
                for ts in range(4):
                    b = bgbank()
                    def sg_fn(e, jc=jc, ts=ts, b=b):
                        last = None
                        for cc in range(NC):
                            last = e.matmul(
                                ps[b][:, 0:512],
                                W[:, 8192 + cc * C + jc * 128:8192 + cc * C + (jc + 1) * 128],
                                xnf[:, cc * T + ts * 512:cc * T + ts * 512 + 512],
                                start=(cc == 0), stop=(cc == NC - 1))
                        return last
                    t_gmm = pb.emit("pe", sg_fn,
                                    waits=tmax(t_shg[0], bank_free[b], pe_attn_done), inc="pe")
                    t_gmm_last = t_gmm
                    spar = _scr[0] % 2
                    _scr[0] += 1
                    scr = H_SCR + spar * 1024
                    t_th = pb.emit("act", lambda e, b=b, scr=scr: e.activation(
                        H[:, scr:scr + 512], ps[b][:, 0:512], AF.Tanh, scale=0.5),
                        waits=tmax(t_gmm, t_scr_rd[spar]), inc="act")
                    t_2s = pb.emit("dve", lambda e, b=b, scr=scr, jc=jc, ts=ts: e.scalar_tensor_tensor(
                        G[:, G_SHH + jc * 2048 + ts * 512:G_SHH + jc * 2048 + (ts + 1) * 512],
                        H[:, scr:scr + 512], 1.0, ps[b][:, 0:512],
                        op0=ALU.add, op1=ALU.mult),
                        waits=tmax(t_th, pe_attn_done), inc="dve")
                    bank_free[b] = t_2s
                    t_scr_rd[spar] = t_2s
                    t_hid[(jc, ts)] = t_2s
            t_shd[0] = pb.emit("pool", lambda e: e.dma_start(
                W[:, 8192:16384].rearrange("p (a v) -> p a v", a=NC),
                shdT_h[:, :].rearrange("(a p) v -> p a v", p=128)),
                waits=tmax(t_gmm_last, t_shg[0]), inc="ds3")
            t_hid_last = None
            for jc in range(NC):
                for ts in range(4):
                    b = bgbank()
                    def su_fn(e, jc=jc, ts=ts, b=b):
                        last = None
                        for cc in range(NC):
                            last = e.matmul(
                                ps[b][:, 0:512],
                                W[:, cc * C + jc * 128:cc * C + (jc + 1) * 128],
                                xnf[:, cc * T + ts * 512:cc * T + ts * 512 + 512],
                                start=(cc == 0), stop=(cc == NC - 1))
                        return last
                    t_umm = pb.emit("pe", su_fn,
                                    waits=tmax(t_shu[0], bank_free[b]), inc="pe")
                    t_hd = pb.emit("dve", lambda e, b=b, jc=jc, ts=ts: e.scalar_tensor_tensor(
                        G[:, G_SHH + jc * 2048 + ts * 512:G_SHH + jc * 2048 + (ts + 1) * 512],
                        G[:, G_SHH + jc * 2048 + ts * 512:G_SHH + jc * 2048 + (ts + 1) * 512],
                        0.5, ps[b][:, 0:512], op0=ALU.mult, op1=ALU.mult),
                        waits=tmax(t_umm, t_hid[(jc, ts)]), inc="dve")
                    bank_free[b] = t_hd
                    t_hid_last = t_hd
            t_out_ready = [None] * NT
            for tt in range(NT):
                for dh in range(2):
                    b = bgbank()
                    def sd_fn(e, tt=tt, dh=dh, b=b):
                        last = None
                        for jc in range(NC):
                            last = e.matmul(
                                ps[b][:, 0:512],
                                G[:, G_SHH + jc * 2048 + tt * 128:G_SHH + jc * 2048 + (tt + 1) * 128],
                                W[:, 8192 + jc * 1024 + dh * 512:8192 + jc * 1024 + dh * 512 + 512],
                                start=(jc == 0), stop=(jc == NC - 1))
                        return last
                    t_dmm = pb.emit("pe", sd_fn,
                                    waits=tmax(t_hid_last, t_shd[0], bank_free[b]), inc="pe")
                    t_de = pb.emit("dve", lambda e, tt=tt, dh=dh, b=b: e.tensor_tensor(
                        xacc[:, tt * C + dh * 512:tt * C + dh * 512 + 512],
                        ps[b][:, 0:512],
                        xacc[:, tt * C + dh * 512:tt * C + dh * 512 + 512],
                        op=ALU.add), waits=[t_dmm], inc="dve", drain=True)
                    bank_free[b] = t_de
                    t_out_ready[tt] = t_de

            for g in range(8):
                eng = "sp" if g % 2 == 0 else "pool"
                pb.emit(eng, lambda e, g=g: e.dma_start(
                    out_h[:, :].rearrange("(r p) n -> p r n", p=128)[:, 2 * g:2 * g + 2, :],
                    xacc[:, :].rearrange("p (r n) -> p r n", n=C)[:, 2 * g:2 * g + 2, :]),
                    waits=tmax(t_out_ready[2 * g + 1]), inc="dout")
            final_dout = pb.cnt["dout"]
            pb.emit("sp", lambda e: e.nop(), waits=[("dout", final_dout)])

        except _Stop:
            pass

        def body(eng_name):
            waited = {}
            def run(eng):
                for ws, fn, inc, drain in pb.lists[eng_name]:
                    for s, c in ws:
                        if waited.get(s, 0) < c:
                            eng.wait_ge(sems[s], c)
                            waited[s] = c
                    ins = fn(eng)
                    if drain:
                        ins = eng.drain()
                    if inc:
                        ins.then_inc(sems[inc], 16 if inc.startswith("d") else 1)
            return run

        block.sync(body("sp"))
        block.gpsimd(body("pool"))
        block.scalar(body("act"))
        block.vector(body("dve"))
        block.tensor(body("pe"))

    return nc


_cache = {}


def host_prep(L0_attn_x, L0_attn_w_qkv, L0_attn_w_proj, L0_attn_b_proj,
              L0_ffn_router_w, L0_ffn_router_bias, L0_ffn_up_proj, L0_ffn_gate_proj,
              L0_ffn_down_proj, L0_ffn_sh_gate, L0_ffn_sh_up, L0_ffn_sh_down):
    x = np.asarray(L0_attn_x, dtype=np.float32)
    wq = np.asarray(L0_attn_w_qkv, dtype=np.float32)
    wp = np.asarray(L0_attn_w_proj, dtype=np.float32)
    bp = np.asarray(L0_attn_b_proj, dtype=np.float32)
    rw = np.asarray(L0_ffn_router_w, dtype=np.float32)
    rb = np.asarray(L0_ffn_router_bias, dtype=np.float32)
    up = np.ascontiguousarray(np.asarray(L0_ffn_up_proj, dtype=np.float32))
    gate = np.ascontiguousarray(np.asarray(L0_ffn_gate_proj, dtype=np.float32))
    down = np.ascontiguousarray(np.asarray(L0_ffn_down_proj, dtype=np.float32))
    shg = np.asarray(L0_ffn_sh_gate, dtype=np.float32)
    shu = np.asarray(L0_ffn_sh_up, dtype=np.float32)
    shd = np.asarray(L0_ffn_sh_down, dtype=np.float32)

    wqkT = np.ascontiguousarray(np.transpose(wq[:, 0:128, :], (2, 0, 1)).reshape(C, NH * 128))
    wvT = np.ascontiguousarray(np.transpose(wq[:, 128:192, :], (2, 0, 1)).reshape(C, NH * HS))
    wpT = np.ascontiguousarray(wp.T)
    routerT = np.ascontiguousarray(rw.T)
    rbias_b = np.ascontiguousarray(np.broadcast_to(rb.reshape(1, NE), (128, NE)))
    bproj = np.ascontiguousarray(bp.reshape(1, C))
    shgT = np.ascontiguousarray(shg.T)
    shuT = np.ascontiguousarray(shu.T)
    shdT = np.ascontiguousarray(shd.T)
    ident = np.eye(128, dtype=ml_dtypes.bfloat16)
    maskneg = np.where(np.triu(np.ones((128, 128), np.float32)) > 0, 0.0, -1e9).astype(ml_dtypes.bfloat16)
    ones128 = np.ones((1, 128), dtype=ml_dtypes.bfloat16)
    iotam = np.ascontiguousarray(np.broadcast_to(
        (np.arange(NE, dtype=np.float32) - 16.0).reshape(1, NE), (128, NE)))

    shared = dict(wqkt=wqkT, wvt=wvT, wpt=wpT, bproj=bproj, routert=routerT,
                  rbias=rbias_b, up=up, gate=gate, down=down, shgt=shgT,
                  shut=shuT, shdt=shdT, ident=ident,
                  ones128=ones128, iotam=iotam, maskneg=maskneg)
    return x, shared


def kernel(**inputs):
    x, shared = host_prep(**inputs)
    if "nc" not in _cache:
        _cache["nc"] = build_nc()
    nc = _cache["nc"]
    in_maps = []
    for i in range(8):
        m = dict(shared)
        m["x"] = np.ascontiguousarray(x[i])
        in_maps.append(m)

    res = run_bass_kernel_spmd(nc, in_maps, core_ids=list(range(8)))
    out = np.stack([np.asarray(r["y"], dtype=np.float32) for r in res.results], axis=0)
    return out
